# version 02
# baseline (speedup 1.0000x reference)
import math
import numpy as np
import ml_dtypes
from contextlib import ExitStack

import concourse.bass as bass
import concourse.mybir as mybir
from concourse.bass_utils import run_bass_kernel_spmd

F32 = mybir.dt.float32
BF16 = mybir.dt.bfloat16
U8 = mybir.dt.uint8
AF = mybir.ActivationFunctionType
ALU = mybir.AluOpType

D = 4096
T = 4096
TO = 2048
HID = 11008
NHC = HID // 128
EPS = 1e-5
INV_SQRT_HD = 1.0 / math.sqrt(128.0)

NF4 = [-1.0, -0.6961928009986877, -0.5250730514526367, -0.39491748809814453,
       -0.28444138169288635, -0.18477343022823334, -0.09105003625154495, 0.0,
       0.07958029955625534, 0.16093020141124725, 0.24611230194568634,
       0.33791524171829224, 0.44070982933044434, 0.5626170039176941,
       0.7229568362236023, 1.0]

ENGS = ["sync", "gpsimd", "scalar", "vector", "tensor"]


def build_nc():
    nc = bass.Bass("TRN2", target_bir_lowering=False)

    x_all = nc.dram_tensor("x_all", [T, D], F32, kind="ExternalInput").ap()
    x_own = nc.dram_tensor("x_own", [TO, D], F32, kind="ExternalInput").ap()
    wq = nc.dram_tensor("wq", [D, D], F32, kind="ExternalInput").ap()
    wo = nc.dram_tensor("wo", [D, D], F32, kind="ExternalInput").ap()
    wk = nc.dram_tensor("wk", [128, D], F32, kind="ExternalInput").ap()
    wv = nc.dram_tensor("wv", [128, D], F32, kind="ExternalInput").ap()
    wq_b = nc.dram_tensor("wq_b", [D], F32, kind="ExternalInput").ap()
    wo_b = nc.dram_tensor("wo_b", [D], F32, kind="ExternalInput").ap()
    wk_b = nc.dram_tensor("wk_b", [128], F32, kind="ExternalInput").ap()
    wv_b = nc.dram_tensor("wv_b", [128], F32, kind="ExternalInput").ap()
    w1qT = nc.dram_tensor("w1qT", [D, HID], U8, kind="ExternalInput").ap()
    w2qT = nc.dram_tensor("w2qT", [D, HID], U8, kind="ExternalInput").ap()
    w3qT = nc.dram_tensor("w3qT", [HID, D], U8, kind="ExternalInput").ap()
    sc1T = nc.dram_tensor("sc1T", [64, HID], F32, kind="ExternalInput").ap()
    sc2T = nc.dram_tensor("sc2T", [64, HID], F32, kind="ExternalInput").ap()
    sc3T = nc.dram_tensor("sc3T", [172, D], F32, kind="ExternalInput").ap()
    tri_in = nc.dram_tensor("tri_in", [128, 128], BF16, kind="ExternalInput").ap()
    mB_in = nc.dram_tensor("mB_in", [128, 128], BF16, kind="ExternalInput").ap()
    iden_in = nc.dram_tensor("iden_in", [128, 128], F32, kind="ExternalInput").ap()

    y_own = nc.dram_tensor("y_own", [TO, D], F32, kind="ExternalOutput").ap()
    wqT_d = nc.dram_tensor("wqT_d", [D, D], BF16, kind="Internal").ap()
    woT_d = nc.dram_tensor("woT_d", [D, D], BF16, kind="Internal").ap()
    xn1_d = nc.dram_tensor("xn1_d", [D, TO], BF16, kind="Internal").ap()
    q_d = nc.dram_tensor("q_d", [D, TO], BF16, kind="Internal").ap()
    attn_d = nc.dram_tensor("attn_d", [D, TO], BF16, kind="Internal").ap()
    r2_d = nc.dram_tensor("r2_d", [D, TO], F32, kind="Internal").ap()
    w1t_d = nc.dram_tensor("w1t_d", [D, HID], BF16, kind="Internal").ap()
    w2t_d = nc.dram_tensor("w2t_d", [D, HID], BF16, kind="Internal").ap()
    w3t_d = nc.dram_tensor("w3t_d", [HID, D], BF16, kind="Internal").ap()
    h_d = nc.dram_tensor("h_d", [HID, TO], BF16, kind="Internal").ap()
    mrow_d = nc.dram_tensor("mrow_d", [8, T], F32, kind="Internal").ap()

    with ExitStack() as top:
        ent = top.enter_context

        P = [ent(nc.psum_tensor(f"pb{i}", [128, 512], F32)) for i in range(8)]

        sem_names = ["dS", "dG", "cT", "cA", "cV"]
        SEMS = {n: ent(nc.semaphore(n)) for n in sem_names}
        CNT = {n: 0 for n in sem_names}
        ROLE = [None]

        def eng(name):
            return getattr(nc, name)

        def do(engname, f, inc=None, waits=()):
            if ROLE[0] == engname:
                e = eng(engname)
                for s, v in waits:
                    if v > 0:
                        e.wait_ge(SEMS[s], v)
                r = f(e)
                if inc is not None:
                    r.then_inc(SEMS[inc[0]], inc[1])
            if inc is not None:
                CNT[inc[0]] += inc[1]

        block = ent(nc.Block())

        def run_phase(fn):
            start = dict(CNT)
            final = None
            for role in ENGS:
                CNT.clear()
                CNT.update(start)
                ROLE[0] = role
                sec = getattr(block, role)

                def _body(e, fn=fn):
                    fn()
                sec(_body)
                if final is None:
                    final = dict(CNT)
                else:
                    assert final == CNT, "nondeterministic phase emission"
            ROLE[0] = None

        def barrier():
            vals = dict(CNT)

            def _b():
                e = eng(ROLE[0])
                for s, v in vals.items():
                    if v > 0:
                        e.wait_ge(SEMS[s], v)
            run_phase(_b)

        s_tri = ent(nc.sbuf_tensor("s_tri", [128, 128], BF16))
        s_mB = ent(nc.sbuf_tensor("s_mB", [128, 128], BF16))
        s_iden = ent(nc.sbuf_tensor("s_iden", [128, 128], F32))
        s_ones = ent(nc.sbuf_tensor("s_ones", [128, 1], BF16))
        s_wqb = ent(nc.sbuf_tensor("s_wqb", [128, 32], F32))
        s_wob = ent(nc.sbuf_tensor("s_wob", [128, 32], F32))
        s_wkb = ent(nc.sbuf_tensor("s_wkb", [128, 1], F32))
        s_wvb = ent(nc.sbuf_tensor("s_wvb", [128, 1], F32))
        s_eps = ent(nc.sbuf_tensor("s_eps", [128, 1], F32))

        p0_stack = ExitStack()
        kv_stack = ExitStack()
        s_wkT = kv_stack.enter_context(nc.sbuf_tensor("s_wkT", [128, D], BF16))
        s_wvT = kv_stack.enter_context(nc.sbuf_tensor("s_wvT", [128, D], BF16))
        s_KF = kv_stack.enter_context(nc.sbuf_tensor("s_KF", [128, T], BF16))
        s_VT = kv_stack.enter_context(nc.sbuf_tensor("s_VT", [128, T], BF16))
        e0 = p0_stack.enter_context
        s_wfl = e0(nc.sbuf_tensor("s_wfl", [128, 2 * D], F32))
        s_wtmp = [e0(nc.sbuf_tensor(f"s_wtmp{i}", [128, D], F32)) for i in range(2)]
        s_wt1 = [e0(nc.sbuf_tensor(f"s_wt1_{i}", [128, 128], BF16)) for i in range(4)]
        s_bt = e0(nc.sbuf_tensor("s_bt", [32, 128], F32))

        def p0():
            do("sync", lambda e: e.dma_start(out=s_tri[:], in_=tri_in), inc=("dS", 16))
            do("sync", lambda e: e.dma_start(out=s_mB[:], in_=mB_in), inc=("dS", 16))
            do("sync", lambda e: e.dma_start(out=s_iden[:], in_=iden_in), inc=("dS", 16))
            do("sync", lambda e: e.dma_start(out=s_wkb[:], in_=wk_b.unsqueeze(1)), inc=("dS", 16))
            do("sync", lambda e: e.dma_start(out=s_wvb[:], in_=wv_b.unsqueeze(1)), inc=("dS", 16))
            do("sync", lambda e: e.dma_start(out=s_wfl[:, 0:D], in_=wk), inc=("dS", 16))
            do("sync", lambda e: e.dma_start(out=s_wfl[:, D:2 * D], in_=wv), inc=("dS", 16))
            do("sync", lambda e: e.dma_start(out=s_bt[:], in_=wq_b.rearrange("(t p) -> t p", p=128)), inc=("dS", 16))
            do("vector", lambda e: e.memset(s_ones[:], 1.0), inc=("cV", 1))
            do("vector", lambda e: e.memset(s_eps[:], EPS), inc=("cV", 1))
            ld8 = CNT["dS"]
            do("tensor", lambda e: e.transpose(out=P[0][:, 0:32], in_=s_bt[:], identity=s_iden[0:32, 0:32]),
               inc=("cT", 1), waits=[("dS", ld8)])
            do("vector", lambda e: e.tensor_copy(s_wqb[:], P[0][:, 0:32]), inc=("cV", 1),
               waits=[("cT", CNT["cT"])])
            do("sync", lambda e: e.dma_start(out=s_bt[:], in_=wo_b.rearrange("(t p) -> t p", p=128)),
               inc=("dS", 16), waits=[("cV", CNT["cV"])])
            do("tensor", lambda e: e.transpose(out=P[1][:, 0:32], in_=s_bt[:], identity=s_iden[0:32, 0:32]),
               inc=("cT", 1), waits=[("dS", CNT["dS"])])
            do("vector", lambda e: e.tensor_copy(s_wob[:], P[1][:, 0:32]), inc=("cV", 1),
               waits=[("cT", CNT["cT"])])

            for w in range(2):
                dst = s_wkT if w == 0 else s_wvT
                for j in range(32):
                    par = (w * 32 + j) % 2
                    do("tensor", lambda e, par=par, w=w, j=j: e.transpose(
                        out=P[2 + par][:, 0:128], in_=s_wfl[:, w * D + j * 128: w * D + (j + 1) * 128],
                        identity=s_iden[:]),
                       inc=("cT", 1), waits=[("dS", ld8), ("cV", CNT["cV"] - 1)])
                    do("vector", lambda e, par=par, dst=dst, j=j: e.tensor_copy(
                        dst[:, j * 128:(j + 1) * 128], P[2 + par][:, 0:128]),
                       inc=("cV", 1), waits=[("cT", CNT["cT"])])

            for w in range(2):
                src = wq if w == 0 else wo
                dstd = wqT_d if w == 0 else woT_d
                for ot in range(32):
                    buf = s_wtmp[ot % 2]
                    do("sync", lambda e, src=src, ot=ot, buf=buf: e.dma_start(
                        out=buf[:], in_=src[ot * 128:(ot + 1) * 128, :]),
                       inc=("dS", 16), waits=[("cT", CNT["cT"])])
                    ldv = CNT["dS"]
                    for j in range(32):
                        k = (w * 32 + ot) * 32 + j
                        par = 4 + (k % 2)
                        bb = s_wt1[k % 4]
                        do("tensor", lambda e, par=par, buf=buf, j=j: e.transpose(
                            out=P[par][:, 0:128], in_=buf[:, j * 128:(j + 1) * 128], identity=s_iden[:]),
                           inc=("cT", 1), waits=[("dS", ldv), ("cV", CNT["cV"] - 1)])
                        do("vector", lambda e, par=par, bb=bb: e.tensor_copy(
                            bb[:], P[par][:, 0:128]),
                           inc=("cV", 1), waits=[("cT", CNT["cT"]), ("dG", CNT["dG"] - 3 * 16)])
                        do("gpsimd", lambda e, dstd=dstd, j=j, ot=ot, bb=bb: e.dma_start(
                            out=dstd[j * 128:(j + 1) * 128, ot * 128:(ot + 1) * 128], in_=bb[:]),
                           inc=("dG", 16), waits=[("cV", CNT["cV"])])

        run_phase(p0)
        barrier()
        p0_stack.close()

        p12_stack = ExitStack()
        e12 = p12_stack.enter_context
        s_xt = [e12(nc.sbuf_tensor(f"s_xt{i}", [128, D], F32)) for i in range(5)]
        s_xn1 = e12(nc.sbuf_tensor("s_xn1", [128, 32 * 512], BF16))
        s_bns = e12(nc.sbuf_tensor("s_bns", [128, 48], F32))
        s_mv = e12(nc.sbuf_tensor("s_mv", [128, 8], F32))
        s_cat = e12(nc.sbuf_tensor("s_cat", [128, 8], F32))
        s_g8 = e12(nc.sbuf_tensor("s_g8", [8, 128], F32))
        s_invb = e12(nc.sbuf_tensor("s_invb", [128, 512], F32))
        s_mib = e12(nc.sbuf_tensor("s_mib", [128, 512], F32))
        s_t1 = e12(nc.sbuf_tensor("s_t1", [128, 512], F32))
        s_vf = e12(nc.sbuf_tensor("s_vf", [128, 512], F32))

        def p12():
            for ch in range(8):
                for i in range(4):
                    t = ch * 4 + i
                    buf = s_xt[t % 5]
                    do("sync", lambda e, t=t, buf=buf: e.dma_start(
                        out=buf[:], in_=x_all[t * 128:(t + 1) * 128, :]),
                       inc=("dS", 16), waits=[("cT", CNT["cT"])] if t >= 5 else [])
                ld = CNT["dS"]
                for i in range(4):
                    buf = s_xt[(ch * 4 + i) % 5]
                    for sch in range(8):
                        w = [("dS", ld - (3 - i) * 16)]
                        if i == 0 and sch == 0:
                            w.append(("cA", CNT["cA"]))
                        do("vector", lambda e, buf=buf, sch=sch: e.bn_stats(
                            out=s_bns[:, sch * 6:(sch + 1) * 6], in_=buf[:, sch * 512:(sch + 1) * 512]),
                           inc=("cV", 1), waits=w)
                    do("vector", lambda e, i=i: e.bn_aggr(
                        out=s_mv[:, 2 * i:2 * i + 2], in_=s_bns[:]), inc=("cV", 1))
                do("scalar", lambda e: e.activation(
                    out=s_cat[:, 0:4], in_=s_mv[:].rearrange("p (i two) -> p two i", two=2)[:, 1, :],
                    func=AF.Sqrt, bias=s_eps[:, 0:1]), inc=("cA", 1), waits=[("cV", CNT["cV"])])
                do("vector", lambda e: e.reciprocal(s_cat[:, 0:4], s_cat[:, 0:4]),
                   inc=("cV", 1), waits=[("cA", CNT["cA"])])
                do("vector", lambda e: e.tensor_tensor(
                    out=s_cat[:, 4:8], in0=s_mv[:].rearrange("p (i two) -> p two i", two=2)[:, 0, :],
                    in1=s_cat[:, 0:4], op=ALU.mult), inc=("cV", 1))
                do("tensor", lambda e: e.transpose(out=P[6][0:8, 0:128], in_=s_cat[:], identity=s_iden[:]),
                   inc=("cT", 1), waits=[("cV", CNT["cV"])])
                do("vector", lambda e: e.tensor_copy(s_g8[:], P[6][0:8, 0:128]),
                   inc=("cV", 1), waits=[("cT", CNT["cT"]), ("dS", CNT["dS"])])
                do("sync", lambda e, ch=ch: e.dma_start(
                    out=mrow_d[0, ch * 512:(ch + 1) * 512].rearrange("(i c) -> i c", i=4), in_=s_g8[0:4, :]),
                   inc=("dS", 16), waits=[("cV", CNT["cV"])])
                do("sync", lambda e, ch=ch: e.dma_start(
                    out=mrow_d[1, ch * 512:(ch + 1) * 512].rearrange("(i c) -> i c", i=4), in_=s_g8[4:8, :]),
                   inc=("dS", 16))
                wrow = CNT["dS"]
                do("sync", lambda e, ch=ch: e.dma_start(
                    out=s_invb[:], in_=mrow_d[0:1, ch * 512:(ch + 1) * 512].broadcast_to((128, 512))),
                   inc=("dS", 16), waits=[("dS", wrow), ("cV", CNT["cV"])])
                do("sync", lambda e, ch=ch: e.dma_start(
                    out=s_mib[:], in_=mrow_d[1:2, ch * 512:(ch + 1) * 512].broadcast_to((128, 512))),
                   inc=("dS", 16))
                ldb = CNT["dS"]
                for dt in range(32):
                    par = dt % 2
                    for i in range(4):
                        buf = s_xt[(ch * 4 + i) % 5]
                        w = [("dS", ldb), ("cV", CNT["cV"] - 2)] if i == 0 else []
                        do("tensor", lambda e, par=par, buf=buf, dt=dt, i=i: e.transpose(
                            out=P[par][:, i * 128:(i + 1) * 128], in_=buf[:, dt * 128:(dt + 1) * 128],
                            identity=s_iden[:]),
                           inc=("cT", 1), waits=w)
                    do("vector", lambda e, par=par: e.tensor_tensor(
                        out=s_t1[:], in0=P[par][:], in1=s_invb[:], op=ALU.mult),
                       inc=("cV", 1),
                       waits=[("cT", CNT["cT"])] + ([("dG", CNT["dG"])] if dt == 0 else []))
                    do("vector", lambda e, dt=dt: e.tensor_tensor(
                        out=s_xn1[:, dt * 512:(dt + 1) * 512], in0=s_t1[:], in1=s_mib[:], op=ALU.subtract),
                       inc=("cV", 1))
                xnv = CNT["cV"]
                for w in range(2):
                    wT = s_wkT if w == 0 else s_wvT
                    pb = P[2 + w]
                    for dt in range(32):
                        do("tensor", lambda e, pb=pb, wT=wT, dt=dt: e.matmul(
                            pb[:], wT[:, dt * 128:(dt + 1) * 128], s_xn1[:, dt * 512:(dt + 1) * 512],
                            start=(dt == 0), stop=(dt == 31)),
                           inc=("cT", 1), waits=[("cV", xnv), ("cA", CNT["cA"])] if dt == 0 else [])
                    if w == 0:
                        do("scalar", lambda e, ch=ch: e.activation(
                            out=s_KF[:, ch * 512:(ch + 1) * 512], in_=P[2][:], func=AF.Identity,
                            bias=s_wkb[:, 0:1]), inc=("cA", 1), waits=[("cT", CNT["cT"])])
                    else:
                        do("scalar", lambda e: e.activation(
                            out=s_vf[:], in_=P[3][:], func=AF.Identity, bias=s_wvb[:, 0:1]),
                           inc=("cA", 1), waits=[("cT", CNT["cT"]), ("cT", CNT["cT"])])
                        for i in range(4):
                            do("tensor", lambda e, i=i: e.transpose(
                                out=P[5][:, 0:128], in_=s_vf[:, i * 128:(i + 1) * 128], identity=s_iden[:]),
                               inc=("cT", 1), waits=[("cA", CNT["cA"]), ("cV", CNT["cV"])])
                            do("vector", lambda e, ch=ch, i=i: e.tensor_copy(
                                s_VT[:, ch * 512 + i * 128: ch * 512 + (i + 1) * 128], P[5][:, 0:128]),
                               inc=("cV", 1), waits=[("cT", CNT["cT"])])
                for j, i in enumerate((0, 2)):
                    do("gpsimd", lambda e, ch=ch, j=j, i=i: e.dma_start(
                        out=xn1_d.rearrange("(dt p) t -> p dt t", p=128)[:, :, ch * 256 + j * 128: ch * 256 + (j + 1) * 128],
                        in_=s_xn1[:].rearrange("p (dt c) -> p dt c", c=512)[:, :, i * 128:(i + 1) * 128]),
                       inc=("dG", 16), waits=[("cV", xnv)])

        run_phase(p12)
        barrier()
        p12_stack.close()

        p2_stack = ExitStack()
        e2 = p2_stack.enter_context
        s_x2 = [e2(nc.sbuf_tensor(f"s_x2_{i}", [128, 32 * 512], BF16)) for i in range(2)]
        s_wq2 = [e2(nc.sbuf_tensor(f"s_wq2_{i}", [128, D], BF16)) for i in range(2)]
        s_qt = [e2(nc.sbuf_tensor(f"s_qt{i}", [128, 512], BF16)) for i in range(2)]

        def p2b():
            tc_done = {}
            for tc in range(4):
                buf = s_x2[tc % 2]
                w = [("cT", tc_done[tc - 2])] if tc >= 2 else []
                for half in range(2):
                    do("sync", lambda e, tc=tc, buf=buf, half=half: e.dma_start(
                        out=buf[:, half * 16 * 512:(half + 1) * 16 * 512].rearrange("p (dt c) -> p dt c", c=512),
                        in_=xn1_d.rearrange("(dt p) t -> p dt t", p=128)[:, half * 16:(half + 1) * 16, tc * 512:(tc + 1) * 512]),
                       inc=("dS", 16), waits=w if half == 0 else [])
                ldx = CNT["dS"]
                for qt in range(32):
                    k = tc * 32 + qt
                    wbuf = s_wq2[k % 2]
                    for half in range(2):
                        do("gpsimd", lambda e, qt=qt, wbuf=wbuf, half=half: e.dma_start(
                            out=wbuf[:, half * 2048:(half + 1) * 2048].rearrange("p (dt c) -> p dt c", c=128),
                            in_=wqT_d.rearrange("(dt p) t -> p dt t", p=128)[:, half * 16:(half + 1) * 16, qt * 128:(qt + 1) * 128]),
                           inc=("dG", 16),
                           waits=[("cT", CNT["cT"] - 32)] if (k >= 2 and half == 0) else [])
                    ldw = CNT["dG"]
                    par = k % 2
                    for dt in range(32):
                        do("tensor", lambda e, par=par, wbuf=wbuf, buf=buf, dt=dt: e.matmul(
                            P[par][:], wbuf[:, dt * 128:(dt + 1) * 128], buf[:, dt * 512:(dt + 1) * 512],
                            start=(dt == 0), stop=(dt == 31)),
                           inc=("cT", 1),
                           waits=[("dS", ldx), ("dG", ldw), ("cA", CNT["cA"])] if dt == 0 else [])
                    qb = s_qt[k % 2]
                    do("scalar", lambda e, par=par, qb=qb, qt=qt: e.activation(
                        out=qb[:], in_=P[par][:], func=AF.Identity, bias=s_wqb[:, qt:qt + 1]),
                       inc=("cA", 1), waits=[("cT", CNT["cT"]), ("dS", CNT["dS"])])
                    do("sync", lambda e, qb=qb, qt=qt, tc=tc: e.dma_start(
                        out=q_d[qt * 128:(qt + 1) * 128, tc * 512:(tc + 1) * 512], in_=qb[:]),
                       inc=("dS", 16), waits=[("cA", CNT["cA"])])
                tc_done[tc] = CNT["cT"]

        run_phase(p2b)
        barrier()
        p2_stack.close()

        p3_stack = ExitStack()
        e3 = p3_stack.enter_context
        s_qh = [e3(nc.sbuf_tensor(f"s_qh{i}", [128, TO], BF16)) for i in range(2)]
        s_pt = [e3(nc.sbuf_tensor(f"s_pt{i}", [128, 512], BF16)) for i in range(3)]
        s_rs = e3(nc.sbuf_tensor("s_rs", [1, 512], F32))
        s_rb = [e3(nc.sbuf_tensor(f"s_rb{i}", [128, 512], F32)) for i in range(2)]
        s_at = [e3(nc.sbuf_tensor(f"s_at{i}", [128, 512], BF16)) for i in range(2)]

        def p3():
            h_done = {}
            for h in range(32):
                qbuf = s_qh[h % 2]
                do("sync", lambda e, h=h, qbuf=qbuf: e.dma_start(
                    out=qbuf[:], in_=q_d[h * 128:(h + 1) * 128, :]),
                   inc=("dS", 16), waits=[("cT", h_done[h - 2])] if h >= 2 else [])
                ldq = CNT["dS"]
                for qc in range(4):
                    u = h * 4 + qc
                    nkt = 8 * qc + 8
                    av = P[2 + (u % 2)]
                    sm = P[4 + (u % 2)]
                    avw = [("cV", CNT["cV"])]
                    for kt in range(nkt):
                        sl = max(0, kt // 2 - 4 * qc)
                        c0 = sl * 128
                        pb = P[kt % 2]
                        ptb = s_pt[kt % 3]
                        do("tensor", lambda e, pb=pb, kt=kt, qbuf=qbuf, qc=qc, c0=c0: e.matmul(
                            pb[:, c0:512], s_KF[:, kt * 128:(kt + 1) * 128],
                            qbuf[:, qc * 512 + c0: (qc + 1) * 512],
                            start=True, stop=True, skip_group_check=True),
                           inc=("cT", 1),
                           waits=([("dS", ldq)] if kt == 0 else []) + [("cA", CNT["cA"] - 1)])
                        do("scalar", lambda e, pb=pb, ptb=ptb, c0=c0: e.activation(
                            out=ptb[:, c0:512], in_=pb[:, c0:512], func=AF.Exp, scale=INV_SQRT_HD),
                           inc=("cA", 1), waits=[("cT", CNT["cT"])])
                        masked = False
                        i_d = kt // 2
                        if 4 * qc <= i_d < 4 * qc + 4:
                            cm = (i_d - 4 * qc) * 128
                            msk = s_tri if kt % 2 == 0 else s_mB
                            do("vector", lambda e, ptb=ptb, cm=cm, msk=msk: e.tensor_tensor(
                                out=ptb[:, cm:cm + 128], in0=ptb[:, cm:cm + 128], in1=msk[:], op=ALU.mult),
                               inc=("cV", 1), waits=[("cA", CNT["cA"])])
                            masked = True
                        pw = [("cA", CNT["cA"])] + ([("cV", CNT["cV"])] if masked else [])
                        do("tensor", lambda e, av=av, kt=kt, ptb=ptb, c0=c0, nkt=nkt: e.matmul(
                            av[:, c0:512], s_VT[:, kt * 128:(kt + 1) * 128], ptb[:, c0:512],
                            start=(kt == 0), stop=(kt == nkt - 1), skip_group_check=True),
                           inc=("cT", 1), waits=pw + (avw if kt == 0 else []))
                        do("tensor", lambda e, sm=sm, kt=kt, ptb=ptb, c0=c0, nkt=nkt: e.matmul(
                            sm[0:1, c0:512], s_ones[:], ptb[:, c0:512],
                            start=(kt == 0), stop=(kt == nkt - 1), skip_group_check=True),
                           inc=("cT", 1))
                    do("vector", lambda e, sm=sm: e.reciprocal(s_rs[:], sm[0:1, :]),
                       inc=("cV", 1), waits=[("cT", CNT["cT"]), ("dS", CNT["dS"])])
                    do("sync", lambda e: e.dma_start(out=mrow_d[2, 0:512].unsqueeze(0), in_=s_rs[:]),
                       inc=("dS", 16), waits=[("cV", CNT["cV"])])
                    rb = s_rb[u % 2]
                    do("sync", lambda e, rb=rb: e.dma_start(
                        out=rb[:], in_=mrow_d[2:3, 0:512].broadcast_to((128, 512))),
                       inc=("dS", 16), waits=[("dS", CNT["dS"]), ("cV", CNT["cV"])])
                    at = s_at[u % 2]
                    do("vector", lambda e, av=av, rb=rb, at=at: e.tensor_tensor(
                        out=at[:], in0=av[:], in1=rb[:], op=ALU.mult),
                       inc=("cV", 1), waits=[("dS", CNT["dS"]), ("dG", CNT["dG"] - 16)])
                    do("gpsimd", lambda e, at=at, h=h, qc=qc: e.dma_start(
                        out=attn_d[h * 128:(h + 1) * 128, qc * 512:(qc + 1) * 512], in_=at[:]),
                       inc=("dG", 16), waits=[("cV", CNT["cV"])])
                h_done[h] = CNT["cT"]

        run_phase(p3)
        barrier()
        p3_stack.close()
        kv_stack.close()

        p4_stack = ExitStack()
        e4 = p4_stack.enter_context
        s_accS = e4(nc.sbuf_tensor("s_accS", [1, TO], F32))
        s_accQ = e4(nc.sbuf_tensor("s_accQ", [1, TO], F32))
        s_ln2m = e4(nc.sbuf_tensor("s_ln2m", [1, TO], F32))
        s_ln2v = e4(nc.sbuf_tensor("s_ln2v", [1, TO], F32))
        s_ac4 = [e4(nc.sbuf_tensor(f"s_ac4_{i}", [128, 32 * 512], BF16)) for i in range(2)]
        s_wo4 = [e4(nc.sbuf_tensor(f"s_wo4_{i}", [128, D], BF16)) for i in range(2)]
        s_xr = [e4(nc.sbuf_tensor(f"s_xr{i}", [128, 512], F32)) for i in range(2)]
        s_tmp4 = e4(nc.sbuf_tensor("s_tmp4", [128, 512], F32))
        s_r2 = [e4(nc.sbuf_tensor(f"s_r2_{i}", [128, 512], F32)) for i in range(2)]
        s_r2b = [e4(nc.sbuf_tensor(f"s_r2b{i}", [128, 512], BF16)) for i in range(2)]
        s_sq4 = [e4(nc.sbuf_tensor(f"s_sq4_{i}", [128, 512], BF16)) for i in range(2)]

        def p4():
            do("vector", lambda e: e.memset(s_accS[:], 0.0), inc=("cV", 1))
            do("vector", lambda e: e.memset(s_accQ[:], 0.0), inc=("cV", 1))
            qc_done = {}
            for qc in range(4):
                abuf = s_ac4[qc % 2]
                w = [("cT", qc_done[qc - 2])] if qc >= 2 else []
                for half in range(2):
                    do("sync", lambda e, qc=qc, abuf=abuf, half=half: e.dma_start(
                        out=abuf[:, half * 16 * 512:(half + 1) * 16 * 512].rearrange("p (dt c) -> p dt c", c=512),
                        in_=attn_d.rearrange("(dt p) t -> p dt t", p=128)[:, half * 16:(half + 1) * 16, qc * 512:(qc + 1) * 512]),
                       inc=("dS", 16), waits=w if half == 0 else [])
                lda = CNT["dS"]
                for ot in range(32):
                    k = qc * 32 + ot
                    wbuf = s_wo4[k % 2]
                    for half in range(2):
                        do("gpsimd", lambda e, ot=ot, wbuf=wbuf, half=half: e.dma_start(
                            out=wbuf[:, half * 2048:(half + 1) * 2048].rearrange("p (dt c) -> p dt c", c=128),
                            in_=woT_d.rearrange("(dt p) t -> p dt t", p=128)[:, half * 16:(half + 1) * 16, ot * 128:(ot + 1) * 128]),
                           inc=("dG", 16),
                           waits=[("cT", CNT["cT"] - 44)] if (k >= 2 and half == 0) else [])
                    ldw = CNT["dG"]
                    xr = s_xr[k % 2]
                    do("sync", lambda e, qc=qc, ot=ot, xr=xr: e.dma_start(
                        out=xr[:].rearrange("p (g c) -> p g c", c=128),
                        in_=x_own.rearrange("(g p) d -> p g d", p=128)[:, qc * 4:(qc + 1) * 4, ot * 128:(ot + 1) * 128]),
                       inc=("dS", 16), waits=[("cT", CNT["cT"] - 40)] if k >= 2 else [])
                    ldx = CNT["dS"]
                    par = k % 2
                    for dt in range(32):
                        do("tensor", lambda e, par=par, wbuf=wbuf, abuf=abuf, dt=dt: e.matmul(
                            P[par][:], wbuf[:, dt * 128:(dt + 1) * 128], abuf[:, dt * 512:(dt + 1) * 512],
                            start=(dt == 0), stop=(dt == 31)),
                           inc=("cT", 1),
                           waits=[("dS", lda), ("dG", ldw), ("cV", CNT["cV"])] if dt == 0 else [])
                    mmv = CNT["cT"]
                    for i in range(4):
                        do("tensor", lambda e, par=par, xr=xr, i=i: e.transpose(
                            out=P[4 + par][:, i * 128:(i + 1) * 128], in_=xr[:, i * 128:(i + 1) * 128],
                            identity=s_iden[:]),
                           inc=("cT", 1), waits=[("dS", ldx), ("cV", CNT["cV"])] if i == 0 else [])
                    do("scalar", lambda e, par=par, ot=ot: e.activation(
                        out=s_tmp4[:], in_=P[par][:], func=AF.Identity, bias=s_wob[:, ot:ot + 1]),
                       inc=("cA", 1), waits=[("cT", mmv), ("cV", CNT["cV"])])
                    r2 = s_r2[k % 2]
                    do("vector", lambda e, par=par, r2=r2: e.tensor_tensor(
                        out=r2[:], in0=s_tmp4[:], in1=P[4 + par][:], op=ALU.add),
                       inc=("cV", 1), waits=[("cA", CNT["cA"]), ("cT", CNT["cT"]), ("dS", CNT["dS"])])
                    do("sync", lambda e, r2=r2, ot=ot, qc=qc: e.dma_start(
                        out=r2_d[ot * 128:(ot + 1) * 128, qc * 512:(qc + 1) * 512], in_=r2[:]),
                       inc=("dS", 16), waits=[("cV", CNT["cV"])])
                    r2b = s_r2b[k % 2]
                    do("vector", lambda e, r2=r2, r2b=r2b: e.tensor_copy(r2b[:], r2[:]),
                       inc=("cV", 1), waits=[("cT", CNT["cT"])])
                    sq = s_sq4[k % 2]
                    do("scalar", lambda e, r2b=r2b, sq=sq: e.activation(out=sq[:], in_=r2b[:], func=AF.Square),
                       inc=("cA", 1), waits=[("cV", CNT["cV"]), ("cT", CNT["cT"])])
                    do("tensor", lambda e, r2b=r2b: e.matmul(
                        P[6][0:1, :], s_ones[:], r2b[:], start=True, stop=True, skip_group_check=True),
                       inc=("cT", 1), waits=[("cV", CNT["cV"])])
                    do("tensor", lambda e, sq=sq: e.matmul(
                        P[7][0:1, :], s_ones[:], sq[:], start=True, stop=True, skip_group_check=True),
                       inc=("cT", 1), waits=[("cA", CNT["cA"])])
                    do("vector", lambda e, qc=qc: e.tensor_tensor(
                        out=s_accS[0:1, qc * 512:(qc + 1) * 512], in0=s_accS[0:1, qc * 512:(qc + 1) * 512],
                        in1=P[6][0:1, :], op=ALU.add),
                       inc=("cV", 1), waits=[("cT", CNT["cT"])])
                    do("vector", lambda e, qc=qc: e.tensor_tensor(
                        out=s_accQ[0:1, qc * 512:(qc + 1) * 512], in0=s_accQ[0:1, qc * 512:(qc + 1) * 512],
                        in1=P[7][0:1, :], op=ALU.add),
                       inc=("cV", 1))
                qc_done[qc] = CNT["cT"]
            do("vector", lambda e: e.tensor_scalar(
                out=s_ln2m[:], in0=s_accS[:], scalar1=1.0 / D, scalar2=None, op0=ALU.mult),
               inc=("cV", 1))
            do("vector", lambda e: e.tensor_tensor(
                out=s_ln2v[:], in0=s_ln2m[:], in1=s_ln2m[:], op=ALU.mult),
               inc=("cV", 1))
            do("vector", lambda e: e.scalar_tensor_tensor(
                out=s_ln2v[:], in0=s_accQ[:], scalar=1.0 / D, in1=s_ln2v[:],
                op0=ALU.mult, op1=ALU.subtract),
               inc=("cV", 1))
            do("scalar", lambda e: e.activation(
                out=s_ln2v[:], in_=s_ln2v[:], func=AF.Sqrt, bias=s_eps[0:1, 0:1]),
               inc=("cA", 1), waits=[("cV", CNT["cV"])])
            do("vector", lambda e: e.reciprocal(s_ln2v[:], s_ln2v[:]),
               inc=("cV", 1), waits=[("cA", CNT["cA"])])
            do("vector", lambda e: e.tensor_tensor(
                out=s_ln2m[:], in0=s_ln2m[:], in1=s_ln2v[:], op=ALU.mult),
               inc=("cV", 1))
            do("sync", lambda e: e.dma_start(out=mrow_d[3, 0:TO].unsqueeze(0), in_=s_ln2v[:]),
               inc=("dS", 16), waits=[("cV", CNT["cV"])])
            do("sync", lambda e: e.dma_start(out=mrow_d[4, 0:TO].unsqueeze(0), in_=s_ln2m[:]),
               inc=("dS", 16))

        run_phase(p4)
        barrier()
        p4_stack.close()

        p5a_stack = ExitStack()
        e5a = p5a_stack.enter_context
        s_cd = [e5a(nc.sbuf_tensor(f"s_cd{i}", [128, 4096], U8)) for i in range(2)]
        s_scb = [e5a(nc.sbuf_tensor(f"s_scb{i}", [128, 4096], F32)) for i in range(2)]
        s_mk = [e5a(nc.sbuf_tensor(f"s_mk{i}", [128, 4096], U8)) for i in range(4)]
        s_lv = e5a(nc.sbuf_tensor("s_lv", [128, 8 * 4096], F32))
        s_wd = [e5a(nc.sbuf_tensor(f"s_wd{i}", [128, 4096], BF16)) for i in range(2)]

        def dequant_one(qT, scT, wt_d, nrows_tiles, fd_cols, unit):
            nch = (fd_cols + 4095) // 4096
            for rt in range(nrows_tiles):
                for c in range(nch):
                    fd = min(4096, fd_cols - c * 4096)
                    k = unit[0]
                    unit[0] += 1
                    cd = s_cd[k % 2]
                    scb = s_scb[k % 2]
                    wd = s_wd[k % 2]
                    wv = [("cV", CNT["cV"] - 12)] if k >= 2 else []
                    do("sync", lambda e, cd=cd, rt=rt, c=c, fd=fd, qT=qT: e.dma_start(
                        out=cd[:, 0:fd], in_=qT[rt * 128:(rt + 1) * 128, c * 4096:c * 4096 + fd]),
                       inc=("dS", 16), waits=wv)
                    do("sync", lambda e, scb=scb, rt=rt, c=c, fd=fd, scT=scT: e.dma_start(
                        out=scb[0:64, 0:fd], in_=scT[2 * rt:2 * rt + 1, c * 4096:c * 4096 + fd].broadcast_to((64, fd))),
                       inc=("dS", 16), waits=wv)
                    do("sync", lambda e, scb=scb, rt=rt, c=c, fd=fd, scT=scT: e.dma_start(
                        out=scb[64:128, 0:fd], in_=scT[2 * rt + 1:2 * rt + 2, c * 4096:c * 4096 + fd].broadcast_to((64, fd))),
                       inc=("dS", 16))
                    ld = CNT["dS"]
                    for j in range(4):
                        do("vector", lambda e, j=j, cd=cd, fd=fd: e.tensor_scalar(
                            out=s_mk[j][:, 0:fd], in0=cd[:, 0:fd], scalar1=1 << j, scalar2=None,
                            op0=ALU.bitwise_and),
                           inc=("cV", 1),
                           waits=[("dS", ld), ("cA", CNT["cA"]), ("cV", CNT["cV"])] if j == 0 else [])
                    mkv = CNT["cV"]
                    for j in range(8):
                        dd = float(NF4[2 * j + 1] - NF4[2 * j])
                        bb = float(NF4[2 * j])
                        do("scalar", lambda e, j=j, dd=dd, bb=bb, fd=fd: e.activation(
                            out=s_lv[:, j * 4096: j * 4096 + fd], in_=s_mk[0][:, 0:fd],
                            func=AF.Copy, bias=bb, scale=dd),
                           inc=("cA", 1), waits=[("cV", mkv)] if j == 0 else [])
                    lvv = CNT["cA"]
                    for i in range(4):
                        do("vector", lambda e, i=i, fd=fd: e.copy_predicated(
                            out=s_lv[:, (2 * i) * 4096:(2 * i) * 4096 + fd], mask=s_mk[1][:, 0:fd],
                            data=s_lv[:, (2 * i + 1) * 4096:(2 * i + 1) * 4096 + fd]),
                           inc=("cV", 1), waits=[("cA", lvv)] if i == 0 else [])
                    for i in range(2):
                        do("vector", lambda e, i=i, fd=fd: e.copy_predicated(
                            out=s_lv[:, (4 * i) * 4096:(4 * i) * 4096 + fd], mask=s_mk[2][:, 0:fd],
                            data=s_lv[:, (4 * i + 2) * 4096:(4 * i + 2) * 4096 + fd]),
                           inc=("cV", 1))
                    do("vector", lambda e, fd=fd: e.copy_predicated(
                        out=s_lv[:, 0:fd], mask=s_mk[3][:, 0:fd], data=s_lv[:, 4 * 4096:4 * 4096 + fd]),
                       inc=("cV", 1))
                    do("vector", lambda e, wd=wd, scb=scb, fd=fd: e.tensor_tensor(
                        out=wd[:, 0:fd], in0=s_lv[:, 0:fd], in1=scb[:, 0:fd], op=ALU.mult),
                       inc=("cV", 1), waits=[("dG", CNT["dG"] - 16)])
                    do("gpsimd", lambda e, wd=wd, wt_d=wt_d, rt=rt, c=c, fd=fd: e.dma_start(
                        out=wt_d[rt * 128:(rt + 1) * 128, c * 4096:c * 4096 + fd], in_=wd[:, 0:fd]),
                       inc=("dG", 16), waits=[("cV", CNT["cV"])])

        def p5a():
            unit = [0]
            dequant_one(w1qT, sc1T, w1t_d, 32, HID, unit)
            dequant_one(w2qT, sc2T, w2t_d, 32, HID, unit)
            dequant_one(w3qT, sc3T, w3t_d, NHC, D, unit)

        run_phase(p5a)
        barrier()
        p5a_stack.close()

        p5b_stack = ExitStack()
        e5b = p5b_stack.enter_context
        s_xn2 = e5b(nc.sbuf_tensor("s_xn2", [128, 32 * 2048], BF16))
        s_i2b = e5b(nc.sbuf_tensor("s_i2b", [128, 512], F32))
        s_m2b = e5b(nc.sbuf_tensor("s_m2b", [128, 512], F32))
        s_r2t = [e5b(nc.sbuf_tensor(f"s_r2t{i}", [128, 512], F32)) for i in range(2)]
        s_t2 = e5b(nc.sbuf_tensor("s_t2", [128, 512], F32))
        s_w12 = [e5b(nc.sbuf_tensor(f"s_w12_{i}", [128, D], BF16)) for i in range(4)]
        s_sg = e5b(nc.sbuf_tensor("s_sg", [128, 512], F32))
        s_ht = [e5b(nc.sbuf_tensor(f"s_ht{i}", [128, 2048], BF16)) for i in range(2)]

        def p5b():
            for tc in range(4):
                do("sync", lambda e, tc=tc: e.dma_start(
                    out=s_i2b[:], in_=mrow_d[3:4, tc * 512:(tc + 1) * 512].broadcast_to((128, 512))),
                   inc=("dS", 16), waits=[("cV", CNT["cV"])])
                do("sync", lambda e, tc=tc: e.dma_start(
                    out=s_m2b[:], in_=mrow_d[4:5, tc * 512:(tc + 1) * 512].broadcast_to((128, 512))),
                   inc=("dS", 16))
                ldb = CNT["dS"]
                for dt in range(32):
                    rb = s_r2t[dt % 2]
                    do("gpsimd", lambda e, rb=rb, dt=dt, tc=tc: e.dma_start(
                        out=rb[:], in_=r2_d[dt * 128:(dt + 1) * 128, tc * 512:(tc + 1) * 512]),
                       inc=("dG", 16), waits=[("cV", CNT["cV"] - 2)] if (tc * 32 + dt) >= 2 else [])
                    do("vector", lambda e, rb=rb: e.tensor_tensor(
                        out=s_t2[:], in0=rb[:], in1=s_i2b[:], op=ALU.mult),
                       inc=("cV", 1), waits=[("dG", CNT["dG"]), ("dS", ldb)])
                    do("vector", lambda e, dt=dt, tc=tc: e.tensor_tensor(
                        out=s_xn2[:, dt * 2048 + tc * 512: dt * 2048 + (tc + 1) * 512],
                        in0=s_t2[:], in1=s_m2b[:], op=ALU.subtract),
                       inc=("cV", 1))
            xnv = CNT["cV"]
            hc_done = {}
            for hc in range(NHC):
                w1b = s_w12[(hc % 2) * 2]
                w2b = s_w12[(hc % 2) * 2 + 1]
                wwar = [("cT", hc_done[hc - 2])] if hc >= 2 else []
                for half in range(2):
                    do("sync", lambda e, w1b=w1b, hc=hc, half=half: e.dma_start(
                        out=w1b[:, half * 2048:(half + 1) * 2048].rearrange("p (dt c) -> p dt c", c=128),
                        in_=w1t_d.rearrange("(dt p) t -> p dt t", p=128)[:, half * 16:(half + 1) * 16, hc * 128:(hc + 1) * 128]),
                       inc=("dS", 16), waits=wwar if half == 0 else [])
                    do("gpsimd", lambda e, w2b=w2b, hc=hc, half=half: e.dma_start(
                        out=w2b[:, half * 2048:(half + 1) * 2048].rearrange("p (dt c) -> p dt c", c=128),
                        in_=w2t_d.rearrange("(dt p) t -> p dt t", p=128)[:, half * 16:(half + 1) * 16, hc * 128:(hc + 1) * 128]),
                       inc=("dG", 16), waits=wwar if half == 0 else [])
                ldw1 = CNT["dS"]
                ldw2 = CNT["dG"]
                ht = s_ht[hc % 2]
                for tc in range(4):
                    p1 = P[tc % 2]
                    p2 = P[2 + tc % 2]
                    for dt in range(32):
                        do("tensor", lambda e, p1=p1, w1b=w1b, dt=dt, tc=tc: e.matmul(
                            p1[:], w1b[:, dt * 128:(dt + 1) * 128],
                            s_xn2[:, dt * 2048 + tc * 512: dt * 2048 + (tc + 1) * 512],
                            start=(dt == 0), stop=(dt == 31)),
                           inc=("cT", 1),
                           waits=[("cV", xnv), ("dS", ldw1), ("cA", CNT["cA"])] if dt == 0 else [])
                    do("scalar", lambda e, p1=p1: e.activation(out=s_sg[:], in_=p1[:], func=AF.Silu),
                       inc=("cA", 1), waits=[("cT", CNT["cT"]), ("cV", CNT["cV"])])
                    for dt in range(32):
                        do("tensor", lambda e, p2=p2, w2b=w2b, dt=dt, tc=tc: e.matmul(
                            p2[:], w2b[:, dt * 128:(dt + 1) * 128],
                            s_xn2[:, dt * 2048 + tc * 512: dt * 2048 + (tc + 1) * 512],
                            start=(dt == 0), stop=(dt == 31)),
                           inc=("cT", 1),
                           waits=[("dG", ldw2), ("cV", CNT["cV"])] if dt == 0 else [])
                    do("vector", lambda e, p2=p2, ht=ht, tc=tc: e.tensor_tensor(
                        out=ht[:, tc * 512:(tc + 1) * 512], in0=p2[:], in1=s_sg[:], op=ALU.mult),
                       inc=("cV", 1), waits=[("cT", CNT["cT"]), ("cA", CNT["cA"]), ("dS", CNT["dS"])])
                do("sync", lambda e, ht=ht, hc=hc: e.dma_start(
                    out=h_d[hc * 128:(hc + 1) * 128, :], in_=ht[:]),
                   inc=("dS", 16), waits=[("cV", CNT["cV"])])
                hc_done[hc] = CNT["cT"]

        run_phase(p5b)
        barrier()
        p5b_stack.close()

        p6_stack = ExitStack()
        e6 = p6_stack.enter_context
        s_H = e6(nc.sbuf_tensor("s_H", [128, NHC * 512], BF16))
        s_w3 = [e6(nc.sbuf_tensor(f"s_w3_{i}", [128, 8 * 256], BF16)) for i in range(3)]
        s_r26 = [e6(nc.sbuf_tensor(f"s_r26_{i}", [128, 512], F32)) for i in range(2)]
        s_of = [e6(nc.sbuf_tensor(f"s_of{i}", [128, 512], F32)) for i in range(2)]
        s_otm = [e6(nc.sbuf_tensor(f"s_otm{i}", [128, D], F32)) for i in range(4)]

        def p6():
            NG = (NHC + 7) // 8
            g_done = {}
            tq_done = {}
            for tq in range(4):
                hw = [("cT", tq_done[tq - 1])] if tq >= 1 else []
                for g in range(NG):
                    n = min(8, NHC - g * 8)
                    do("sync", lambda e, g=g, n=n, tq=tq: e.dma_start(
                        out=s_H[:, g * 8 * 512:(g * 8 + n) * 512].rearrange("p (j c) -> p j c", c=512),
                        in_=h_d.rearrange("(j p) t -> p j t", p=128)[:, g * 8:g * 8 + n, tq * 512:(tq + 1) * 512]),
                       inc=("dS", 16), waits=hw if g == 0 else [])
                ldh = CNT["dS"]
                for og in range(16):
                    u = tq * 16 + og
                    for g in range(NG):
                        n = min(8, NHC - g * 8)
                        gi = u * NG + g
                        wb = s_w3[gi % 3]
                        do("gpsimd", lambda e, wb=wb, g=g, n=n, og=og: e.dma_start(
                            out=wb[:, 0:n * 256].rearrange("p (j c) -> p j c", c=256),
                            in_=w3t_d.rearrange("(j p) t -> p j t", p=128)[:, g * 8:g * 8 + n, og * 256:(og + 1) * 256]),
                           inc=("dG", 16),
                           waits=[("cT", g_done[gi - 3])] if gi >= 3 else [])
                        gld = CNT["dG"]
                        for j in range(n):
                            hc = g * 8 + j
                            for sub in range(2):
                                do("tensor", lambda e, wb=wb, j=j, sub=sub, hc=hc: e.matmul(
                                    P[sub][:], wb[:, j * 256 + sub * 128: j * 256 + (sub + 1) * 128],
                                    s_H[:, hc * 512:(hc + 1) * 512],
                                    start=(hc == 0), stop=(hc == NHC - 1), skip_group_check=True),
                                   inc=("cT", 1),
                                   waits=([("dS", ldh), ("dG", gld), ("cV", CNT["cV"])] if hc == 0 else
                                          ([("dG", gld)] if j == 0 else [])))
                        g_done[gi] = CNT["cT"]
                    mmv = CNT["cT"]
                    for sub in range(2):
                        ot = og * 2 + sub
                        r26 = s_r26[sub]
                        do("sync", lambda e, r26=r26, ot=ot, tq=tq: e.dma_start(
                            out=r26[:], in_=r2_d[ot * 128:(ot + 1) * 128, tq * 512:(tq + 1) * 512]),
                           inc=("dS", 16), waits=[("cV", CNT["cV"])] if u >= 1 else [])
                        of = s_of[sub]
                        do("vector", lambda e, sub=sub, r26=r26, of=of: e.tensor_tensor(
                            out=of[:], in0=P[sub][:], in1=r26[:], op=ALU.add),
                           inc=("cV", 1), waits=[("cT", mmv), ("dS", CNT["dS"])])
                        ofv = CNT["cV"]
                        for tt in range(4):
                            do("tensor", lambda e, of=of, tt=tt, sub=sub: e.transpose(
                                out=P[4 + sub][:, tt * 128:(tt + 1) * 128], in_=of[:, tt * 128:(tt + 1) * 128],
                                identity=s_iden[:]),
                               inc=("cT", 1), waits=[("cV", ofv)] if tt == 0 else [])
                        trv = CNT["cT"]
                        for tt in range(4):
                            do("vector", lambda e, tt=tt, sub=sub, ot=ot: e.tensor_copy(
                                s_otm[tt][:, ot * 128:(ot + 1) * 128], P[4 + sub][:, tt * 128:(tt + 1) * 128]),
                               inc=("cV", 1), waits=[("cT", trv), ("dS", CNT["dS"])] if tt == 0 else [])
                for tt in range(4):
                    do("sync", lambda e, tt=tt, tq=tq: e.dma_start(
                        out=y_own[tq * 512 + tt * 128: tq * 512 + (tt + 1) * 128, :], in_=s_otm[tt][:]),
                       inc=("dS", 16), waits=[("cV", CNT["cV"])] if tt == 0 else [])
                tq_done[tq] = CNT["cT"]

        run_phase(p6)
        barrier()
        p6_stack.close()

    return nc


_NC_CACHE = None


def kernel(**inputs):
    global _NC_CACHE
    B = 4
    x = np.asarray(inputs["L0_attn_x"], dtype=np.float32)
    wq_w = np.asarray(inputs["L0_attn_wq_w"], dtype=np.float32)
    wq_b = np.asarray(inputs["L0_attn_wq_b"], dtype=np.float32)
    wk_w = np.asarray(inputs["L0_attn_wk_w"], dtype=np.float32)
    wk_b = np.asarray(inputs["L0_attn_wk_b"], dtype=np.float32)
    wv_w = np.asarray(inputs["L0_attn_wv_w"], dtype=np.float32)
    wv_b = np.asarray(inputs["L0_attn_wv_b"], dtype=np.float32)
    wo_w = np.asarray(inputs["L0_attn_wo_w"], dtype=np.float32)
    wo_b = np.asarray(inputs["L0_attn_wo_b"], dtype=np.float32)

    def codes_u8_T(a):
        t = np.ascontiguousarray(np.asarray(a).T)
        return np.ascontiguousarray(t.view(np.uint8)[:, 0::4])

    w1qT = codes_u8_T(inputs["L0_ffn_w1_q"])
    w2qT = codes_u8_T(inputs["L0_ffn_w2_q"])
    w3qT = codes_u8_T(inputs["L0_ffn_w3_q"])
    sc1T = np.ascontiguousarray(np.asarray(inputs["L0_ffn_w1_scale"], dtype=np.float32).T)
    sc2T = np.ascontiguousarray(np.asarray(inputs["L0_ffn_w2_scale"], dtype=np.float32).T)
    sc3T = np.ascontiguousarray(np.asarray(inputs["L0_ffn_w3_scale"], dtype=np.float32).T)

    tri = np.tril(np.ones((128, 128), dtype=np.float32)).T.astype(ml_dtypes.bfloat16)
    iden = np.eye(128, dtype=np.float32)
    ones128 = np.ones((128, 128), dtype=ml_dtypes.bfloat16)
    zeros128 = np.zeros((128, 128), dtype=ml_dtypes.bfloat16)

    if _NC_CACHE is None:
        _NC_CACHE = build_nc()
    nc = _NC_CACHE

    in_maps = []
    for c in range(8):
        b, h = c // 2, c % 2
        xt = x[b].reshape(32, 128, D)
        if h == 0:
            perm = np.arange(32)
        else:
            perm = np.arange(32).reshape(16, 2)[:, ::-1].reshape(32)
        x_perm = np.ascontiguousarray(xt[perm].reshape(T, D))
        x_own_a = np.ascontiguousarray(xt[perm[0::2]].reshape(TO, D))
        in_maps.append({
            "x_all": x_perm, "x_own": x_own_a,
            "wq": wq_w, "wo": wo_w, "wk": wk_w, "wv": wv_w,
            "wq_b": wq_b, "wo_b": wo_b, "wk_b": wk_b, "wv_b": wv_b,
            "w1qT": w1qT, "w2qT": w2qT, "w3qT": w3qT,
            "sc1T": sc1T, "sc2T": sc2T, "sc3T": sc3T,
            "tri_in": tri, "mB_in": zeros128 if h == 0 else ones128,
            "iden_in": iden,
        })

    res = run_bass_kernel_spmd(nc, in_maps, core_ids=list(range(8)))

    out = np.empty((B, T, D), dtype=np.float32)
    for c in range(8):
        b, h = c // 2, c % 2
        yo = np.asarray(res.results[c]["y_own"], dtype=np.float32).reshape(16, 128, D)
        ob = out[b].reshape(32, 128, D)
        for k in range(16):
            ob[2 * k + h] = yo[k]
    return out


# version 07
# speedup vs baseline: 1.0558x; 1.0558x over previous
import math
import numpy as np
import ml_dtypes
from contextlib import ExitStack

import concourse.bass as bass
import concourse.mybir as mybir
from concourse.bass_utils import run_bass_kernel_spmd

F32 = mybir.dt.float32
BF16 = mybir.dt.bfloat16
U8 = mybir.dt.uint8
AF = mybir.ActivationFunctionType
ALU = mybir.AluOpType

D = 4096
T = 4096
TO = 2048
HID = 11008
NHC = HID // 128
EPS = 1e-5
INV_SQRT_HD = 1.0 / math.sqrt(128.0)

W12_ROWS = D
W3_ROWS = HID
FD_Q = 512

NF4 = [-1.0, -0.6961928009986877, -0.5250730514526367, -0.39491748809814453,
       -0.28444138169288635, -0.18477343022823334, -0.09105003625154495, 0.0,
       0.07958029955625534, 0.16093020141124725, 0.24611230194568634,
       0.33791524171829224, 0.44070982933044434, 0.5626170039176941,
       0.7229568362236023, 1.0]

ENGS = ["sync", "gpsimd", "scalar", "vector", "tensor"]


def build_nc():
    nc = bass.Bass("TRN2", target_bir_lowering=False, num_devices=8)

    x_all = nc.dram_tensor("x_all", [T, D], F32, kind="ExternalInput").ap()
    x_own = nc.dram_tensor("x_own", [TO, D], F32, kind="ExternalInput").ap()
    wq = nc.dram_tensor("wq", [D, D], F32, kind="ExternalInput").ap()
    wo = nc.dram_tensor("wo", [D, D], F32, kind="ExternalInput").ap()
    wk = nc.dram_tensor("wk", [128, D], F32, kind="ExternalInput").ap()
    wv = nc.dram_tensor("wv", [128, D], F32, kind="ExternalInput").ap()
    wq_b = nc.dram_tensor("wq_b", [D], F32, kind="ExternalInput").ap()
    wo_b = nc.dram_tensor("wo_b", [D], F32, kind="ExternalInput").ap()
    wk_b = nc.dram_tensor("wk_b", [128], F32, kind="ExternalInput").ap()
    wv_b = nc.dram_tensor("wv_b", [128], F32, kind="ExternalInput").ap()
    w1qT = nc.dram_tensor("w1qT", [W12_ROWS, HID], U8, kind="ExternalInput").ap()
    w2qT = nc.dram_tensor("w2qT", [W12_ROWS, HID], U8, kind="ExternalInput").ap()
    w3qT = nc.dram_tensor("w3qT", [W3_ROWS, D], U8, kind="ExternalInput").ap()
    sc1T = nc.dram_tensor("sc1T", [64, HID], F32, kind="ExternalInput").ap()
    sc2T = nc.dram_tensor("sc2T", [64, HID], F32, kind="ExternalInput").ap()
    sc3T = nc.dram_tensor("sc3T", [172, D], F32, kind="ExternalInput").ap()
    tri_in = nc.dram_tensor("tri_in", [128, 128], BF16, kind="ExternalInput").ap()
    mB_in = nc.dram_tensor("mB_in", [128, 128], BF16, kind="ExternalInput").ap()
    iden_in = nc.dram_tensor("iden_in", [128, 128], F32, kind="ExternalInput").ap()

    y_own = nc.dram_tensor("y_own", [TO, D], F32, kind="ExternalOutput").ap()
    wqT_d = nc.dram_tensor("wqT_d", [D, D], BF16, kind="Internal").ap()
    woT_d = nc.dram_tensor("woT_d", [D, D], BF16, kind="Internal").ap()
    xn1_d = nc.dram_tensor("xn1_d", [D, TO], BF16, kind="Internal").ap()
    q_d = nc.dram_tensor("q_d", [D, TO], BF16, kind="Internal").ap()
    attn_d = nc.dram_tensor("attn_d", [D, TO], BF16, kind="Internal").ap()
    r2_d = nc.dram_tensor("r2_d", [D, TO], F32, kind="Internal").ap()
    w1t_d = nc.dram_tensor("w1t_d", [D, HID], BF16, kind="Internal").ap()
    w2t_d = nc.dram_tensor("w2t_d", [D, HID], BF16, kind="Internal").ap()
    w3t_d = nc.dram_tensor("w3t_d", [HID, D], BF16, kind="Internal").ap()
    h_d = nc.dram_tensor("h_d", [HID, TO], BF16, kind="Internal").ap()
    mrow_d = nc.dram_tensor("mrow_d", [8, T], F32, kind="Internal").ap()

    with ExitStack() as top:
        ent = top.enter_context

        P = [ent(nc.psum_tensor(f"pb{i}", [128, 512], F32)) for i in range(8)]

        sem_names = ["dS", "dG", "dA", "cT", "cA", "cV", "cG", "qD", "qV", "qA", "qC"]
        SEMS = {n: ent(nc.semaphore(n)) for n in sem_names}
        CNT = {n: 0 for n in sem_names}
        ROLE = [None]

        def eng(name):
            return getattr(nc, name)

        def do(engname, f, inc=None, waits=()):
            if ROLE[0] == engname:
                e = eng(engname)
                for s, v in waits:
                    if v > 0:
                        e.wait_ge(SEMS[s], v)
                r = f(e)
                if inc is not None:
                    r.then_inc(SEMS[inc[0]], inc[1])
            if inc is not None:
                CNT[inc[0]] += inc[1]

        block = ent(nc.Block())


        def mk_bufset(e, tag, fd, nb):
            R = dict(side="right")
            return {
                "cd": [e(nc.sbuf_tensor(f"dq_cd{tag}{i}", [128, fd], U8, **R)) for i in range(nb)],
                "scb": [e(nc.sbuf_tensor(f"dq_sc{tag}{i}", [128, fd], F32, **R)) for i in range(nb)],
                "mk": [e(nc.sbuf_tensor(f"dq_mk{tag}{i}", [128, fd], U8, **R)) for i in range(8)],
                "lv": [e(nc.sbuf_tensor(f"dq_lv{tag}{i}", [128, 8 * fd], F32, **R)) for i in range(2)],
                "wd": [e(nc.sbuf_tensor(f"dq_wd{tag}{i}", [128, fd], BF16, **R)) for i in range(2)],
                "fd": fd, "nb": nb,
            }
        bufsetA = mk_bufset(ent, "A", FD_Q, 3)
        BUFSETS = {"A": bufsetA}

        class Dequant:
            def __init__(self):
                self.units = []
                self.mark_after = {}
                nch12 = (HID + FD_Q - 1) // FD_Q
                for c in range(nch12):
                    fd = min(FD_Q, HID - c * FD_Q)
                    for w, (qT, scT, dst) in enumerate([(w1qT, sc1T, w1t_d),
                                                        (w2qT, sc2T, w2t_d)]):
                        for rt in range(32):
                            self.units.append((qT, scT, dst, rt, c * FD_Q, fd))
                    self.mark_after[len(self.units) - 1] = ("m12", c)
                nch3 = (D + FD_Q - 1) // FD_Q
                for rt in range(W3_ROWS // 128):
                    for c in range(nch3):
                        fd = min(FD_Q, D - c * FD_Q)
                        self.units.append((w3qT, sc3T, w3t_d, rt, c * FD_Q, fd))
                    self.mark_after[len(self.units) - 1] = ("m3", rt)
                nu = len(self.units)
                gl = [("ld", 0), ("ld", 1)]
                for k in range(nu):
                    gl.append(("m01", k))
                    gl.append(("m23", k))
                    if k + 2 < nu:
                        gl.append(("ld", k + 2))
                    for j in range(4):
                        gl.append(("leaf", (k, j)))
                    gl.append(("cp12", k))
                    gl.append(("cp34", k))
                    gl.append(("cp56", k))
                    gl.append(("cp7", k))
                    gl.append(("sc", k))
                    gl.append(("st", k))
                self.glist = gl
                self.i = 0
                self.m12 = {}
                self.m3 = {}

            def emit_group(self, g):
                kind, pay = g
                k = pay[0] if isinstance(pay, tuple) else pay
                qT, scT, dst, rt, c0, fd = self.units[k]
                bs = BUFSETS["A"]
                nb = bs["nb"]
                cd = bs["cd"][k % nb]
                scb = bs["scb"][k % nb]
                mk = bs["mk"][(k % 2) * 4:(k % 2) * 4 + 4]
                lv = bs["lv"][k % 2]
                lfd = bs["fd"]
                wd = bs["wd"][k % 2]
                if kind == "ld":
                    wv = []
                    if k >= nb:
                        wv = [("qV", 11 * (k - nb) + 4), ("cG", k - nb + 1)]
                    do("scalar", lambda e: e.dma_start(
                        out=cd[:, 0:fd], in_=qT[rt * 128:(rt + 1) * 128, c0:c0 + fd]),
                       inc=("qD", 16), waits=wv)
                    do("scalar", lambda e: e.dma_start(
                        out=scb[0:64, 0:fd], in_=scT[2 * rt:2 * rt + 1, c0:c0 + fd].broadcast_to((64, fd))),
                       inc=("qD", 16))
                    do("scalar", lambda e: e.dma_start(
                        out=scb[64:128, 0:fd], in_=scT[2 * rt + 1:2 * rt + 2, c0:c0 + fd].broadcast_to((64, fd))),
                       inc=("qD", 16))
                elif kind == "m01":
                    do("vector", lambda e: e.tensor_scalar(
                        out=mk[0][:, 0:fd], in0=cd[:, 0:fd], scalar1=1, scalar2=None, op0=ALU.bitwise_and),
                       inc=("qV", 1), waits=[("qD", 64 * k + 48), ("qA", 8 * (k - 1)), ("qV", 11 * (k - 1))])
                    do("vector", lambda e: e.tensor_scalar(
                        out=mk[1][:, 0:fd], in0=cd[:, 0:fd], scalar1=2, scalar2=None, op0=ALU.bitwise_and),
                       inc=("qV", 1))
                elif kind == "m23":
                    do("vector", lambda e: e.tensor_scalar(
                        out=mk[2][:, 0:fd], in0=cd[:, 0:fd], scalar1=4, scalar2=None, op0=ALU.bitwise_and),
                       inc=("qV", 1))
                    do("vector", lambda e: e.tensor_scalar(
                        out=mk[3][:, 0:fd], in0=cd[:, 0:fd], scalar1=8, scalar2=None, op0=ALU.bitwise_and),
                       inc=("qV", 1))
                elif kind == "leaf":
                    j2 = pay[1]
                    for j in (2 * j2, 2 * j2 + 1):
                        dd = float(NF4[2 * j + 1] - NF4[2 * j])
                        bb = float(NF4[2 * j])
                        do("scalar", lambda e, j=j, dd=dd, bb=bb: e.activation(
                            out=lv[:, j * lfd: j * lfd + fd], in_=mk[0][:, 0:fd],
                            func=AF.Copy, bias=bb, scale=dd),
                           inc=("qA", 1),
                           waits=[("qV", 11 * k + 4), ("cG", k - 1)] if (j2 == 0 and j % 2 == 0) else [])
                elif kind in ("cp12", "cp34"):
                    ii = (0, 1) if kind == "cp12" else (2, 3)
                    for i in ii:
                        do("vector", lambda e, i=i: e.copy_predicated(
                            out=lv[:, (2 * i) * lfd:(2 * i) * lfd + fd], mask=mk[1][:, 0:fd],
                            data=lv[:, (2 * i + 1) * lfd:(2 * i + 1) * lfd + fd]),
                           inc=("qV", 1), waits=[("qA", 8 * k + 8)] if i == 0 else [])
                elif kind == "cp56":
                    for i in range(2):
                        do("vector", lambda e, i=i: e.copy_predicated(
                            out=lv[:, (4 * i) * lfd:(4 * i) * lfd + fd], mask=mk[2][:, 0:fd],
                            data=lv[:, (4 * i + 2) * lfd:(4 * i + 2) * lfd + fd]),
                           inc=("qV", 1))
                elif kind == "cp7":
                    do("vector", lambda e: e.copy_predicated(
                        out=lv[:, 0:fd], mask=mk[3][:, 0:fd], data=lv[:, 4 * lfd:4 * lfd + fd]),
                       inc=("qV", 1))
                elif kind == "sc":
                    wv = [("qV", 11 * k + 11)]
                    if k >= 2:
                        wv.append(("qD", 64 * (k - 2) + 64))
                    do("gpsimd", lambda e: e.tensor_tensor(
                        out=wd[:, 0:fd], in0=lv[:, 0:fd], in1=scb[:, 0:fd], op=ALU.mult),
                       inc=("cG", 1), waits=wv)
                elif kind == "st":
                    do("scalar", lambda e: e.dma_start(
                        out=dst[rt * 128:(rt + 1) * 128, c0:c0 + fd], in_=wd[:, 0:fd]),
                       inc=("qD", 16), waits=[("cG", k + 1)])
                    if kind == "st" and k in self.mark_after:
                        mk2, idx = self.mark_after[k]
                        if mk2 == "m12":
                            self.m12[idx] = CNT["qD"]
                        else:
                            self.m3[idx] = CNT["qD"]

            def pump(self, n):
                while n > 0 and self.i < len(self.glist):
                    self.emit_group(self.glist[self.i])
                    self.i += 1
                    n -= 1

            def ensure12(self, c):
                while c not in self.m12:
                    self.pump(1)
                return self.m12[c]

            def ensure3(self, rt):
                while rt not in self.m3:
                    self.pump(1)
                return self.m3[rt]

            def drain_all(self):
                while self.i < len(self.glist):
                    self.emit_group(self.glist[self.i])
                    self.i += 1

        DQ = Dequant()

        def run_phase(fn):
            start = dict(CNT)
            dqs = (DQ.i, dict(DQ.m12), dict(DQ.m3))
            final = None
            for role in ENGS:
                CNT.clear()
                CNT.update(start)
                DQ.i, m12s, m3s = dqs[0], dict(dqs[1]), dict(dqs[2])
                DQ.m12, DQ.m3 = m12s, m3s
                ROLE[0] = role
                sec = getattr(block, role)

                def _body(e, fn=fn):
                    fn()
                sec(_body)
                if final is None:
                    final = (dict(CNT), DQ.i, dict(DQ.m12), dict(DQ.m3))
                else:
                    assert final == (CNT, DQ.i, DQ.m12, DQ.m3), "nondeterministic phase emission"
            ROLE[0] = None

        def barrier():
            vals = dict(CNT)

            def _b():
                e = eng(ROLE[0])
                for s, v in vals.items():
                    if v > 0:
                        e.wait_ge(SEMS[s], v)
            run_phase(_b)

        s_tri = ent(nc.sbuf_tensor("s_tri", [128, 128], BF16))
        s_mB = ent(nc.sbuf_tensor("s_mB", [128, 128], BF16))
        s_iden = ent(nc.sbuf_tensor("s_iden", [128, 128], F32))
        s_ones = ent(nc.sbuf_tensor("s_ones", [128, 1], BF16))
        s_onef = ent(nc.sbuf_tensor("s_onef", [1, 128], F32))
        s_wqb = ent(nc.sbuf_tensor("s_wqb", [128, 32], F32))
        s_wob = ent(nc.sbuf_tensor("s_wob", [128, 32], F32))
        s_wkb = ent(nc.sbuf_tensor("s_wkb", [128, 1], F32))
        s_wvb = ent(nc.sbuf_tensor("s_wvb", [128, 1], F32))
        s_eps = ent(nc.sbuf_tensor("s_eps", [128, 1], F32))

        p0_stack = ExitStack()
        kv_stack = ExitStack()
        s_wkT = kv_stack.enter_context(nc.sbuf_tensor("s_wkT", [128, D], BF16))
        s_wvT = kv_stack.enter_context(nc.sbuf_tensor("s_wvT", [128, D], BF16))
        s_KF = kv_stack.enter_context(nc.sbuf_tensor("s_KF", [128, T], BF16))
        s_VT = kv_stack.enter_context(nc.sbuf_tensor("s_VT", [128, T], BF16))
        e0 = p0_stack.enter_context
        s_wfl = e0(nc.sbuf_tensor("s_wfl", [128, 2 * D], F32))
        s_wtmp = [e0(nc.sbuf_tensor(f"s_wtmp{i}", [128, D], F32)) for i in range(2)]
        s_wt4 = [e0(nc.sbuf_tensor(f"s_wt4_{i}", [128, 512], BF16)) for i in range(3)]
        s_bt = e0(nc.sbuf_tensor("s_bt", [32, 128], F32))

        def p0():
            do("sync", lambda e: e.dma_start(out=s_tri[:], in_=tri_in), inc=("dS", 16))
            do("sync", lambda e: e.dma_start(out=s_mB[:], in_=mB_in), inc=("dS", 16))
            do("sync", lambda e: e.dma_start(out=s_iden[:], in_=iden_in), inc=("dS", 16))
            do("sync", lambda e: e.dma_start(out=s_wkb[:], in_=wk_b.unsqueeze(1)), inc=("dS", 16))
            do("sync", lambda e: e.dma_start(out=s_wvb[:], in_=wv_b.unsqueeze(1)), inc=("dS", 16))
            do("sync", lambda e: e.dma_start(out=s_wfl[:, 0:D], in_=wk), inc=("dS", 16))
            do("sync", lambda e: e.dma_start(out=s_wfl[:, D:2 * D], in_=wv), inc=("dS", 16))
            do("sync", lambda e: e.dma_start(out=s_bt[:], in_=wq_b.rearrange("(t p) -> t p", p=128)), inc=("dS", 16))
            do("vector", lambda e: e.memset(s_ones[:], 1.0), inc=("cV", 1))
            do("vector", lambda e: e.memset(s_onef[:], 1.0), inc=("cV", 1))
            do("vector", lambda e: e.memset(s_eps[:], EPS), inc=("cV", 1))
            ld8 = CNT["dS"]
            do("tensor", lambda e: e.transpose(out=P[0][:, 0:32], in_=s_bt[:], identity=s_iden[0:32, 0:32]),
               inc=("cT", 1), waits=[("dS", ld8)])
            do("vector", lambda e: e.tensor_copy(s_wqb[:], P[0][:, 0:32]), inc=("cV", 1),
               waits=[("cT", CNT["cT"])])
            do("sync", lambda e: e.dma_start(out=s_bt[:], in_=wo_b.rearrange("(t p) -> t p", p=128)),
               inc=("dS", 16), waits=[("cV", CNT["cV"])])
            do("tensor", lambda e: e.transpose(out=P[1][:, 0:32], in_=s_bt[:], identity=s_iden[0:32, 0:32]),
               inc=("cT", 1), waits=[("dS", CNT["dS"])])
            do("vector", lambda e: e.tensor_copy(s_wob[:], P[1][:, 0:32]), inc=("cV", 1),
               waits=[("cT", CNT["cT"])])

            for w in range(2):
                dst = s_wkT if w == 0 else s_wvT
                for j in range(32):
                    par = (w * 32 + j) % 2
                    do("tensor", lambda e, par=par, w=w, j=j: e.transpose(
                        out=P[2 + par][:, 0:128], in_=s_wfl[:, w * D + j * 128: w * D + (j + 1) * 128],
                        identity=s_iden[:]),
                       inc=("cT", 1), waits=[("dS", ld8), ("cV", CNT["cV"] - 1)])
                    do("vector", lambda e, par=par, dst=dst, j=j: e.tensor_copy(
                        dst[:, j * 128:(j + 1) * 128], P[2 + par][:, 0:128]),
                       inc=("cV", 1), waits=[("cT", CNT["cT"])])

            for w in range(2):
                src = wq if w == 0 else wo
                dstd = wqT_d if w == 0 else woT_d
                for ot in range(32):
                    buf = s_wtmp[ot % 2]
                    do("sync", lambda e, src=src, ot=ot, buf=buf: e.dma_start(
                        out=buf[:], in_=src[ot * 128:(ot + 1) * 128, :]),
                       inc=("dS", 16), waits=[("cT", CNT["cT"])])
                    ldv = CNT["dS"]
                    for jb in range(8):
                        k = (w * 32 + ot) * 8 + jb
                        bb = s_wt4[k % 3]
                        par = 4 + (k % 2)
                        for ji in range(4):
                            j = jb * 4 + ji
                            do("tensor", lambda e, par=par, buf=buf, j=j, ji=ji: e.transpose(
                                out=P[par][:, ji * 128:(ji + 1) * 128], in_=buf[:, j * 128:(j + 1) * 128],
                                identity=s_iden[:]),
                               inc=("cT", 1),
                               waits=[("dS", ldv), ("cV", CNT["cV"])] if ji == 0 else [])
                        do("vector", lambda e, par=par, bb=bb: e.tensor_copy(
                            bb[:], P[par][:]),
                           inc=("cV", 1), waits=[("cT", CNT["cT"]), ("dA", CNT["dA"] - 2 * 16)])
                        do("scalar", lambda e, dstd=dstd, jb=jb, ot=ot, bb=bb: e.dma_start(
                            out=dstd.rearrange("(j p) t -> p j t", p=128)[:, jb * 4:(jb + 1) * 4, ot * 128:(ot + 1) * 128],
                            in_=bb[:].rearrange("p (j c) -> p j c", c=128)),
                           inc=("dA", 16), waits=[("cV", CNT["cV"])])
                    DQ.pump(8)

        run_phase(p0)
        barrier()
        p0_stack.close()

        p12_stack = ExitStack()
        e12 = p12_stack.enter_context
        s_xt = [e12(nc.sbuf_tensor(f"s_xt{i}", [128, D], F32)) for i in range(4)]
        s_xn1 = e12(nc.sbuf_tensor("s_xn1", [128, 32 * 512], BF16))
        s_bns = e12(nc.sbuf_tensor("s_bns", [128, 48], F32))
        s_mv = e12(nc.sbuf_tensor("s_mv", [128, 8], F32))
        s_cat = e12(nc.sbuf_tensor("s_cat", [128, 8], F32))
        s_g8 = e12(nc.sbuf_tensor("s_g8", [8, 128], F32))
        s_invb = e12(nc.sbuf_tensor("s_invb", [128, 512], F32))
        s_mib = e12(nc.sbuf_tensor("s_mib", [128, 512], F32))
        s_t1 = e12(nc.sbuf_tensor("s_t1", [128, 512], F32))
        s_vf = e12(nc.sbuf_tensor("s_vf", [128, 512], F32))

        def p12():
            for ch in range(8):
                for i in range(4):
                    t = ch * 4 + i
                    buf = s_xt[t % 4]
                    do("sync", lambda e, t=t, buf=buf: e.dma_start(
                        out=buf[:], in_=x_all[t * 128:(t + 1) * 128, :]),
                       inc=("dS", 16), waits=[("cT", CNT["cT"])] if t >= 4 else [])
                ld = CNT["dS"]
                for i in range(4):
                    buf = s_xt[(ch * 4 + i) % 4]
                    for sch in range(8):
                        w = [("dS", ld - (3 - i) * 16)]
                        if i == 0 and sch == 0:
                            w.append(("cA", CNT["cA"]))
                        do("vector", lambda e, buf=buf, sch=sch: e.bn_stats(
                            out=s_bns[:, sch * 6:(sch + 1) * 6], in_=buf[:, sch * 512:(sch + 1) * 512]),
                           inc=("cV", 1), waits=w)
                    do("vector", lambda e, i=i: e.bn_aggr(
                        out=s_mv[:, 2 * i:2 * i + 2], in_=s_bns[:]), inc=("cV", 1))
                do("scalar", lambda e: e.activation(
                    out=s_cat[:, 0:4], in_=s_mv[:].rearrange("p (i two) -> p two i", two=2)[:, 1, :],
                    func=AF.Sqrt, bias=s_eps[:, 0:1]), inc=("cA", 1), waits=[("cV", CNT["cV"])])
                do("vector", lambda e: e.reciprocal(s_cat[:, 0:4], s_cat[:, 0:4]),
                   inc=("cV", 1), waits=[("cA", CNT["cA"])])
                do("vector", lambda e: e.tensor_tensor(
                    out=s_cat[:, 4:8], in0=s_mv[:].rearrange("p (i two) -> p two i", two=2)[:, 0, :],
                    in1=s_cat[:, 0:4], op=ALU.mult), inc=("cV", 1))
                do("tensor", lambda e: e.transpose(out=P[6][0:8, 0:128], in_=s_cat[:], identity=s_iden[:]),
                   inc=("cT", 1), waits=[("cV", CNT["cV"])])
                do("vector", lambda e: e.tensor_copy(s_g8[:], P[6][0:8, 0:128]),
                   inc=("cV", 1), waits=[("cT", CNT["cT"]), ("dS", CNT["dS"])])
                do("sync", lambda e, ch=ch: e.dma_start(
                    out=mrow_d[0, ch * 512:(ch + 1) * 512].rearrange("(i c) -> i c", i=4), in_=s_g8[0:4, :]),
                   inc=("dS", 16), waits=[("cV", CNT["cV"])])
                do("sync", lambda e, ch=ch: e.dma_start(
                    out=mrow_d[1, ch * 512:(ch + 1) * 512].rearrange("(i c) -> i c", i=4), in_=s_g8[4:8, :]),
                   inc=("dS", 16))
                wrow = CNT["dS"]
                do("sync", lambda e, ch=ch: e.dma_start(
                    out=s_invb[:], in_=mrow_d[0:1, ch * 512:(ch + 1) * 512].broadcast_to((128, 512))),
                   inc=("dS", 16), waits=[("dS", wrow), ("cV", CNT["cV"])])
                do("sync", lambda e, ch=ch: e.dma_start(
                    out=s_mib[:], in_=mrow_d[1:2, ch * 512:(ch + 1) * 512].broadcast_to((128, 512))),
                   inc=("dS", 16))
                ldb = CNT["dS"]
                for dt in range(32):
                    par = dt % 2
                    for i in range(4):
                        buf = s_xt[(ch * 4 + i) % 4]
                        w = [("dS", ldb), ("cV", CNT["cV"] - 2)] if i == 0 else []
                        do("tensor", lambda e, par=par, buf=buf, dt=dt, i=i: e.transpose(
                            out=P[par][:, i * 128:(i + 1) * 128], in_=buf[:, dt * 128:(dt + 1) * 128],
                            identity=s_iden[:]),
                           inc=("cT", 1), waits=w)
                    do("vector", lambda e, par=par: e.tensor_tensor(
                        out=s_t1[:], in0=P[par][:], in1=s_invb[:], op=ALU.mult),
                       inc=("cV", 1),
                       waits=[("cT", CNT["cT"])] + ([("dG", CNT["dG"])] if dt == 0 else []))
                    do("vector", lambda e, dt=dt: e.tensor_tensor(
                        out=s_xn1[:, dt * 512:(dt + 1) * 512], in0=s_t1[:], in1=s_mib[:], op=ALU.subtract),
                       inc=("cV", 1))
                    if dt % 4 == 3:
                        DQ.pump(3)
                xnv = CNT["cV"]
                for w in range(2):
                    wT = s_wkT if w == 0 else s_wvT
                    pb = P[2 + w]
                    for dt in range(32):
                        do("tensor", lambda e, pb=pb, wT=wT, dt=dt: e.matmul(
                            pb[:], wT[:, dt * 128:(dt + 1) * 128], s_xn1[:, dt * 512:(dt + 1) * 512],
                            start=(dt == 0), stop=(dt == 31)),
                           inc=("cT", 1), waits=[("cV", xnv), ("cA", CNT["cA"])] if dt == 0 else [])
                    if w == 0:
                        do("scalar", lambda e, ch=ch: e.activation(
                            out=s_KF[:, ch * 512:(ch + 1) * 512], in_=P[2][:], func=AF.Identity,
                            bias=s_wkb[:, 0:1]), inc=("cA", 1), waits=[("cT", CNT["cT"])])
                    else:
                        do("scalar", lambda e: e.activation(
                            out=s_vf[:], in_=P[3][:], func=AF.Identity, bias=s_wvb[:, 0:1]),
                           inc=("cA", 1), waits=[("cT", CNT["cT"])])
                        for i in range(4):
                            do("tensor", lambda e, i=i: e.transpose(
                                out=P[5][:, 0:128], in_=s_vf[:, i * 128:(i + 1) * 128], identity=s_iden[:]),
                               inc=("cT", 1), waits=[("cA", CNT["cA"]), ("cV", CNT["cV"])])
                            do("vector", lambda e, ch=ch, i=i: e.tensor_copy(
                                s_VT[:, ch * 512 + i * 128: ch * 512 + (i + 1) * 128], P[5][:, 0:128]),
                               inc=("cV", 1), waits=[("cT", CNT["cT"])])
                for j, i in enumerate((0, 2)):
                    do("gpsimd", lambda e, ch=ch, j=j, i=i: e.dma_start(
                        out=xn1_d.rearrange("(dt p) t -> p dt t", p=128)[:, :, ch * 256 + j * 128: ch * 256 + (j + 1) * 128],
                        in_=s_xn1[:].rearrange("p (dt c) -> p dt c", c=512)[:, :, i * 128:(i + 1) * 128]),
                       inc=("dG", 16), waits=[("cV", xnv)])
                DQ.pump(6)

        run_phase(p12)
        barrier()
        p12_stack.close()

        p2_stack = ExitStack()
        e2 = p2_stack.enter_context
        s_x2 = [e2(nc.sbuf_tensor(f"s_x2_{i}", [128, 32 * 512], BF16)) for i in range(2)]
        s_wq2 = [e2(nc.sbuf_tensor(f"s_wq2_{i}", [128, D], BF16)) for i in range(2)]
        s_qt = [e2(nc.sbuf_tensor(f"s_qt{i}", [128, 512], BF16)) for i in range(2)]

        def p2b():
            tc_done = {}
            for tc in range(4):
                buf = s_x2[tc % 2]
                w = [("cT", tc_done[tc - 2])] if tc >= 2 else []
                for half in range(2):
                    do("sync", lambda e, tc=tc, buf=buf, half=half: e.dma_start(
                        out=buf[:, half * 16 * 512:(half + 1) * 16 * 512].rearrange("p (dt c) -> p dt c", c=512),
                        in_=xn1_d.rearrange("(dt p) t -> p dt t", p=128)[:, half * 16:(half + 1) * 16, tc * 512:(tc + 1) * 512]),
                       inc=("dS", 16), waits=w if half == 0 else [])
                ldx = CNT["dS"]
                for qt in range(32):
                    k = tc * 32 + qt
                    wbuf = s_wq2[k % 2]
                    for half in range(2):
                        do("gpsimd", lambda e, qt=qt, wbuf=wbuf, half=half: e.dma_start(
                            out=wbuf[:, half * 2048:(half + 1) * 2048].rearrange("p (dt c) -> p dt c", c=128),
                            in_=wqT_d.rearrange("(dt p) t -> p dt t", p=128)[:, half * 16:(half + 1) * 16, qt * 128:(qt + 1) * 128]),
                           inc=("dG", 16),
                           waits=[("cT", CNT["cT"] - 32)] if (k >= 2 and half == 0) else [])
                    ldw = CNT["dG"]
                    par = k % 2
                    for dt in range(32):
                        do("tensor", lambda e, par=par, wbuf=wbuf, buf=buf, dt=dt: e.matmul(
                            P[par][:], wbuf[:, dt * 128:(dt + 1) * 128], buf[:, dt * 512:(dt + 1) * 512],
                            start=(dt == 0), stop=(dt == 31)),
                           inc=("cT", 1),
                           waits=[("dS", ldx), ("dG", ldw), ("cA", CNT["cA"])] if dt == 0 else [])
                    qb = s_qt[k % 2]
                    do("scalar", lambda e, par=par, qb=qb, qt=qt: e.activation(
                        out=qb[:], in_=P[par][:], func=AF.Identity, bias=s_wqb[:, qt:qt + 1]),
                       inc=("cA", 1), waits=[("cT", CNT["cT"]), ("dS", CNT["dS"])])
                    do("sync", lambda e, qb=qb, qt=qt, tc=tc: e.dma_start(
                        out=q_d[qt * 128:(qt + 1) * 128, tc * 512:(tc + 1) * 512], in_=qb[:]),
                       inc=("dS", 16), waits=[("cA", CNT["cA"])])
                    DQ.pump(14)
                tc_done[tc] = CNT["cT"]

        run_phase(p2b)
        barrier()
        p2_stack.close()

        p3_stack = ExitStack()
        e3 = p3_stack.enter_context
        s_qh = [e3(nc.sbuf_tensor(f"s_qh{i}", [128, TO], BF16)) for i in range(2)]
        s_pt = [e3(nc.sbuf_tensor(f"s_pt{i}", [128, 512], BF16)) for i in range(3)]
        s_rs = e3(nc.sbuf_tensor("s_rs", [1, 512], F32))
        s_rb = [e3(nc.sbuf_tensor(f"s_rb{i}", [128, 512], F32)) for i in range(2)]
        s_at = [e3(nc.sbuf_tensor(f"s_at{i}", [128, 512], BF16)) for i in range(2)]

        def p3():
            h_done = {}
            for h in range(32):
                qbuf = s_qh[h % 2]
                do("sync", lambda e, h=h, qbuf=qbuf: e.dma_start(
                    out=qbuf[:], in_=q_d[h * 128:(h + 1) * 128, :]),
                   inc=("dS", 16), waits=[("cT", h_done[h - 2])] if h >= 2 else [])
                ldq = CNT["dS"]
                for qc in range(4):
                    u = h * 4 + qc
                    nkt = 8 * qc + 8
                    av = P[2 + (u % 2)]
                    sm = P[4 + (u % 2)]
                    avw = [("cV", CNT["cV"])]
                    for kt in range(nkt):
                        sl = max(0, kt // 2 - 4 * qc)
                        c0 = sl * 128
                        pb = P[kt % 2]
                        ptb = s_pt[kt % 3]
                        do("tensor", lambda e, pb=pb, kt=kt, qbuf=qbuf, qc=qc, c0=c0: e.matmul(
                            pb[:, c0:512], s_KF[:, kt * 128:(kt + 1) * 128],
                            qbuf[:, qc * 512 + c0: (qc + 1) * 512],
                            start=True, stop=True, skip_group_check=True),
                           inc=("cT", 1),
                           waits=([("dS", ldq)] if kt == 0 else []) + [("cA", CNT["cA"] - 1)])
                        do("scalar", lambda e, pb=pb, ptb=ptb, c0=c0: e.activation(
                            out=ptb[:, c0:512], in_=pb[:, c0:512], func=AF.Exp, scale=INV_SQRT_HD),
                           inc=("cA", 1), waits=[("cT", CNT["cT"])])
                        masked = False
                        i_d = kt // 2
                        if 4 * qc <= i_d < 4 * qc + 4:
                            cm = (i_d - 4 * qc) * 128
                            msk = s_tri if kt % 2 == 0 else s_mB
                            do("vector", lambda e, ptb=ptb, cm=cm, msk=msk: e.tensor_tensor(
                                out=ptb[:, cm:cm + 128], in0=ptb[:, cm:cm + 128], in1=msk[:], op=ALU.mult),
                               inc=("cV", 1), waits=[("cA", CNT["cA"])])
                            masked = True
                        pw = [("cA", CNT["cA"])] + ([("cV", CNT["cV"])] if masked else [])
                        do("tensor", lambda e, av=av, kt=kt, ptb=ptb, c0=c0, nkt=nkt: e.matmul(
                            av[:, c0:512], s_VT[:, kt * 128:(kt + 1) * 128], ptb[:, c0:512],
                            start=(kt == 0), stop=(kt == nkt - 1), skip_group_check=True),
                           inc=("cT", 1), waits=pw + (avw if kt == 0 else []))
                        do("tensor", lambda e, sm=sm, kt=kt, ptb=ptb, c0=c0, nkt=nkt: e.matmul(
                            sm[0:1, c0:512], s_ones[:], ptb[:, c0:512],
                            start=(kt == 0), stop=(kt == nkt - 1), skip_group_check=True),
                           inc=("cT", 1))
                    do("vector", lambda e, sm=sm: e.reciprocal(s_rs[:], sm[0:1, :]),
                       inc=("cV", 1), waits=[("cT", CNT["cT"]), ("cA", CNT["cA"] - 1)])
                    pr = P[6 + (u % 2)]
                    do("tensor", lambda e, pr=pr: e.matmul(
                        pr[:], s_onef[:], s_rs[:], start=True, stop=True, skip_group_check=True),
                       inc=("cT", 1), waits=[("cV", CNT["cV"]), ("cA", CNT["cA"] - 1)])
                    rb = s_rb[u % 2]
                    do("scalar", lambda e, pr=pr, rb=rb: e.activation(
                        out=rb[:], in_=pr[:], func=AF.Copy),
                       inc=("cA", 1), waits=[("cT", CNT["cT"]), ("cV", CNT["cV"] - 1)])
                    at = s_at[u % 2]
                    do("vector", lambda e, av=av, rb=rb, at=at: e.tensor_tensor(
                        out=at[:], in0=av[:], in1=rb[:], op=ALU.mult),
                       inc=("cV", 1), waits=[("cA", CNT["cA"]), ("dG", CNT["dG"] - 16)])
                    do("gpsimd", lambda e, at=at, h=h, qc=qc: e.dma_start(
                        out=attn_d[h * 128:(h + 1) * 128, qc * 512:(qc + 1) * 512], in_=at[:]),
                       inc=("dG", 16), waits=[("cV", CNT["cV"])])
                    DQ.pump(14)
                h_done[h] = CNT["cT"]

        run_phase(p3)
        barrier()
        p3_stack.close()
        kv_stack.close()

        p4_stack = ExitStack()
        e4 = p4_stack.enter_context
        s_accS = e4(nc.sbuf_tensor("s_accS", [1, TO], F32))
        s_accQ = e4(nc.sbuf_tensor("s_accQ", [1, TO], F32))
        s_ln2m = e4(nc.sbuf_tensor("s_ln2m", [1, TO], F32))
        s_ln2v = e4(nc.sbuf_tensor("s_ln2v", [1, TO], F32))
        s_ac4 = [e4(nc.sbuf_tensor(f"s_ac4_{i}", [128, 32 * 512], BF16)) for i in range(2)]
        s_wo4 = [e4(nc.sbuf_tensor(f"s_wo4_{i}", [128, D], BF16)) for i in range(2)]
        s_xr = [e4(nc.sbuf_tensor(f"s_xr{i}", [128, 512], F32)) for i in range(2)]
        s_tmp4 = e4(nc.sbuf_tensor("s_tmp4", [128, 512], F32))
        s_r2 = [e4(nc.sbuf_tensor(f"s_r2_{i}", [128, 512], F32)) for i in range(2)]
        s_r2b = [e4(nc.sbuf_tensor(f"s_r2b{i}", [128, 512], BF16)) for i in range(2)]
        s_sq4 = [e4(nc.sbuf_tensor(f"s_sq4_{i}", [128, 512], BF16)) for i in range(2)]

        def p4():
            do("vector", lambda e: e.memset(s_accS[:], 0.0), inc=("cV", 1))
            do("vector", lambda e: e.memset(s_accQ[:], 0.0), inc=("cV", 1))
            qc_done = {}
            for qc in range(4):
                abuf = s_ac4[qc % 2]
                w = [("cT", qc_done[qc - 2])] if qc >= 2 else []
                for half in range(2):
                    do("sync", lambda e, qc=qc, abuf=abuf, half=half: e.dma_start(
                        out=abuf[:, half * 16 * 512:(half + 1) * 16 * 512].rearrange("p (dt c) -> p dt c", c=512),
                        in_=attn_d.rearrange("(dt p) t -> p dt t", p=128)[:, half * 16:(half + 1) * 16, qc * 512:(qc + 1) * 512]),
                       inc=("dS", 16), waits=w if half == 0 else [])
                lda = CNT["dS"]
                for ot in range(32):
                    k = qc * 32 + ot
                    wbuf = s_wo4[k % 2]
                    for half in range(2):
                        do("gpsimd", lambda e, ot=ot, wbuf=wbuf, half=half: e.dma_start(
                            out=wbuf[:, half * 2048:(half + 1) * 2048].rearrange("p (dt c) -> p dt c", c=128),
                            in_=woT_d.rearrange("(dt p) t -> p dt t", p=128)[:, half * 16:(half + 1) * 16, ot * 128:(ot + 1) * 128]),
                           inc=("dG", 16),
                           waits=[("cT", CNT["cT"] - 44)] if (k >= 2 and half == 0) else [])
                    ldw = CNT["dG"]
                    xr = s_xr[k % 2]
                    do("sync", lambda e, qc=qc, ot=ot, xr=xr: e.dma_start(
                        out=xr[:].rearrange("p (g c) -> p g c", c=128),
                        in_=x_own.rearrange("(g p) d -> p g d", p=128)[:, qc * 4:(qc + 1) * 4, ot * 128:(ot + 1) * 128]),
                       inc=("dS", 16), waits=[("cT", CNT["cT"] - 40)] if k >= 2 else [])
                    ldx = CNT["dS"]
                    par = k % 2
                    for dt in range(32):
                        do("tensor", lambda e, par=par, wbuf=wbuf, abuf=abuf, dt=dt: e.matmul(
                            P[par][:], wbuf[:, dt * 128:(dt + 1) * 128], abuf[:, dt * 512:(dt + 1) * 512],
                            start=(dt == 0), stop=(dt == 31)),
                           inc=("cT", 1),
                           waits=[("dS", lda), ("dG", ldw), ("cV", CNT["cV"])] if dt == 0 else [])
                    mmv = CNT["cT"]
                    for i in range(4):
                        do("tensor", lambda e, par=par, xr=xr, i=i: e.transpose(
                            out=P[4 + par][:, i * 128:(i + 1) * 128], in_=xr[:, i * 128:(i + 1) * 128],
                            identity=s_iden[:]),
                           inc=("cT", 1), waits=[("dS", ldx), ("cV", CNT["cV"])] if i == 0 else [])
                    do("scalar", lambda e, par=par, ot=ot: e.activation(
                        out=s_tmp4[:], in_=P[par][:], func=AF.Identity, bias=s_wob[:, ot:ot + 1]),
                       inc=("cA", 1), waits=[("cT", mmv), ("cV", CNT["cV"])])
                    r2 = s_r2[k % 2]
                    do("vector", lambda e, par=par, r2=r2: e.tensor_tensor(
                        out=r2[:], in0=s_tmp4[:], in1=P[4 + par][:], op=ALU.add),
                       inc=("cV", 1), waits=[("cA", CNT["cA"]), ("cT", CNT["cT"]), ("dS", CNT["dS"])])
                    do("sync", lambda e, r2=r2, ot=ot, qc=qc: e.dma_start(
                        out=r2_d[ot * 128:(ot + 1) * 128, qc * 512:(qc + 1) * 512], in_=r2[:]),
                       inc=("dS", 16), waits=[("cV", CNT["cV"])])
                    r2b = s_r2b[k % 2]
                    do("vector", lambda e, r2=r2, r2b=r2b: e.tensor_copy(r2b[:], r2[:]),
                       inc=("cV", 1), waits=[("cT", CNT["cT"])])
                    sq = s_sq4[k % 2]
                    do("scalar", lambda e, r2b=r2b, sq=sq: e.activation(out=sq[:], in_=r2b[:], func=AF.Square),
                       inc=("cA", 1), waits=[("cV", CNT["cV"]), ("cT", CNT["cT"])])
                    do("tensor", lambda e, r2b=r2b: e.matmul(
                        P[6][0:1, :], s_ones[:], r2b[:], start=True, stop=True, skip_group_check=True),
                       inc=("cT", 1), waits=[("cV", CNT["cV"])])
                    do("tensor", lambda e, sq=sq: e.matmul(
                        P[7][0:1, :], s_ones[:], sq[:], start=True, stop=True, skip_group_check=True),
                       inc=("cT", 1), waits=[("cA", CNT["cA"])])
                    do("vector", lambda e, qc=qc: e.tensor_tensor(
                        out=s_accS[0:1, qc * 512:(qc + 1) * 512], in0=s_accS[0:1, qc * 512:(qc + 1) * 512],
                        in1=P[6][0:1, :], op=ALU.add),
                       inc=("cV", 1), waits=[("cT", CNT["cT"])])
                    do("vector", lambda e, qc=qc: e.tensor_tensor(
                        out=s_accQ[0:1, qc * 512:(qc + 1) * 512], in0=s_accQ[0:1, qc * 512:(qc + 1) * 512],
                        in1=P[7][0:1, :], op=ALU.add),
                       inc=("cV", 1))
                    DQ.pump(14)
                qc_done[qc] = CNT["cT"]
            do("vector", lambda e: e.tensor_scalar(
                out=s_ln2m[:], in0=s_accS[:], scalar1=1.0 / D, scalar2=None, op0=ALU.mult),
               inc=("cV", 1))
            do("vector", lambda e: e.tensor_tensor(
                out=s_ln2v[:], in0=s_ln2m[:], in1=s_ln2m[:], op=ALU.mult),
               inc=("cV", 1))
            do("vector", lambda e: e.scalar_tensor_tensor(
                out=s_ln2v[:], in0=s_accQ[:], scalar=1.0 / D, in1=s_ln2v[:],
                op0=ALU.mult, op1=ALU.subtract),
               inc=("cV", 1))
            do("scalar", lambda e: e.activation(
                out=s_ln2v[:], in_=s_ln2v[:], func=AF.Sqrt, bias=s_eps[0:1, 0:1]),
               inc=("cA", 1), waits=[("cV", CNT["cV"])])
            do("vector", lambda e: e.reciprocal(s_ln2v[:], s_ln2v[:]),
               inc=("cV", 1), waits=[("cA", CNT["cA"])])
            do("vector", lambda e: e.tensor_tensor(
                out=s_ln2m[:], in0=s_ln2m[:], in1=s_ln2v[:], op=ALU.mult),
               inc=("cV", 1))
            do("sync", lambda e: e.dma_start(out=mrow_d[3, 0:TO].unsqueeze(0), in_=s_ln2v[:]),
               inc=("dS", 16), waits=[("cV", CNT["cV"])])
            do("sync", lambda e: e.dma_start(out=mrow_d[4, 0:TO].unsqueeze(0), in_=s_ln2m[:]),
               inc=("dS", 16))

        run_phase(p4)
        barrier()
        p4_stack.close()

        p5b_stack = ExitStack()
        e5b = p5b_stack.enter_context
        s_xn2 = e5b(nc.sbuf_tensor("s_xn2", [128, 32 * 2048], BF16))
        s_r2t = [e5b(nc.sbuf_tensor("s_r2t0", [128, 512], F32))] * 2
        s_w12 = [e5b(nc.sbuf_tensor(f"s_w12_{i}", [128, D], BF16)) for i in range(3)]
        s_sg = e5b(nc.sbuf_tensor("s_sg", [128, 512], F32))
        s_ht = [e5b(nc.sbuf_tensor("s_ht0", [128, 2048], BF16))] * 2

        def p5b():
            s_i2b = s_w12[0].bitcast(F32)[:, 0:512]
            s_m2b = s_w12[0].bitcast(F32)[:, 512:1024]
            for tc in range(4):
                do("sync", lambda e, tc=tc: e.dma_start(
                    out=s_i2b, in_=mrow_d[3:4, tc * 512:(tc + 1) * 512].broadcast_to((128, 512))),
                   inc=("dS", 16), waits=[("cV", CNT["cV"])])
                do("sync", lambda e, tc=tc: e.dma_start(
                    out=s_m2b, in_=mrow_d[4:5, tc * 512:(tc + 1) * 512].broadcast_to((128, 512))),
                   inc=("dS", 16))
                ldb = CNT["dS"]
                for dt in range(32):
                    rb = s_r2t[dt % 2]
                    do("gpsimd", lambda e, rb=rb, dt=dt, tc=tc: e.dma_start(
                        out=rb[:], in_=r2_d[dt * 128:(dt + 1) * 128, tc * 512:(tc + 1) * 512]),
                       inc=("dG", 16), waits=[("cV", CNT["cV"] - 1)] if (tc * 32 + dt) >= 1 else [])
                    do("vector", lambda e, rb=rb: e.tensor_tensor(
                        out=s_sg[:], in0=rb[:], in1=s_i2b, op=ALU.mult),
                       inc=("cV", 1), waits=[("dG", CNT["dG"]), ("dS", ldb)])
                    do("vector", lambda e, dt=dt, tc=tc: e.tensor_tensor(
                        out=s_xn2[:, dt * 2048 + tc * 512: dt * 2048 + (tc + 1) * 512],
                        in0=s_sg[:], in1=s_m2b, op=ALU.subtract),
                       inc=("cV", 1))
            xnv = CNT["cV"]
            hc_done = {}
            for hc in range(NHC):
                w1b = s_w12[hc % 2]
                w2b = s_w12[2]
                wwar = ([("cT", hc_done[hc - 2])] if hc >= 2 else [("cV", xnv)]) + [("qD", DQ.ensure12(hc // 4))]
                wwar2 = ([("cT", hc_done[hc - 1])] if hc >= 1 else [("cV", xnv)]) + [("qD", DQ.ensure12(hc // 4))]
                for half in range(2):
                    do("sync", lambda e, w1b=w1b, hc=hc, half=half: e.dma_start(
                        out=w1b[:, half * 2048:(half + 1) * 2048].rearrange("p (dt c) -> p dt c", c=128),
                        in_=w1t_d.rearrange("(dt p) t -> p dt t", p=128)[:, half * 16:(half + 1) * 16, hc * 128:(hc + 1) * 128]),
                       inc=("dS", 16), waits=wwar if half == 0 else [])
                    do("gpsimd", lambda e, w2b=w2b, hc=hc, half=half: e.dma_start(
                        out=w2b[:, half * 2048:(half + 1) * 2048].rearrange("p (dt c) -> p dt c", c=128),
                        in_=w2t_d.rearrange("(dt p) t -> p dt t", p=128)[:, half * 16:(half + 1) * 16, hc * 128:(hc + 1) * 128]),
                       inc=("dG", 16), waits=wwar2 if half == 0 else [])
                ldw1 = CNT["dS"]
                ldw2 = CNT["dG"]
                ht = s_ht[hc % 2]
                for tc in range(4):
                    p1 = P[tc % 2]
                    p2 = P[2 + tc % 2]
                    for dt in range(32):
                        do("tensor", lambda e, p1=p1, w1b=w1b, dt=dt, tc=tc: e.matmul(
                            p1[:], w1b[:, dt * 128:(dt + 1) * 128],
                            s_xn2[:, dt * 2048 + tc * 512: dt * 2048 + (tc + 1) * 512],
                            start=(dt == 0), stop=(dt == 31)),
                           inc=("cT", 1),
                           waits=[("cV", xnv), ("dS", ldw1), ("cA", CNT["cA"])] if dt == 0 else [])
                        if dt % 8 == 7:
                            DQ.pump(3)
                    do("scalar", lambda e, p1=p1: e.activation(out=s_sg[:], in_=p1[:], func=AF.Silu),
                       inc=("cA", 1), waits=[("cT", CNT["cT"]), ("cV", CNT["cV"])])
                    for dt in range(32):
                        do("tensor", lambda e, p2=p2, w2b=w2b, dt=dt, tc=tc: e.matmul(
                            p2[:], w2b[:, dt * 128:(dt + 1) * 128],
                            s_xn2[:, dt * 2048 + tc * 512: dt * 2048 + (tc + 1) * 512],
                            start=(dt == 0), stop=(dt == 31)),
                           inc=("cT", 1),
                           waits=[("dG", ldw2), ("cV", CNT["cV"])] if dt == 0 else [])
                        if dt % 8 == 7:
                            DQ.pump(3)
                    do("vector", lambda e, p2=p2, ht=ht, tc=tc: e.tensor_tensor(
                        out=ht[:, tc * 512:(tc + 1) * 512], in0=p2[:], in1=s_sg[:], op=ALU.mult),
                       inc=("cV", 1), waits=[("cT", CNT["cT"]), ("cA", CNT["cA"]), ("dS", CNT["dS"])])

                do("sync", lambda e, ht=ht, hc=hc: e.dma_start(
                    out=h_d[hc * 128:(hc + 1) * 128, :], in_=ht[:]),
                   inc=("dS", 16), waits=[("cV", CNT["cV"])])
                hc_done[hc] = CNT["cT"]

        run_phase(p5b)
        barrier()
        p5b_stack.close()

        p6_stack = ExitStack()
        e6 = p6_stack.enter_context
        s_H = e6(nc.sbuf_tensor("s_H", [128, NHC * 512], BF16))
        s_w3 = [e6(nc.sbuf_tensor(f"s_w3_{i}", [128, 8 * 256], BF16)) for i in range(3)]
        s_r26 = [e6(nc.sbuf_tensor(f"s_r26_{i}", [128, 512], F32)) for i in range(2)]
        s_of = [e6(nc.sbuf_tensor(f"s_of{i}", [128, 512], F32)) for i in range(2)]
        s_ot4 = [e6(nc.sbuf_tensor(f"s_ot4_{i}", [128, 512], F32)) for i in range(2)]

        def p6():
            NG = (NHC + 7) // 8
            g_done = {}
            tq_done = {}
            for tq in range(4):
                hw = [("cT", tq_done[tq - 1])] if tq >= 1 else []
                for g in range(NG):
                    n = min(8, NHC - g * 8)
                    do("sync", lambda e, g=g, n=n, tq=tq: e.dma_start(
                        out=s_H[:, g * 8 * 512:(g * 8 + n) * 512].rearrange("p (j c) -> p j c", c=512),
                        in_=h_d.rearrange("(j p) t -> p j t", p=128)[:, g * 8:g * 8 + n, tq * 512:(tq + 1) * 512]),
                       inc=("dS", 16), waits=hw if g == 0 else [])
                ldh = CNT["dS"]
                for og in range(16):
                    u = tq * 16 + og
                    for g in range(NG):
                        n = min(8, NHC - g * 8)
                        gi = u * NG + g
                        wb = s_w3[gi % 3]
                        do("scalar", lambda e, wb=wb, g=g, n=n, og=og: e.dma_start(
                            out=wb[:, 0:n * 256].rearrange("p (j c) -> p j c", c=256),
                            in_=w3t_d.rearrange("(j p) t -> p j t", p=128)[:, g * 8:g * 8 + n, og * 256:(og + 1) * 256]),
                           inc=("dA", 16),
                           waits=([("cT", g_done[gi - 3])] if gi >= 3 else []) + [("qD", DQ.ensure3(g * 8 + n - 1))])
                        gld = CNT["dA"]
                        for j in range(n):
                            hc = g * 8 + j
                            for sub in range(2):
                                do("tensor", lambda e, wb=wb, j=j, sub=sub, hc=hc: e.matmul(
                                    P[sub][:], wb[:, j * 256 + sub * 128: j * 256 + (sub + 1) * 128],
                                    s_H[:, hc * 512:(hc + 1) * 512],
                                    start=(hc == 0), stop=(hc == NHC - 1), skip_group_check=True),
                                   inc=("cT", 1),
                                   waits=([("dS", ldh), ("dA", gld), ("cV", CNT["cV"])] if hc == 0 else
                                          ([("dA", gld)] if j == 0 else [])))
                        g_done[gi] = CNT["cT"]
                        DQ.pump(12)
                    mmv = CNT["cT"]
                    for sub in range(2):
                        ot = og * 2 + sub
                        r26 = s_r26[sub]
                        do("sync", lambda e, r26=r26, ot=ot, tq=tq: e.dma_start(
                            out=r26[:], in_=r2_d[ot * 128:(ot + 1) * 128, tq * 512:(tq + 1) * 512]),
                           inc=("dS", 16), waits=[("cV", CNT["cV"])] if u >= 1 else [])
                        of = s_of[sub]
                        do("vector", lambda e, sub=sub, r26=r26, of=of: e.tensor_tensor(
                            out=of[:], in0=P[sub][:], in1=r26[:], op=ALU.add),
                           inc=("cV", 1), waits=[("cT", mmv), ("dS", CNT["dS"])])
                        ofv = CNT["cV"]
                        for tt in range(4):
                            do("tensor", lambda e, of=of, tt=tt, sub=sub: e.transpose(
                                out=P[4 + sub][:, tt * 128:(tt + 1) * 128], in_=of[:, tt * 128:(tt + 1) * 128],
                                identity=s_iden[:]),
                               inc=("cT", 1), waits=[("cV", ofv)] if tt == 0 else [])
                        trv = CNT["cT"]
                        o4 = s_ot4[sub]
                        do("vector", lambda e, o4=o4, sub=sub: e.tensor_copy(
                            o4[:], P[4 + sub][:]),
                           inc=("cV", 1), waits=[("cT", trv), ("dS", CNT["dS"])])
                        do("sync", lambda e, o4=o4, ot=ot, tq=tq: e.dma_start(
                            out=y_own.rearrange("(g p) d -> p g d", p=128)[:, tq * 4:(tq + 1) * 4, ot * 128:(ot + 1) * 128],
                            in_=o4[:].rearrange("p (g c) -> p g c", c=128)),
                           inc=("dS", 16), waits=[("cV", CNT["cV"])])
                tq_done[tq] = CNT["cT"]
            DQ.drain_all()

        run_phase(p6)
        barrier()
        p6_stack.close()

    return nc


_NC_CACHE = None


def kernel(**inputs):
    global _NC_CACHE
    B = 4
    x = np.asarray(inputs["L0_attn_x"], dtype=np.float32)
    wq_w = np.asarray(inputs["L0_attn_wq_w"], dtype=np.float32)
    wq_b = np.asarray(inputs["L0_attn_wq_b"], dtype=np.float32)
    wk_w = np.asarray(inputs["L0_attn_wk_w"], dtype=np.float32)
    wk_b = np.asarray(inputs["L0_attn_wk_b"], dtype=np.float32)
    wv_w = np.asarray(inputs["L0_attn_wv_w"], dtype=np.float32)
    wv_b = np.asarray(inputs["L0_attn_wv_b"], dtype=np.float32)
    wo_w = np.asarray(inputs["L0_attn_wo_w"], dtype=np.float32)
    wo_b = np.asarray(inputs["L0_attn_wo_b"], dtype=np.float32)

    def codes_u8_T(a):
        t = np.ascontiguousarray(np.asarray(a).T)
        return np.ascontiguousarray(t.view(np.uint8)[:, 0::4])

    w1qT = codes_u8_T(inputs["L0_ffn_w1_q"])
    w2qT = codes_u8_T(inputs["L0_ffn_w2_q"])
    w3qT = codes_u8_T(inputs["L0_ffn_w3_q"])
    sc1T = np.ascontiguousarray(np.asarray(inputs["L0_ffn_w1_scale"], dtype=np.float32).T)
    sc2T = np.ascontiguousarray(np.asarray(inputs["L0_ffn_w2_scale"], dtype=np.float32).T)
    sc3T = np.ascontiguousarray(np.asarray(inputs["L0_ffn_w3_scale"], dtype=np.float32).T)

    tri = np.tril(np.ones((128, 128), dtype=np.float32)).T.astype(ml_dtypes.bfloat16)
    iden = np.eye(128, dtype=np.float32)
    ones128 = np.ones((128, 128), dtype=ml_dtypes.bfloat16)
    zeros128 = np.zeros((128, 128), dtype=ml_dtypes.bfloat16)

    if _NC_CACHE is None:
        _NC_CACHE = build_nc()
    nc = _NC_CACHE

    in_maps = []
    for c in range(8):
        b, h = c // 2, c % 2
        r4, r2 = c % 4, c % 2
        xt = x[b].reshape(32, 128, D)
        if h == 0:
            perm = np.arange(32)
        else:
            perm = np.arange(32).reshape(16, 2)[:, ::-1].reshape(32)
        x_perm = np.ascontiguousarray(xt[perm].reshape(T, D))
        x_own_a = np.ascontiguousarray(xt[perm[0::2]].reshape(TO, D))
        in_maps.append({
            "x_all": x_perm, "x_own": x_own_a,
            "wq": wq_w, "wo": wo_w, "wk": wk_w, "wv": wv_w,
            "wq_b": wq_b, "wo_b": wo_b, "wk_b": wk_b, "wv_b": wv_b,
            "w1qT": w1qT, "w2qT": w2qT, "w3qT": w3qT,
            "sc1T": sc1T, "sc2T": sc2T, "sc3T": sc3T,
            "tri_in": tri, "mB_in": zeros128 if h == 0 else ones128,
            "iden_in": iden,
        })

    res = run_bass_kernel_spmd(nc, in_maps, core_ids=list(range(8)))

    out = np.empty((B, T, D), dtype=np.float32)
    for c in range(8):
        b, h = c // 2, c % 2
        yo = np.asarray(res.results[c]["y_own"], dtype=np.float32).reshape(16, 128, D)
        ob = out[b].reshape(32, 128, D)
        for k in range(16):
            ob[2 * k + h] = yo[k]
    return out


# version 15
# speedup vs baseline: 1.0652x; 1.0089x over previous
import math
import numpy as np
import ml_dtypes
from contextlib import ExitStack

import concourse.bass as bass
import concourse.mybir as mybir
from concourse.bass_utils import run_bass_kernel_spmd

F32 = mybir.dt.float32
BF16 = mybir.dt.bfloat16
U8 = mybir.dt.uint8
AF = mybir.ActivationFunctionType
ALU = mybir.AluOpType

D = 4096
T = 4096
TO = 2048
HID = 11008
NHC = HID // 128
EPS = 1e-5
INV_SQRT_HD = 1.0 / math.sqrt(128.0)

W12_ROWS = D
W3_ROWS = HID
FD_Q = 512

NF4 = [-1.0, -0.6961928009986877, -0.5250730514526367, -0.39491748809814453,
       -0.28444138169288635, -0.18477343022823334, -0.09105003625154495, 0.0,
       0.07958029955625534, 0.16093020141124725, 0.24611230194568634,
       0.33791524171829224, 0.44070982933044434, 0.5626170039176941,
       0.7229568362236023, 1.0]

ENGS = ["sync", "gpsimd", "scalar", "vector", "tensor"]


def build_nc():
    nc = bass.Bass("TRN2", target_bir_lowering=False, num_devices=8)

    x_all = nc.dram_tensor("x_all", [T, D], F32, kind="ExternalInput").ap()
    x_own = nc.dram_tensor("x_own", [TO, D], F32, kind="ExternalInput").ap()
    wq = nc.dram_tensor("wq", [D, D], F32, kind="ExternalInput").ap()
    wo = nc.dram_tensor("wo", [D, D], F32, kind="ExternalInput").ap()
    wk = nc.dram_tensor("wk", [128, D], F32, kind="ExternalInput").ap()
    wv = nc.dram_tensor("wv", [128, D], F32, kind="ExternalInput").ap()
    wq_b = nc.dram_tensor("wq_b", [D], F32, kind="ExternalInput").ap()
    wo_b = nc.dram_tensor("wo_b", [D], F32, kind="ExternalInput").ap()
    wk_b = nc.dram_tensor("wk_b", [128], F32, kind="ExternalInput").ap()
    wv_b = nc.dram_tensor("wv_b", [128], F32, kind="ExternalInput").ap()
    w1qT = nc.dram_tensor("w1qT", [W12_ROWS, HID], U8, kind="ExternalInput").ap()
    w2qT = nc.dram_tensor("w2qT", [W12_ROWS, HID], U8, kind="ExternalInput").ap()
    w3qT = nc.dram_tensor("w3qT", [W3_ROWS, D], U8, kind="ExternalInput").ap()
    sc1T = nc.dram_tensor("sc1T", [64, HID], F32, kind="ExternalInput").ap()
    sc2T = nc.dram_tensor("sc2T", [64, HID], F32, kind="ExternalInput").ap()
    sc3T = nc.dram_tensor("sc3T", [172, D], F32, kind="ExternalInput").ap()
    tri_in = nc.dram_tensor("tri_in", [128, 128], BF16, kind="ExternalInput").ap()
    mB_in = nc.dram_tensor("mB_in", [128, 128], BF16, kind="ExternalInput").ap()
    iden_in = nc.dram_tensor("iden_in", [128, 128], F32, kind="ExternalInput").ap()

    y_own = nc.dram_tensor("y_own", [TO, D], F32, kind="ExternalOutput").ap()
    wqT_d = nc.dram_tensor("wqT_d", [D, D], BF16, kind="Internal").ap()
    woT_d = nc.dram_tensor("woT_d", [D, D], BF16, kind="Internal").ap()
    xn1_d = nc.dram_tensor("xn1_d", [D, TO], BF16, kind="Internal").ap()
    q_d = nc.dram_tensor("q_d", [D, TO], BF16, kind="Internal").ap()
    attn_d = nc.dram_tensor("attn_d", [D, TO], BF16, kind="Internal").ap()
    r2_d = nc.dram_tensor("r2_d", [D, TO], F32, kind="Internal").ap()
    w1t_d = nc.dram_tensor("w1t_d", [D, HID], BF16, kind="Internal").ap()
    w2t_d = nc.dram_tensor("w2t_d", [D, HID], BF16, kind="Internal").ap()
    w3t_d = nc.dram_tensor("w3t_d", [HID, D], BF16, kind="Internal").ap()
    h_d = nc.dram_tensor("h_d", [HID, TO], BF16, kind="Internal").ap()
    mrow_d = nc.dram_tensor("mrow_d", [8, T], F32, kind="Internal").ap()

    with ExitStack() as top:
        ent = top.enter_context

        P = [ent(nc.psum_tensor(f"pb{i}", [128, 512], F32)) for i in range(8)]

        sem_names = ["dS", "dG", "dA", "cT", "cA", "cV", "cG", "qD", "qV", "qA", "qC"]
        SEMS = {n: ent(nc.semaphore(n)) for n in sem_names}
        CNT = {n: 0 for n in sem_names}
        ROLE = [None]

        def eng(name):
            return getattr(nc, name)

        def do(engname, f, inc=None, waits=()):
            if ROLE[0] == engname:
                e = eng(engname)
                for s, v in waits:
                    if v > 0:
                        e.wait_ge(SEMS[s], v)
                r = f(e)
                if inc is not None:
                    r.then_inc(SEMS[inc[0]], inc[1])
            if inc is not None:
                CNT[inc[0]] += inc[1]

        block = ent(nc.Block())


        def mk_bufset(e, tag, fd, nb):
            R = dict(side="right")
            return {
                "cd": [e(nc.sbuf_tensor(f"dq_cd{tag}{i}", [128, fd], U8, **R)) for i in range(nb)],
                "scb": [e(nc.sbuf_tensor(f"dq_sc{tag}{i}", [128, fd], F32, **R)) for i in range(nb)],
                "mk": [e(nc.sbuf_tensor(f"dq_mk{tag}{i}", [128, fd], U8, **R)) for i in range(12)],
                "lv": [e(nc.sbuf_tensor(f"dq_lv{tag}{i}", [128, 8 * fd], BF16, **R)) for i in range(3)],
                "wd": [e(nc.sbuf_tensor(f"dq_wd{tag}{i}", [128, fd], BF16, **R)) for i in range(2)],
                "fd": fd, "nb": nb,
            }
        bufsetA = mk_bufset(ent, "A", FD_Q, 3)
        BUFSETS = {"A": bufsetA}

        class Dequant:
            def __init__(self):
                self.units = []
                self.mark_after = {}
                nch12 = (HID + FD_Q - 1) // FD_Q
                for c in range(nch12):
                    fd = min(FD_Q, HID - c * FD_Q)
                    for w, (qT, scT, dst) in enumerate([(w1qT, sc1T, w1t_d),
                                                        (w2qT, sc2T, w2t_d)]):
                        for rt in range(32):
                            self.units.append((qT, scT, dst, rt, c * FD_Q, fd))
                    self.mark_after[len(self.units) - 1] = ("m12", c)
                nch3 = (D + FD_Q - 1) // FD_Q
                for rt in range(W3_ROWS // 128):
                    for c in range(nch3):
                        fd = min(FD_Q, D - c * FD_Q)
                        self.units.append((w3qT, sc3T, w3t_d, rt, c * FD_Q, fd))
                    self.mark_after[len(self.units) - 1] = ("m3", rt)
                nu = len(self.units)
                gl = [("ld", 0), ("ld", 1)]
                for k in range(nu):
                    gl.append(("m01", k))
                    gl.append(("m23", k))
                    if k + 2 < nu:
                        gl.append(("ld", k + 2))
                    for j in range(4):
                        gl.append(("leaf", (k, j)))
                    gl.append(("cp12", k))
                    gl.append(("cp34", k))
                    gl.append(("cp56", k))
                    gl.append(("cp7", k))
                    gl.append(("sc", k))
                    gl.append(("st", k))
                self.glist = gl
                self.i = 0
                self.m12 = {}
                self.m3 = {}

            def emit_group(self, g):
                kind, pay = g
                k = pay[0] if isinstance(pay, tuple) else pay
                qT, scT, dst, rt, c0, fd = self.units[k]
                bs = BUFSETS["A"]
                nb = bs["nb"]
                cd = bs["cd"][k % nb]
                scb = bs["scb"][k % nb]
                mk = bs["mk"][(k % 3) * 4:(k % 3) * 4 + 4]
                lv = bs["lv"][k % 3]
                lfd = bs["fd"]
                wd = bs["wd"][k % 2]
                if kind == "ld":
                    wv = []
                    if k >= nb:
                        wv = [("qV", 11 * (k - nb) + 4), ("cG", k - nb + 1)]
                    do("scalar", lambda e: e.dma_start(
                        out=cd[:, 0:fd], in_=qT[rt * 128:(rt + 1) * 128, c0:c0 + fd]),
                       inc=("qD", 16), waits=wv)
                    do("scalar", lambda e: e.dma_start(
                        out=scb[0:64, 0:fd], in_=scT[2 * rt:2 * rt + 1, c0:c0 + fd].broadcast_to((64, fd))),
                       inc=("qD", 16))
                    do("scalar", lambda e: e.dma_start(
                        out=scb[64:128, 0:fd], in_=scT[2 * rt + 1:2 * rt + 2, c0:c0 + fd].broadcast_to((64, fd))),
                       inc=("qD", 16))
                elif kind == "m01":
                    do("vector", lambda e: e.tensor_scalar(
                        out=mk[0][:, 0:fd], in0=cd[:, 0:fd], scalar1=1, scalar2=None, op0=ALU.bitwise_and),
                       inc=("qV", 1), waits=[("qD", 64 * k + 48), ("qA", 8 * (k - 2)), ("qV", 11 * (k - 2))])
                    do("vector", lambda e: e.tensor_scalar(
                        out=mk[1][:, 0:fd], in0=cd[:, 0:fd], scalar1=2, scalar2=None, op0=ALU.bitwise_and),
                       inc=("qV", 1))
                elif kind == "m23":
                    do("vector", lambda e: e.tensor_scalar(
                        out=mk[2][:, 0:fd], in0=cd[:, 0:fd], scalar1=4, scalar2=None, op0=ALU.bitwise_and),
                       inc=("qV", 1))
                    do("vector", lambda e: e.tensor_scalar(
                        out=mk[3][:, 0:fd], in0=cd[:, 0:fd], scalar1=8, scalar2=None, op0=ALU.bitwise_and),
                       inc=("qV", 1))
                elif kind == "leaf":
                    j2 = pay[1]
                    for j in (2 * j2, 2 * j2 + 1):
                        dd = float(NF4[2 * j + 1] - NF4[2 * j])
                        bb = float(NF4[2 * j])
                        do("scalar", lambda e, j=j, dd=dd, bb=bb: e.activation(
                            out=lv[:, j * lfd: j * lfd + fd], in_=mk[0][:, 0:fd],
                            func=AF.Copy, bias=bb, scale=dd),
                           inc=("qA", 1),
                           waits=[("qV", 11 * k + 4), ("cG", k - 2)] if (j2 == 0 and j % 2 == 0) else [])
                elif kind in ("cp12", "cp34"):
                    ii = (0, 1) if kind == "cp12" else (2, 3)
                    for i in ii:
                        do("vector", lambda e, i=i: e.copy_predicated(
                            out=lv[:, (2 * i) * lfd:(2 * i) * lfd + fd], mask=mk[1][:, 0:fd],
                            data=lv[:, (2 * i + 1) * lfd:(2 * i + 1) * lfd + fd]),
                           inc=("qV", 1), waits=[("qA", 8 * k + 8)] if i == 0 else [])
                elif kind == "cp56":
                    for i in range(2):
                        do("vector", lambda e, i=i: e.copy_predicated(
                            out=lv[:, (4 * i) * lfd:(4 * i) * lfd + fd], mask=mk[2][:, 0:fd],
                            data=lv[:, (4 * i + 2) * lfd:(4 * i + 2) * lfd + fd]),
                           inc=("qV", 1))
                elif kind == "cp7":
                    do("vector", lambda e: e.copy_predicated(
                        out=lv[:, 0:fd], mask=mk[3][:, 0:fd], data=lv[:, 4 * lfd:4 * lfd + fd]),
                       inc=("qV", 1))
                elif kind == "sc":
                    wv = [("qV", 11 * k + 11)]
                    if k >= 2:
                        wv.append(("qD", 64 * (k - 2) + 64))
                    do("gpsimd", lambda e: e.tensor_tensor(
                        out=wd[:, 0:fd], in0=lv[:, 0:fd], in1=scb[:, 0:fd], op=ALU.mult),
                       inc=("cG", 1), waits=wv)
                elif kind == "st":
                    do("scalar", lambda e: e.dma_start(
                        out=dst[rt * 128:(rt + 1) * 128, c0:c0 + fd], in_=wd[:, 0:fd]),
                       inc=("qD", 16), waits=[("cG", k + 1)])
                    if kind == "st" and k in self.mark_after:
                        mk2, idx = self.mark_after[k]
                        if mk2 == "m12":
                            self.m12[idx] = CNT["qD"]
                        else:
                            self.m3[idx] = CNT["qD"]

            def pump(self, n):
                while n > 0 and self.i < len(self.glist):
                    self.emit_group(self.glist[self.i])
                    self.i += 1
                    n -= 1

            def ensure12(self, c):
                while c not in self.m12:
                    self.pump(1)
                return self.m12[c]

            def ensure3(self, rt):
                while rt not in self.m3:
                    self.pump(1)
                return self.m3[rt]

            def drain_all(self):
                while self.i < len(self.glist):
                    self.emit_group(self.glist[self.i])
                    self.i += 1

        DQ = Dequant()

        def run_phase(fn):
            start = dict(CNT)
            dqs = (DQ.i, dict(DQ.m12), dict(DQ.m3))
            final = None
            for role in ENGS:
                CNT.clear()
                CNT.update(start)
                DQ.i, m12s, m3s = dqs[0], dict(dqs[1]), dict(dqs[2])
                DQ.m12, DQ.m3 = m12s, m3s
                ROLE[0] = role
                sec = getattr(block, role)

                def _body(e, fn=fn):
                    fn()
                sec(_body)
                if final is None:
                    final = (dict(CNT), DQ.i, dict(DQ.m12), dict(DQ.m3))
                else:
                    assert final == (CNT, DQ.i, DQ.m12, DQ.m3), "nondeterministic phase emission"
            ROLE[0] = None

        def barrier():
            vals = dict(CNT)

            def _b():
                e = eng(ROLE[0])
                for s, v in vals.items():
                    if v > 0:
                        e.wait_ge(SEMS[s], v)
            run_phase(_b)

        s_tri = ent(nc.sbuf_tensor("s_tri", [128, 128], BF16))
        s_mB = ent(nc.sbuf_tensor("s_mB", [128, 128], BF16))
        s_iden = ent(nc.sbuf_tensor("s_iden", [128, 128], F32))
        s_ones = ent(nc.sbuf_tensor("s_ones", [128, 1], BF16))
        s_onef = ent(nc.sbuf_tensor("s_onef", [1, 128], F32))
        s_wqb = ent(nc.sbuf_tensor("s_wqb", [128, 32], F32))
        s_wob = ent(nc.sbuf_tensor("s_wob", [128, 32], F32))
        s_wkb = ent(nc.sbuf_tensor("s_wkb", [128, 1], F32))
        s_wvb = ent(nc.sbuf_tensor("s_wvb", [128, 1], F32))
        s_eps = ent(nc.sbuf_tensor("s_eps", [128, 1], F32))

        p0_stack = ExitStack()
        kv_stack = ExitStack()
        s_wkT = kv_stack.enter_context(nc.sbuf_tensor("s_wkT", [128, D], BF16))
        s_wvT = kv_stack.enter_context(nc.sbuf_tensor("s_wvT", [128, D], BF16))
        s_KF = kv_stack.enter_context(nc.sbuf_tensor("s_KF", [128, T], BF16))
        s_VT = kv_stack.enter_context(nc.sbuf_tensor("s_VT", [128, T], BF16))
        e0 = p0_stack.enter_context
        s_wfl = e0(nc.sbuf_tensor("s_wfl", [128, 2 * D], F32))
        s_wtmp = [e0(nc.sbuf_tensor(f"s_wtmp{i}", [128, D], F32)) for i in range(2)]
        s_wt4 = [e0(nc.sbuf_tensor(f"s_wt4_{i}", [128, 512], BF16)) for i in range(3)]
        s_bt = e0(nc.sbuf_tensor("s_bt", [32, 128], F32))

        def p0():
            do("sync", lambda e: e.dma_start(out=s_tri[:], in_=tri_in), inc=("dS", 16))
            do("sync", lambda e: e.dma_start(out=s_mB[:], in_=mB_in), inc=("dS", 16))
            do("sync", lambda e: e.dma_start(out=s_iden[:], in_=iden_in), inc=("dS", 16))
            do("sync", lambda e: e.dma_start(out=s_wkb[:], in_=wk_b.unsqueeze(1)), inc=("dS", 16))
            do("sync", lambda e: e.dma_start(out=s_wvb[:], in_=wv_b.unsqueeze(1)), inc=("dS", 16))
            do("sync", lambda e: e.dma_start(out=s_wfl[:, 0:D], in_=wk), inc=("dS", 16))
            do("sync", lambda e: e.dma_start(out=s_wfl[:, D:2 * D], in_=wv), inc=("dS", 16))
            do("sync", lambda e: e.dma_start(out=s_bt[:], in_=wq_b.rearrange("(t p) -> t p", p=128)), inc=("dS", 16))
            do("vector", lambda e: e.memset(s_ones[:], 1.0), inc=("cV", 1))
            do("vector", lambda e: e.memset(s_onef[:], 1.0), inc=("cV", 1))
            do("vector", lambda e: e.memset(s_eps[:], EPS), inc=("cV", 1))
            ld8 = CNT["dS"]
            do("tensor", lambda e: e.transpose(out=P[0][:, 0:32], in_=s_bt[:], identity=s_iden[0:32, 0:32]),
               inc=("cT", 1), waits=[("dS", ld8)])
            do("vector", lambda e: e.tensor_copy(s_wqb[:], P[0][:, 0:32]), inc=("cV", 1),
               waits=[("cT", CNT["cT"])])
            do("sync", lambda e: e.dma_start(out=s_bt[:], in_=wo_b.rearrange("(t p) -> t p", p=128)),
               inc=("dS", 16), waits=[("cV", CNT["cV"])])
            do("tensor", lambda e: e.transpose(out=P[1][:, 0:32], in_=s_bt[:], identity=s_iden[0:32, 0:32]),
               inc=("cT", 1), waits=[("dS", CNT["dS"])])
            do("vector", lambda e: e.tensor_copy(s_wob[:], P[1][:, 0:32]), inc=("cV", 1),
               waits=[("cT", CNT["cT"])])

            for w in range(2):
                dst = s_wkT if w == 0 else s_wvT
                for j in range(32):
                    par = (w * 32 + j) % 2
                    do("tensor", lambda e, par=par, w=w, j=j: e.transpose(
                        out=P[2 + par][:, 0:128], in_=s_wfl[:, w * D + j * 128: w * D + (j + 1) * 128],
                        identity=s_iden[:]),
                       inc=("cT", 1), waits=[("dS", ld8), ("cV", CNT["cV"] - 1)])
                    do("vector", lambda e, par=par, dst=dst, j=j: e.tensor_copy(
                        dst[:, j * 128:(j + 1) * 128], P[2 + par][:, 0:128]),
                       inc=("cV", 1), waits=[("cT", CNT["cT"])])

            for w in range(2):
                src = wq if w == 0 else wo
                dstd = wqT_d if w == 0 else woT_d
                for ot in range(32):
                    buf = s_wtmp[ot % 2]
                    do("sync", lambda e, src=src, ot=ot, buf=buf: e.dma_start(
                        out=buf[:], in_=src[ot * 128:(ot + 1) * 128, :]),
                       inc=("dS", 16), waits=[("cT", CNT["cT"])])
                    ldv = CNT["dS"]
                    for jb in range(8):
                        k = (w * 32 + ot) * 8 + jb
                        bb = s_wt4[k % 3]
                        par = 4 + (k % 2)
                        for ji in range(4):
                            j = jb * 4 + ji
                            do("tensor", lambda e, par=par, buf=buf, j=j, ji=ji: e.transpose(
                                out=P[par][:, ji * 128:(ji + 1) * 128], in_=buf[:, j * 128:(j + 1) * 128],
                                identity=s_iden[:]),
                               inc=("cT", 1),
                               waits=[("dS", ldv), ("cV", CNT["cV"])] if ji == 0 else [])
                        do("vector", lambda e, par=par, bb=bb: e.tensor_copy(
                            bb[:], P[par][:]),
                           inc=("cV", 1), waits=[("cT", CNT["cT"]), ("dA", CNT["dA"] - 2 * 16)])
                        do("scalar", lambda e, dstd=dstd, jb=jb, ot=ot, bb=bb: e.dma_start(
                            out=dstd.rearrange("(j p) t -> p j t", p=128)[:, jb * 4:(jb + 1) * 4, ot * 128:(ot + 1) * 128],
                            in_=bb[:].rearrange("p (j c) -> p j c", c=128)),
                           inc=("dA", 16), waits=[("cV", CNT["cV"])])
                    DQ.pump(24)

        run_phase(p0)
        barrier()
        p0_stack.close()

        p12_stack = ExitStack()
        e12 = p12_stack.enter_context
        s_xt = [e12(nc.sbuf_tensor(f"s_xt{i}", [128, D], F32)) for i in range(4)]
        s_xn1 = e12(nc.sbuf_tensor("s_xn1", [128, 32 * 512], BF16))
        s_bns = e12(nc.sbuf_tensor("s_bns", [128, 48], F32))
        s_mv = e12(nc.sbuf_tensor("s_mv", [128, 8], F32))
        s_cat = e12(nc.sbuf_tensor("s_cat", [128, 8], F32))
        s_g8 = e12(nc.sbuf_tensor("s_g8", [8, 128], F32))
        s_invb = e12(nc.sbuf_tensor("s_invb", [128, 512], F32))
        s_mib = e12(nc.sbuf_tensor("s_mib", [128, 512], F32))
        s_t1 = e12(nc.sbuf_tensor("s_t1", [128, 512], F32))
        s_vf = e12(nc.sbuf_tensor("s_vf", [128, 512], F32))

        def p12():
            for ch in range(8):
                for i in range(4):
                    t = ch * 4 + i
                    buf = s_xt[t % 4]
                    do("sync", lambda e, t=t, buf=buf: e.dma_start(
                        out=buf[:], in_=x_all[t * 128:(t + 1) * 128, :]),
                       inc=("dS", 16), waits=[("cT", CNT["cT"])] if t >= 4 else [])
                ld = CNT["dS"]
                for i in range(4):
                    buf = s_xt[(ch * 4 + i) % 4]
                    for sch in range(8):
                        w = [("dS", ld - (3 - i) * 16)]
                        if i == 0 and sch == 0:
                            w.append(("cA", CNT["cA"]))
                        do("vector", lambda e, buf=buf, sch=sch: e.bn_stats(
                            out=s_bns[:, sch * 6:(sch + 1) * 6], in_=buf[:, sch * 512:(sch + 1) * 512]),
                           inc=("cV", 1), waits=w)
                    do("vector", lambda e, i=i: e.bn_aggr(
                        out=s_mv[:, 2 * i:2 * i + 2], in_=s_bns[:]), inc=("cV", 1))
                do("scalar", lambda e: e.activation(
                    out=s_cat[:, 0:4], in_=s_mv[:].rearrange("p (i two) -> p two i", two=2)[:, 1, :],
                    func=AF.Sqrt, bias=s_eps[:, 0:1]), inc=("cA", 1), waits=[("cV", CNT["cV"])])
                do("vector", lambda e: e.reciprocal(s_cat[:, 0:4], s_cat[:, 0:4]),
                   inc=("cV", 1), waits=[("cA", CNT["cA"])])
                do("vector", lambda e: e.tensor_tensor(
                    out=s_cat[:, 4:8], in0=s_mv[:].rearrange("p (i two) -> p two i", two=2)[:, 0, :],
                    in1=s_cat[:, 0:4], op=ALU.mult), inc=("cV", 1))
                do("tensor", lambda e: e.transpose(out=P[6][0:8, 0:128], in_=s_cat[:], identity=s_iden[:]),
                   inc=("cT", 1), waits=[("cV", CNT["cV"])])
                do("vector", lambda e: e.tensor_copy(s_g8[:], P[6][0:8, 0:128]),
                   inc=("cV", 1), waits=[("cT", CNT["cT"]), ("dS", CNT["dS"])])
                do("sync", lambda e, ch=ch: e.dma_start(
                    out=mrow_d[0, ch * 512:(ch + 1) * 512].rearrange("(i c) -> i c", i=4), in_=s_g8[0:4, :]),
                   inc=("dS", 16), waits=[("cV", CNT["cV"])])
                do("sync", lambda e, ch=ch: e.dma_start(
                    out=mrow_d[1, ch * 512:(ch + 1) * 512].rearrange("(i c) -> i c", i=4), in_=s_g8[4:8, :]),
                   inc=("dS", 16))
                wrow = CNT["dS"]
                do("sync", lambda e, ch=ch: e.dma_start(
                    out=s_invb[:], in_=mrow_d[0:1, ch * 512:(ch + 1) * 512].broadcast_to((128, 512))),
                   inc=("dS", 16), waits=[("dS", wrow), ("cV", CNT["cV"])])
                do("sync", lambda e, ch=ch: e.dma_start(
                    out=s_mib[:], in_=mrow_d[1:2, ch * 512:(ch + 1) * 512].broadcast_to((128, 512))),
                   inc=("dS", 16))
                ldb = CNT["dS"]
                for dt in range(32):
                    par = dt % 2
                    for i in range(4):
                        buf = s_xt[(ch * 4 + i) % 4]
                        w = [("dS", ldb), ("cV", CNT["cV"] - 2)] if i == 0 else []
                        do("tensor", lambda e, par=par, buf=buf, dt=dt, i=i: e.transpose(
                            out=P[par][:, i * 128:(i + 1) * 128], in_=buf[:, dt * 128:(dt + 1) * 128],
                            identity=s_iden[:]),
                           inc=("cT", 1), waits=w)
                    do("vector", lambda e, par=par: e.tensor_tensor(
                        out=s_t1[:], in0=P[par][:], in1=s_invb[:], op=ALU.mult),
                       inc=("cV", 1),
                       waits=[("cT", CNT["cT"])] + ([("dG", CNT["dG"])] if dt == 0 else []))
                    do("vector", lambda e, dt=dt: e.tensor_tensor(
                        out=s_xn1[:, dt * 512:(dt + 1) * 512], in0=s_t1[:], in1=s_mib[:], op=ALU.subtract),
                       inc=("cV", 1))
                    DQ.pump(2)
                xnv = CNT["cV"]
                for w in range(2):
                    wT = s_wkT if w == 0 else s_wvT
                    pb = P[2 + w]
                    for dt in range(32):
                        do("tensor", lambda e, pb=pb, wT=wT, dt=dt: e.matmul(
                            pb[:], wT[:, dt * 128:(dt + 1) * 128], s_xn1[:, dt * 512:(dt + 1) * 512],
                            start=(dt == 0), stop=(dt == 31)),
                           inc=("cT", 1), waits=[("cV", xnv), ("cA", CNT["cA"])] if dt == 0 else [])
                    if w == 0:
                        do("scalar", lambda e, ch=ch: e.activation(
                            out=s_KF[:, ch * 512:(ch + 1) * 512], in_=P[2][:], func=AF.Identity,
                            bias=s_wkb[:, 0:1]), inc=("cA", 1), waits=[("cT", CNT["cT"])])
                    else:
                        do("scalar", lambda e: e.activation(
                            out=s_vf[:], in_=P[3][:], func=AF.Identity, bias=s_wvb[:, 0:1]),
                           inc=("cA", 1), waits=[("cT", CNT["cT"])])
                        for i in range(4):
                            do("tensor", lambda e, i=i: e.transpose(
                                out=P[5][:, 0:128], in_=s_vf[:, i * 128:(i + 1) * 128], identity=s_iden[:]),
                               inc=("cT", 1), waits=[("cA", CNT["cA"]), ("cV", CNT["cV"])])
                            do("vector", lambda e, ch=ch, i=i: e.tensor_copy(
                                s_VT[:, ch * 512 + i * 128: ch * 512 + (i + 1) * 128], P[5][:, 0:128]),
                               inc=("cV", 1), waits=[("cT", CNT["cT"])])
                for j, i in enumerate((0, 2)):
                    do("gpsimd", lambda e, ch=ch, j=j, i=i: e.dma_start(
                        out=xn1_d.rearrange("(dt p) t -> p dt t", p=128)[:, :, ch * 256 + j * 128: ch * 256 + (j + 1) * 128],
                        in_=s_xn1[:].rearrange("p (dt c) -> p dt c", c=512)[:, :, i * 128:(i + 1) * 128]),
                       inc=("dG", 16), waits=[("cV", xnv)])
                DQ.pump(10)

        run_phase(p12)
        barrier()
        p12_stack.close()

        p2_stack = ExitStack()
        e2 = p2_stack.enter_context
        s_x2 = [e2(nc.sbuf_tensor(f"s_x2_{i}", [128, 32 * 512], BF16)) for i in range(2)]
        s_wq2 = [e2(nc.sbuf_tensor(f"s_wq2_{i}", [128, D], BF16)) for i in range(2)]
        s_qt = [e2(nc.sbuf_tensor(f"s_qt{i}", [128, 512], BF16)) for i in range(2)]

        def p2b():
            tc_done = {}
            for tc in range(4):
                buf = s_x2[tc % 2]
                w = [("cT", tc_done[tc - 2])] if tc >= 2 else []
                for half in range(2):
                    do("sync", lambda e, tc=tc, buf=buf, half=half: e.dma_start(
                        out=buf[:, half * 16 * 512:(half + 1) * 16 * 512].rearrange("p (dt c) -> p dt c", c=512),
                        in_=xn1_d.rearrange("(dt p) t -> p dt t", p=128)[:, half * 16:(half + 1) * 16, tc * 512:(tc + 1) * 512]),
                       inc=("dS", 16), waits=w if half == 0 else [])
                ldx = CNT["dS"]
                for qt in range(32):
                    k = tc * 32 + qt
                    wbuf = s_wq2[k % 2]
                    for half in range(2):
                        do("gpsimd", lambda e, qt=qt, wbuf=wbuf, half=half: e.dma_start(
                            out=wbuf[:, half * 2048:(half + 1) * 2048].rearrange("p (dt c) -> p dt c", c=128),
                            in_=wqT_d.rearrange("(dt p) t -> p dt t", p=128)[:, half * 16:(half + 1) * 16, qt * 128:(qt + 1) * 128]),
                           inc=("dG", 16),
                           waits=[("cT", CNT["cT"] - 32)] if (k >= 2 and half == 0) else [])
                    ldw = CNT["dG"]
                    par = k % 2
                    for dt in range(32):
                        do("tensor", lambda e, par=par, wbuf=wbuf, buf=buf, dt=dt: e.matmul(
                            P[par][:], wbuf[:, dt * 128:(dt + 1) * 128], buf[:, dt * 512:(dt + 1) * 512],
                            start=(dt == 0), stop=(dt == 31)),
                           inc=("cT", 1),
                           waits=[("dS", ldx), ("dG", ldw), ("cA", CNT["cA"])] if dt == 0 else [])
                    qb = s_qt[k % 2]
                    do("scalar", lambda e, par=par, qb=qb, qt=qt: e.activation(
                        out=qb[:], in_=P[par][:], func=AF.Identity, bias=s_wqb[:, qt:qt + 1]),
                       inc=("cA", 1), waits=[("cT", CNT["cT"]), ("dS", CNT["dS"])])
                    do("sync", lambda e, qb=qb, qt=qt, tc=tc: e.dma_start(
                        out=q_d[qt * 128:(qt + 1) * 128, tc * 512:(tc + 1) * 512], in_=qb[:]),
                       inc=("dS", 16), waits=[("cA", CNT["cA"])])
                    DQ.pump(33)
                tc_done[tc] = CNT["cT"]

        run_phase(p2b)
        barrier()
        p2_stack.close()

        p3_stack = ExitStack()
        e3 = p3_stack.enter_context
        s_qh = [e3(nc.sbuf_tensor(f"s_qh{i}", [128, TO], BF16)) for i in range(2)]
        s_pt = [e3(nc.sbuf_tensor(f"s_pt{i}", [128, 512], BF16)) for i in range(3)]
        s_rs = e3(nc.sbuf_tensor("s_rs", [1, 512], F32))
        s_rb = [e3(nc.sbuf_tensor(f"s_rb{i}", [128, 512], F32)) for i in range(2)]
        s_at = [e3(nc.sbuf_tensor(f"s_at{i}", [128, 512], BF16)) for i in range(2)]

        def p3():
            h_done = {}
            for h in range(32):
                qbuf = s_qh[h % 2]
                do("sync", lambda e, h=h, qbuf=qbuf: e.dma_start(
                    out=qbuf[:], in_=q_d[h * 128:(h + 1) * 128, :]),
                   inc=("dS", 16), waits=[("cT", h_done[h - 2])] if h >= 2 else [])
                ldq = CNT["dS"]
                for qc in range(4):
                    u = h * 4 + qc
                    nkt = 8 * qc + 8
                    av = P[2 + (u % 2)]
                    sm = P[4 + (u % 2)]
                    avw = [("cV", CNT["cV"])]
                    for kt in range(nkt):
                        sl = max(0, kt // 2 - 4 * qc)
                        c0 = sl * 128
                        pb = P[kt % 2]
                        ptb = s_pt[kt % 3]
                        do("tensor", lambda e, pb=pb, kt=kt, qbuf=qbuf, qc=qc, c0=c0: e.matmul(
                            pb[:, c0:512], s_KF[:, kt * 128:(kt + 1) * 128],
                            qbuf[:, qc * 512 + c0: (qc + 1) * 512],
                            start=True, stop=True, skip_group_check=True),
                           inc=("cT", 1),
                           waits=([("dS", ldq)] if kt == 0 else []) + [("cA", CNT["cA"] - 1)])
                        do("scalar", lambda e, pb=pb, ptb=ptb, c0=c0: e.activation(
                            out=ptb[:, c0:512], in_=pb[:, c0:512], func=AF.Exp, scale=INV_SQRT_HD),
                           inc=("cA", 1), waits=[("cT", CNT["cT"])])
                        masked = False
                        i_d = kt // 2
                        if 4 * qc <= i_d < 4 * qc + 4:
                            cm = (i_d - 4 * qc) * 128
                            msk = s_tri if kt % 2 == 0 else s_mB
                            do("vector", lambda e, ptb=ptb, cm=cm, msk=msk: e.tensor_tensor(
                                out=ptb[:, cm:cm + 128], in0=ptb[:, cm:cm + 128], in1=msk[:], op=ALU.mult),
                               inc=("cV", 1), waits=[("cA", CNT["cA"])])
                            masked = True
                        pw = [("cA", CNT["cA"])] + ([("cV", CNT["cV"])] if masked else [])
                        do("tensor", lambda e, av=av, kt=kt, ptb=ptb, c0=c0, nkt=nkt: e.matmul(
                            av[:, c0:512], s_VT[:, kt * 128:(kt + 1) * 128], ptb[:, c0:512],
                            start=(kt == 0), stop=(kt == nkt - 1), skip_group_check=True),
                           inc=("cT", 1), waits=pw + (avw if kt == 0 else []))
                        do("tensor", lambda e, sm=sm, kt=kt, ptb=ptb, c0=c0, nkt=nkt: e.matmul(
                            sm[0:1, c0:512], s_ones[:], ptb[:, c0:512],
                            start=(kt == 0), stop=(kt == nkt - 1), skip_group_check=True),
                           inc=("cT", 1))
                    do("vector", lambda e, sm=sm: e.reciprocal(s_rs[:], sm[0:1, :]),
                       inc=("cV", 1), waits=[("cT", CNT["cT"]), ("cA", CNT["cA"] - 1)])
                    pr = P[6 + (u % 2)]
                    do("tensor", lambda e, pr=pr: e.matmul(
                        pr[:], s_onef[:], s_rs[:], start=True, stop=True, skip_group_check=True),
                       inc=("cT", 1), waits=[("cV", CNT["cV"]), ("cA", CNT["cA"] - 1)])
                    rb = s_rb[u % 2]
                    do("scalar", lambda e, pr=pr, rb=rb: e.activation(
                        out=rb[:], in_=pr[:], func=AF.Copy),
                       inc=("cA", 1), waits=[("cT", CNT["cT"]), ("cV", CNT["cV"] - 1)])
                    at = s_at[u % 2]
                    do("vector", lambda e, av=av, rb=rb, at=at: e.tensor_tensor(
                        out=at[:], in0=av[:], in1=rb[:], op=ALU.mult),
                       inc=("cV", 1), waits=[("cA", CNT["cA"]), ("dG", CNT["dG"] - 16)])
                    do("gpsimd", lambda e, at=at, h=h, qc=qc: e.dma_start(
                        out=attn_d[h * 128:(h + 1) * 128, qc * 512:(qc + 1) * 512], in_=at[:]),
                       inc=("dG", 16), waits=[("cV", CNT["cV"])])
                    DQ.pump(33)
                h_done[h] = CNT["cT"]

        run_phase(p3)
        barrier()
        p3_stack.close()
        kv_stack.close()

        p4_stack = ExitStack()
        e4 = p4_stack.enter_context
        s_accS = e4(nc.sbuf_tensor("s_accS", [1, TO], F32))
        s_accQ = e4(nc.sbuf_tensor("s_accQ", [1, TO], F32))
        s_ln2m = e4(nc.sbuf_tensor("s_ln2m", [1, TO], F32))
        s_ln2v = e4(nc.sbuf_tensor("s_ln2v", [1, TO], F32))
        s_ac4 = [e4(nc.sbuf_tensor(f"s_ac4_{i}", [128, 32 * 512], BF16)) for i in range(2)]
        s_wo4 = [e4(nc.sbuf_tensor(f"s_wo4_{i}", [128, D], BF16)) for i in range(2)]
        s_xr = [e4(nc.sbuf_tensor(f"s_xr{i}", [128, 512], F32)) for i in range(2)]
        s_tmp4 = e4(nc.sbuf_tensor("s_tmp4", [128, 512], F32))
        s_r2 = [e4(nc.sbuf_tensor(f"s_r2_{i}", [128, 512], F32)) for i in range(2)]
        s_r2b = [e4(nc.sbuf_tensor(f"s_r2b{i}", [128, 512], BF16)) for i in range(2)]
        s_sq4 = [e4(nc.sbuf_tensor(f"s_sq4_{i}", [128, 512], BF16)) for i in range(2)]

        def p4():
            do("vector", lambda e: e.memset(s_accS[:], 0.0), inc=("cV", 1))
            do("vector", lambda e: e.memset(s_accQ[:], 0.0), inc=("cV", 1))
            qc_done = {}
            for qc in range(4):
                abuf = s_ac4[qc % 2]
                w = [("cT", qc_done[qc - 2])] if qc >= 2 else []
                for half in range(2):
                    do("sync", lambda e, qc=qc, abuf=abuf, half=half: e.dma_start(
                        out=abuf[:, half * 16 * 512:(half + 1) * 16 * 512].rearrange("p (dt c) -> p dt c", c=512),
                        in_=attn_d.rearrange("(dt p) t -> p dt t", p=128)[:, half * 16:(half + 1) * 16, qc * 512:(qc + 1) * 512]),
                       inc=("dS", 16), waits=w if half == 0 else [])
                lda = CNT["dS"]
                for ot in range(32):
                    k = qc * 32 + ot
                    wbuf = s_wo4[k % 2]
                    for half in range(2):
                        do("gpsimd", lambda e, ot=ot, wbuf=wbuf, half=half: e.dma_start(
                            out=wbuf[:, half * 2048:(half + 1) * 2048].rearrange("p (dt c) -> p dt c", c=128),
                            in_=woT_d.rearrange("(dt p) t -> p dt t", p=128)[:, half * 16:(half + 1) * 16, ot * 128:(ot + 1) * 128]),
                           inc=("dG", 16),
                           waits=[("cT", CNT["cT"] - 44)] if (k >= 2 and half == 0) else [])
                    ldw = CNT["dG"]
                    xr = s_xr[k % 2]
                    do("sync", lambda e, qc=qc, ot=ot, xr=xr: e.dma_start(
                        out=xr[:].rearrange("p (g c) -> p g c", c=128),
                        in_=x_own.rearrange("(g p) d -> p g d", p=128)[:, qc * 4:(qc + 1) * 4, ot * 128:(ot + 1) * 128]),
                       inc=("dS", 16), waits=[("cT", CNT["cT"] - 40)] if k >= 2 else [])
                    ldx = CNT["dS"]
                    par = k % 2
                    for dt in range(32):
                        do("tensor", lambda e, par=par, wbuf=wbuf, abuf=abuf, dt=dt: e.matmul(
                            P[par][:], wbuf[:, dt * 128:(dt + 1) * 128], abuf[:, dt * 512:(dt + 1) * 512],
                            start=(dt == 0), stop=(dt == 31)),
                           inc=("cT", 1),
                           waits=[("dS", lda), ("dG", ldw), ("cV", CNT["cV"])] if dt == 0 else [])
                    mmv = CNT["cT"]
                    for i in range(4):
                        do("tensor", lambda e, par=par, xr=xr, i=i: e.transpose(
                            out=P[4 + par][:, i * 128:(i + 1) * 128], in_=xr[:, i * 128:(i + 1) * 128],
                            identity=s_iden[:]),
                           inc=("cT", 1), waits=[("dS", ldx), ("cV", CNT["cV"])] if i == 0 else [])
                    do("scalar", lambda e, par=par, ot=ot: e.activation(
                        out=s_tmp4[:], in_=P[par][:], func=AF.Identity, bias=s_wob[:, ot:ot + 1]),
                       inc=("cA", 1), waits=[("cT", mmv), ("cV", CNT["cV"])])
                    r2 = s_r2[k % 2]
                    do("vector", lambda e, par=par, r2=r2: e.tensor_tensor(
                        out=r2[:], in0=s_tmp4[:], in1=P[4 + par][:], op=ALU.add),
                       inc=("cV", 1), waits=[("cA", CNT["cA"]), ("cT", CNT["cT"]), ("dS", CNT["dS"])])
                    do("sync", lambda e, r2=r2, ot=ot, qc=qc: e.dma_start(
                        out=r2_d[ot * 128:(ot + 1) * 128, qc * 512:(qc + 1) * 512], in_=r2[:]),
                       inc=("dS", 16), waits=[("cV", CNT["cV"])])
                    r2b = s_r2b[k % 2]
                    do("vector", lambda e, r2=r2, r2b=r2b: e.tensor_copy(r2b[:], r2[:]),
                       inc=("cV", 1), waits=[("cT", CNT["cT"])])
                    sq = s_sq4[k % 2]
                    do("scalar", lambda e, r2b=r2b, sq=sq: e.activation(out=sq[:], in_=r2b[:], func=AF.Square),
                       inc=("cA", 1), waits=[("cV", CNT["cV"]), ("cT", CNT["cT"])])
                    do("tensor", lambda e, r2b=r2b: e.matmul(
                        P[6][0:1, :], s_ones[:], r2b[:], start=True, stop=True, skip_group_check=True),
                       inc=("cT", 1), waits=[("cV", CNT["cV"])])
                    do("tensor", lambda e, sq=sq: e.matmul(
                        P[7][0:1, :], s_ones[:], sq[:], start=True, stop=True, skip_group_check=True),
                       inc=("cT", 1), waits=[("cA", CNT["cA"])])
                    do("vector", lambda e, qc=qc: e.tensor_tensor(
                        out=s_accS[0:1, qc * 512:(qc + 1) * 512], in0=s_accS[0:1, qc * 512:(qc + 1) * 512],
                        in1=P[6][0:1, :], op=ALU.add),
                       inc=("cV", 1), waits=[("cT", CNT["cT"])])
                    do("vector", lambda e, qc=qc: e.tensor_tensor(
                        out=s_accQ[0:1, qc * 512:(qc + 1) * 512], in0=s_accQ[0:1, qc * 512:(qc + 1) * 512],
                        in1=P[7][0:1, :], op=ALU.add),
                       inc=("cV", 1))
                    DQ.pump(33)
                qc_done[qc] = CNT["cT"]
            do("vector", lambda e: e.tensor_scalar(
                out=s_ln2m[:], in0=s_accS[:], scalar1=1.0 / D, scalar2=None, op0=ALU.mult),
               inc=("cV", 1))
            do("vector", lambda e: e.tensor_tensor(
                out=s_ln2v[:], in0=s_ln2m[:], in1=s_ln2m[:], op=ALU.mult),
               inc=("cV", 1))
            do("vector", lambda e: e.scalar_tensor_tensor(
                out=s_ln2v[:], in0=s_accQ[:], scalar=1.0 / D, in1=s_ln2v[:],
                op0=ALU.mult, op1=ALU.subtract),
               inc=("cV", 1))
            do("scalar", lambda e: e.activation(
                out=s_ln2v[:], in_=s_ln2v[:], func=AF.Sqrt, bias=s_eps[0:1, 0:1]),
               inc=("cA", 1), waits=[("cV", CNT["cV"])])
            do("vector", lambda e: e.reciprocal(s_ln2v[:], s_ln2v[:]),
               inc=("cV", 1), waits=[("cA", CNT["cA"])])
            do("vector", lambda e: e.tensor_tensor(
                out=s_ln2m[:], in0=s_ln2m[:], in1=s_ln2v[:], op=ALU.mult),
               inc=("cV", 1))
            do("sync", lambda e: e.dma_start(out=mrow_d[3, 0:TO].unsqueeze(0), in_=s_ln2v[:]),
               inc=("dS", 16), waits=[("cV", CNT["cV"])])
            do("sync", lambda e: e.dma_start(out=mrow_d[4, 0:TO].unsqueeze(0), in_=s_ln2m[:]),
               inc=("dS", 16))

        run_phase(p4)
        barrier()
        p4_stack.close()

        p5b_stack = ExitStack()
        e5b = p5b_stack.enter_context
        s_xn2 = e5b(nc.sbuf_tensor("s_xn2", [128, 32 * 2048], BF16))
        s_r2t = [e5b(nc.sbuf_tensor("s_r2t0", [128, 512], F32))] * 2
        s_w12 = [e5b(nc.sbuf_tensor(f"s_w12_{i}", [128, D], BF16)) for i in range(3)]
        s_sg = e5b(nc.sbuf_tensor("s_sg", [128, 512], F32))
        s_ht = [e5b(nc.sbuf_tensor("s_ht0", [128, 2048], BF16))] * 2

        def p5b():
            s_i2b = s_w12[0].bitcast(F32)[:, 0:512]
            s_m2b = s_w12[0].bitcast(F32)[:, 512:1024]
            for tc in range(4):
                do("sync", lambda e, tc=tc: e.dma_start(
                    out=s_i2b, in_=mrow_d[3:4, tc * 512:(tc + 1) * 512].broadcast_to((128, 512))),
                   inc=("dS", 16), waits=[("cV", CNT["cV"])])
                do("sync", lambda e, tc=tc: e.dma_start(
                    out=s_m2b, in_=mrow_d[4:5, tc * 512:(tc + 1) * 512].broadcast_to((128, 512))),
                   inc=("dS", 16))
                ldb = CNT["dS"]
                for dt in range(32):
                    rb = s_r2t[dt % 2]
                    do("gpsimd", lambda e, rb=rb, dt=dt, tc=tc: e.dma_start(
                        out=rb[:], in_=r2_d[dt * 128:(dt + 1) * 128, tc * 512:(tc + 1) * 512]),
                       inc=("dG", 16), waits=[("cV", CNT["cV"] - 1)] if (tc * 32 + dt) >= 1 else [])
                    do("vector", lambda e, rb=rb: e.tensor_tensor(
                        out=s_sg[:], in0=rb[:], in1=s_i2b, op=ALU.mult),
                       inc=("cV", 1), waits=[("dG", CNT["dG"]), ("dS", ldb)])
                    do("vector", lambda e, dt=dt, tc=tc: e.tensor_tensor(
                        out=s_xn2[:, dt * 2048 + tc * 512: dt * 2048 + (tc + 1) * 512],
                        in0=s_sg[:], in1=s_m2b, op=ALU.subtract),
                       inc=("cV", 1))
            xnv = CNT["cV"]
            hc_done = {}
            for hc in range(NHC):
                w1b = s_w12[hc % 2]
                w2b = s_w12[2]
                wwar = ([("cT", hc_done[hc - 2])] if hc >= 2 else [("cV", xnv)]) + [("qD", DQ.ensure12(hc // 4))]
                wwar2 = ([("cT", hc_done[hc - 1])] if hc >= 1 else [("cV", xnv)]) + [("qD", DQ.ensure12(hc // 4))]
                for half in range(2):
                    do("sync", lambda e, w1b=w1b, hc=hc, half=half: e.dma_start(
                        out=w1b[:, half * 2048:(half + 1) * 2048].rearrange("p (dt c) -> p dt c", c=128),
                        in_=w1t_d.rearrange("(dt p) t -> p dt t", p=128)[:, half * 16:(half + 1) * 16, hc * 128:(hc + 1) * 128]),
                       inc=("dS", 16), waits=wwar if half == 0 else [])
                    do("gpsimd", lambda e, w2b=w2b, hc=hc, half=half: e.dma_start(
                        out=w2b[:, half * 2048:(half + 1) * 2048].rearrange("p (dt c) -> p dt c", c=128),
                        in_=w2t_d.rearrange("(dt p) t -> p dt t", p=128)[:, half * 16:(half + 1) * 16, hc * 128:(hc + 1) * 128]),
                       inc=("dG", 16), waits=wwar2 if half == 0 else [])
                ldw1 = CNT["dS"]
                ldw2 = CNT["dG"]
                ht = s_ht[hc % 2]
                for tc in range(4):
                    p1 = P[tc % 2]
                    p2 = P[2 + tc % 2]
                    for dt in range(32):
                        do("tensor", lambda e, p1=p1, w1b=w1b, dt=dt, tc=tc: e.matmul(
                            p1[:], w1b[:, dt * 128:(dt + 1) * 128],
                            s_xn2[:, dt * 2048 + tc * 512: dt * 2048 + (tc + 1) * 512],
                            start=(dt == 0), stop=(dt == 31)),
                           inc=("cT", 1),
                           waits=[("cV", xnv), ("dS", ldw1), ("cA", CNT["cA"])] if dt == 0 else [])
                        if dt % 8 == 7:
                            DQ.pump(3)
                    do("scalar", lambda e, p1=p1: e.activation(out=s_sg[:], in_=p1[:], func=AF.Silu),
                       inc=("cA", 1), waits=[("cT", CNT["cT"]), ("cV", CNT["cV"])])
                    for dt in range(32):
                        do("tensor", lambda e, p2=p2, w2b=w2b, dt=dt, tc=tc: e.matmul(
                            p2[:], w2b[:, dt * 128:(dt + 1) * 128],
                            s_xn2[:, dt * 2048 + tc * 512: dt * 2048 + (tc + 1) * 512],
                            start=(dt == 0), stop=(dt == 31)),
                           inc=("cT", 1),
                           waits=[("dG", ldw2), ("cV", CNT["cV"])] if dt == 0 else [])
                        if dt % 8 == 7:
                            DQ.pump(3)
                    do("vector", lambda e, p2=p2, ht=ht, tc=tc: e.tensor_tensor(
                        out=ht[:, tc * 512:(tc + 1) * 512], in0=p2[:], in1=s_sg[:], op=ALU.mult),
                       inc=("cV", 1), waits=[("cT", CNT["cT"]), ("cA", CNT["cA"]), ("dS", CNT["dS"])])

                do("sync", lambda e, ht=ht, hc=hc: e.dma_start(
                    out=h_d[hc * 128:(hc + 1) * 128, :], in_=ht[:]),
                   inc=("dS", 16), waits=[("cV", CNT["cV"])])
                hc_done[hc] = CNT["cT"]

        run_phase(p5b)
        barrier()
        p5b_stack.close()

        p6_stack = ExitStack()
        e6 = p6_stack.enter_context
        s_H = e6(nc.sbuf_tensor("s_H", [128, NHC * 512], BF16))
        s_w3 = [e6(nc.sbuf_tensor(f"s_w3_{i}", [128, 8 * 256], BF16)) for i in range(3)]
        s_r26 = [e6(nc.sbuf_tensor(f"s_r26_{i}", [128, 512], F32)) for i in range(2)]
        s_of = [e6(nc.sbuf_tensor(f"s_of{i}", [128, 512], F32)) for i in range(2)]
        s_ot4 = [e6(nc.sbuf_tensor(f"s_ot4_{i}", [128, 512], F32)) for i in range(2)]

        def p6():
            NG = (NHC + 7) // 8
            g_done = {}
            tq_done = {}
            for tq in range(4):
                hw = [("cT", tq_done[tq - 1])] if tq >= 1 else []
                for g in range(NG):
                    n = min(8, NHC - g * 8)
                    do("sync", lambda e, g=g, n=n, tq=tq: e.dma_start(
                        out=s_H[:, g * 8 * 512:(g * 8 + n) * 512].rearrange("p (j c) -> p j c", c=512),
                        in_=h_d.rearrange("(j p) t -> p j t", p=128)[:, g * 8:g * 8 + n, tq * 512:(tq + 1) * 512]),
                       inc=("dS", 16), waits=hw if g == 0 else [])
                ldh = CNT["dS"]
                for og in range(16):
                    u = tq * 16 + og
                    for g in range(NG):
                        n = min(8, NHC - g * 8)
                        gi = u * NG + g
                        wb = s_w3[gi % 3]
                        do("scalar", lambda e, wb=wb, g=g, n=n, og=og: e.dma_start(
                            out=wb[:, 0:n * 256].rearrange("p (j c) -> p j c", c=256),
                            in_=w3t_d.rearrange("(j p) t -> p j t", p=128)[:, g * 8:g * 8 + n, og * 256:(og + 1) * 256]),
                           inc=("dA", 16),
                           waits=([("cT", g_done[gi - 3])] if gi >= 3 else []) + [("qD", DQ.ensure3(g * 8 + n - 1))])
                        gld = CNT["dA"]
                        for j in range(n):
                            hc = g * 8 + j
                            for sub in range(2):
                                do("tensor", lambda e, wb=wb, j=j, sub=sub, hc=hc: e.matmul(
                                    P[sub][:], wb[:, j * 256 + sub * 128: j * 256 + (sub + 1) * 128],
                                    s_H[:, hc * 512:(hc + 1) * 512],
                                    start=(hc == 0), stop=(hc == NHC - 1), skip_group_check=True),
                                   inc=("cT", 1),
                                   waits=([("dS", ldh), ("dA", gld), ("cV", CNT["cV"])] if hc == 0 else
                                          ([("dA", gld)] if j == 0 else [])))
                        g_done[gi] = CNT["cT"]
                        DQ.pump(12)
                    mmv = CNT["cT"]
                    for sub in range(2):
                        ot = og * 2 + sub
                        r26 = s_r26[sub]
                        do("sync", lambda e, r26=r26, ot=ot, tq=tq: e.dma_start(
                            out=r26[:], in_=r2_d[ot * 128:(ot + 1) * 128, tq * 512:(tq + 1) * 512]),
                           inc=("dS", 16), waits=[("cV", CNT["cV"])] if u >= 1 else [])
                        of = s_of[sub]
                        do("vector", lambda e, sub=sub, r26=r26, of=of: e.tensor_tensor(
                            out=of[:], in0=P[sub][:], in1=r26[:], op=ALU.add),
                           inc=("cV", 1), waits=[("cT", mmv), ("dS", CNT["dS"])])
                        ofv = CNT["cV"]
                        for tt in range(4):
                            do("tensor", lambda e, of=of, tt=tt, sub=sub: e.transpose(
                                out=P[4 + sub][:, tt * 128:(tt + 1) * 128], in_=of[:, tt * 128:(tt + 1) * 128],
                                identity=s_iden[:]),
                               inc=("cT", 1), waits=[("cV", ofv)] if tt == 0 else [])
                        trv = CNT["cT"]
                        o4 = s_ot4[sub]
                        do("vector", lambda e, o4=o4, sub=sub: e.tensor_copy(
                            o4[:], P[4 + sub][:]),
                           inc=("cV", 1), waits=[("cT", trv), ("dS", CNT["dS"])])
                        do("sync", lambda e, o4=o4, ot=ot, tq=tq: e.dma_start(
                            out=y_own.rearrange("(g p) d -> p g d", p=128)[:, tq * 4:(tq + 1) * 4, ot * 128:(ot + 1) * 128],
                            in_=o4[:].rearrange("p (g c) -> p g c", c=128)),
                           inc=("dS", 16), waits=[("cV", CNT["cV"])])
                tq_done[tq] = CNT["cT"]
            DQ.drain_all()

        run_phase(p6)
        barrier()
        p6_stack.close()

    return nc


_NC_CACHE = None


def kernel(**inputs):
    global _NC_CACHE
    B = 4
    x = np.asarray(inputs["L0_attn_x"], dtype=np.float32)
    wq_w = np.asarray(inputs["L0_attn_wq_w"], dtype=np.float32)
    wq_b = np.asarray(inputs["L0_attn_wq_b"], dtype=np.float32)
    wk_w = np.asarray(inputs["L0_attn_wk_w"], dtype=np.float32)
    wk_b = np.asarray(inputs["L0_attn_wk_b"], dtype=np.float32)
    wv_w = np.asarray(inputs["L0_attn_wv_w"], dtype=np.float32)
    wv_b = np.asarray(inputs["L0_attn_wv_b"], dtype=np.float32)
    wo_w = np.asarray(inputs["L0_attn_wo_w"], dtype=np.float32)
    wo_b = np.asarray(inputs["L0_attn_wo_b"], dtype=np.float32)

    def codes_u8_T(a):
        t = np.ascontiguousarray(np.asarray(a).T)
        return np.ascontiguousarray(t.view(np.uint8)[:, 0::4])

    w1qT = codes_u8_T(inputs["L0_ffn_w1_q"])
    w2qT = codes_u8_T(inputs["L0_ffn_w2_q"])
    w3qT = codes_u8_T(inputs["L0_ffn_w3_q"])
    sc1T = np.ascontiguousarray(np.asarray(inputs["L0_ffn_w1_scale"], dtype=np.float32).T)
    sc2T = np.ascontiguousarray(np.asarray(inputs["L0_ffn_w2_scale"], dtype=np.float32).T)
    sc3T = np.ascontiguousarray(np.asarray(inputs["L0_ffn_w3_scale"], dtype=np.float32).T)

    tri = np.tril(np.ones((128, 128), dtype=np.float32)).T.astype(ml_dtypes.bfloat16)
    iden = np.eye(128, dtype=np.float32)
    ones128 = np.ones((128, 128), dtype=ml_dtypes.bfloat16)
    zeros128 = np.zeros((128, 128), dtype=ml_dtypes.bfloat16)

    if _NC_CACHE is None:
        _NC_CACHE = build_nc()
    nc = _NC_CACHE

    in_maps = []
    for c in range(8):
        b, h = c // 2, c % 2
        r4, r2 = c % 4, c % 2
        xt = x[b].reshape(32, 128, D)
        if h == 0:
            perm = np.arange(32)
        else:
            perm = np.arange(32).reshape(16, 2)[:, ::-1].reshape(32)
        x_perm = np.ascontiguousarray(xt[perm].reshape(T, D))
        x_own_a = np.ascontiguousarray(xt[perm[0::2]].reshape(TO, D))
        in_maps.append({
            "x_all": x_perm, "x_own": x_own_a,
            "wq": wq_w, "wo": wo_w, "wk": wk_w, "wv": wv_w,
            "wq_b": wq_b, "wo_b": wo_b, "wk_b": wk_b, "wv_b": wv_b,
            "w1qT": w1qT, "w2qT": w2qT, "w3qT": w3qT,
            "sc1T": sc1T, "sc2T": sc2T, "sc3T": sc3T,
            "tri_in": tri, "mB_in": zeros128 if h == 0 else ones128,
            "iden_in": iden,
        })

    res = run_bass_kernel_spmd(nc, in_maps, core_ids=list(range(8)))

    out = np.empty((B, T, D), dtype=np.float32)
    for c in range(8):
        b, h = c // 2, c % 2
        yo = np.asarray(res.results[c]["y_own"], dtype=np.float32).reshape(16, 128, D)
        ob = out[b].reshape(32, 128, D)
        for k in range(16):
            ob[2 * k + h] = yo[k]
    return out


# version 18 (final)
# speedup vs baseline: 1.4978x; 1.4061x over previous
import math
import numpy as np
import ml_dtypes
from contextlib import ExitStack

import concourse.bass as bass
import concourse.mybir as mybir
from concourse.bass_utils import run_bass_kernel_spmd

F32 = mybir.dt.float32
BF16 = mybir.dt.bfloat16
U8 = mybir.dt.uint8
AF = mybir.ActivationFunctionType
ALU = mybir.AluOpType

D = 4096
T = 4096
TO = 2048
HID = 11008
NHC = HID // 128
EPS = 1e-5
INV_SQRT_HD = 1.0 / math.sqrt(128.0)

W12_ROWS = D
W3_ROWS = HID
FD_Q = 512

NF4 = [-1.0, -0.6961928009986877, -0.5250730514526367, -0.39491748809814453,
       -0.28444138169288635, -0.18477343022823334, -0.09105003625154495, 0.0,
       0.07958029955625534, 0.16093020141124725, 0.24611230194568634,
       0.33791524171829224, 0.44070982933044434, 0.5626170039176941,
       0.7229568362236023, 1.0]

ENGS = ["sync", "gpsimd", "scalar", "vector", "tensor"]


def build_nc():
    nc = bass.Bass("TRN2", target_bir_lowering=False, num_devices=8)

    x_all = nc.dram_tensor("x_all", [T, D], F32, kind="ExternalInput").ap()
    x_own = nc.dram_tensor("x_own", [TO, D], F32, kind="ExternalInput").ap()
    wq = nc.dram_tensor("wq", [D, D], F32, kind="ExternalInput").ap()
    wo = nc.dram_tensor("wo", [D, D], F32, kind="ExternalInput").ap()
    wk = nc.dram_tensor("wk", [128, D], F32, kind="ExternalInput").ap()
    wv = nc.dram_tensor("wv", [128, D], F32, kind="ExternalInput").ap()
    wq_b = nc.dram_tensor("wq_b", [D], F32, kind="ExternalInput").ap()
    wo_b = nc.dram_tensor("wo_b", [D], F32, kind="ExternalInput").ap()
    wk_b = nc.dram_tensor("wk_b", [128], F32, kind="ExternalInput").ap()
    wv_b = nc.dram_tensor("wv_b", [128], F32, kind="ExternalInput").ap()
    w1qT = nc.dram_tensor("w1qT", [W12_ROWS, HID], U8, kind="ExternalInput").ap()
    w2qT = nc.dram_tensor("w2qT", [W12_ROWS, HID], U8, kind="ExternalInput").ap()
    w3qT = nc.dram_tensor("w3qT", [W3_ROWS, D], U8, kind="ExternalInput").ap()
    sc1T = nc.dram_tensor("sc1T", [64, HID], F32, kind="ExternalInput").ap()
    sc2T = nc.dram_tensor("sc2T", [64, HID], F32, kind="ExternalInput").ap()
    sc3T = nc.dram_tensor("sc3T", [172, D], F32, kind="ExternalInput").ap()
    tri_in = nc.dram_tensor("tri_in", [128, 128], BF16, kind="ExternalInput").ap()
    mB_in = nc.dram_tensor("mB_in", [128, 128], BF16, kind="ExternalInput").ap()
    iden_in = nc.dram_tensor("iden_in", [128, 128], F32, kind="ExternalInput").ap()

    y_own = nc.dram_tensor("y_own", [TO, D], F32, kind="ExternalOutput").ap()
    wqT_d = nc.dram_tensor("wqT_d", [D, D], BF16, kind="Internal").ap()
    woT_d = nc.dram_tensor("woT_d", [D, D], BF16, kind="Internal").ap()
    xn1_d = nc.dram_tensor("xn1_d", [D, TO], BF16, kind="Internal").ap()
    q_d = nc.dram_tensor("q_d", [D, TO], BF16, kind="Internal").ap()
    attn_d = nc.dram_tensor("attn_d", [D, TO], BF16, kind="Internal").ap()
    r2_d = nc.dram_tensor("r2_d", [D, TO], F32, kind="Internal").ap()
    w1t_d = nc.dram_tensor("w1t_d", [D, HID], BF16, kind="Internal").ap()
    w2t_d = nc.dram_tensor("w2t_d", [D, HID], BF16, kind="Internal").ap()
    w3t_d = nc.dram_tensor("w3t_d", [HID, D], BF16, kind="Internal").ap()
    h_d = nc.dram_tensor("h_d", [HID, TO], BF16, kind="Internal").ap()
    mrow_d = nc.dram_tensor("mrow_d", [8, T], F32, kind="Internal").ap()

    with ExitStack() as top:
        ent = top.enter_context

        P = [ent(nc.psum_tensor(f"pb{i}", [128, 512], F32)) for i in range(8)]

        sem_names = ["dS", "dG", "dA", "cT", "cA", "cV", "cG", "qD", "qL", "qM", "qV", "qP", "qA", "qC"]
        SEMS = {n: ent(nc.semaphore(n)) for n in sem_names}
        CNT = {n: 0 for n in sem_names}
        ROLE = [None]

        def eng(name):
            return getattr(nc, name)

        def do(engname, f, inc=None, waits=()):
            if ROLE[0] == engname:
                e = eng(engname)
                for s, v in waits:
                    if v > 0:
                        e.wait_ge(SEMS[s], v)
                r = f(e)
                if inc is not None:
                    r.then_inc(SEMS[inc[0]], inc[1])
            if inc is not None:
                CNT[inc[0]] += inc[1]

        block = ent(nc.Block())


        def mk_bufset(e, tag, fd, nb):
            R = dict(side="right")
            return {
                "cd": [e(nc.sbuf_tensor(f"dq_cd{tag}{i}", [128, fd], U8, **R)) for i in range(nb)],
                "scb": [e(nc.sbuf_tensor(f"dq_sc{tag}{i}", [128, fd], F32, **R)) for i in range(nb)],
                "mk": [e(nc.sbuf_tensor(f"dq_mk{tag}{i}", [128, fd], U8, **R)) for i in range(12)],
                "lv": [e(nc.sbuf_tensor(f"dq_lv{tag}{i}", [128, 8 * fd], BF16, **R)) for i in range(2)],
                "wd": [e(nc.sbuf_tensor(f"dq_wd{tag}{i}", [128, fd], BF16, **R)) for i in range(2)],
                "fd": fd, "nb": nb,
            }
        bufsetA = mk_bufset(ent, "A", FD_Q, 3)
        BUFSETS = {"A": bufsetA}

        class Dequant:
            def __init__(self):
                self.units = []
                self.mark_after = {}
                nch12 = (HID + FD_Q - 1) // FD_Q
                for c in range(nch12):
                    fd = min(FD_Q, HID - c * FD_Q)
                    for w, (qT, scT, dst) in enumerate([(w1qT, sc1T, w1t_d),
                                                        (w2qT, sc2T, w2t_d)]):
                        for rt in range(32):
                            self.units.append((qT, scT, dst, rt, c * FD_Q, fd))
                    self.mark_after[len(self.units) - 1] = ("m12", c)
                nch3 = (D + FD_Q - 1) // FD_Q
                for rt in range(W3_ROWS // 128):
                    for c in range(nch3):
                        fd = min(FD_Q, D - c * FD_Q)
                        self.units.append((w3qT, sc3T, w3t_d, rt, c * FD_Q, fd))
                    self.mark_after[len(self.units) - 1] = ("m3", rt)
                nu = len(self.units)
                gl = []
                for s in range(-2, nu + 3):
                    if 0 <= s < nu:
                        gl.append(("m01", s))
                        gl.append(("m23", s))
                    if 0 <= s - 1 < nu:
                        for j in range(4):
                            gl.append(("leaf", (s - 1, j)))
                    if 0 <= s - 2 < nu:
                        gl.append(("cp12", s - 2))
                        gl.append(("cp34", s - 2))
                        gl.append(("cp56", s - 2))
                        gl.append(("cp7", s - 2))
                        gl.append(("sc", s - 2))
                    if 0 <= s - 3 < nu:
                        gl.append(("st", s - 3))
                    if 0 <= s + 2 < nu:
                        gl.append(("ldc", s + 2))
                    if 0 <= s < nu:
                        gl.append(("lds", s))
                self.glist = gl
                self.i = 0
                self.m12 = {}
                self.m3 = {}

            def emit_group(self, g):
                kind, pay = g
                k = pay[0] if isinstance(pay, tuple) else pay
                qT, scT, dst, rt, c0, fd = self.units[k]
                bs = BUFSETS["A"]
                nb = bs["nb"]
                cd = bs["cd"][k % nb]
                scb = bs["scb"][k % nb]
                mk = bs["mk"][(k % 3) * 4:(k % 3) * 4 + 4]
                lv = bs["lv"][k % 2]
                lfd = bs["fd"]
                wd = bs["wd"][k % 2]
                if kind == "ldc":
                    do("sync", lambda e: e.dma_start(
                        out=cd[:, 0:fd], in_=qT[rt * 128:(rt + 1) * 128, c0:c0 + fd]),
                       inc=("qL", 16), waits=[("qV", 4 * (k - 2))])
                elif kind == "lds":
                    do("sync", lambda e: e.dma_start(
                        out=scb[0:64, 0:fd], in_=scT[2 * rt:2 * rt + 1, c0:c0 + fd].broadcast_to((64, fd))),
                       inc=("qM", 16), waits=[("cG", k - 2)])
                    do("sync", lambda e: e.dma_start(
                        out=scb[64:128, 0:fd], in_=scT[2 * rt + 1:2 * rt + 2, c0:c0 + fd].broadcast_to((64, fd))),
                       inc=("qM", 16))
                elif kind == "m01":
                    do("vector", lambda e: e.tensor_scalar(
                        out=mk[0][:, 0:fd], in0=cd[:, 0:fd], scalar1=1, scalar2=None, op0=ALU.bitwise_and),
                       inc=("qV", 1), waits=[("qL", 16 * (k + 1)), ("qA", 8 * (k - 2)), ("qP", 7 * (k - 2))])
                    do("vector", lambda e: e.tensor_scalar(
                        out=mk[1][:, 0:fd], in0=cd[:, 0:fd], scalar1=2, scalar2=None, op0=ALU.bitwise_and),
                       inc=("qV", 1))
                elif kind == "m23":
                    do("vector", lambda e: e.tensor_scalar(
                        out=mk[2][:, 0:fd], in0=cd[:, 0:fd], scalar1=4, scalar2=None, op0=ALU.bitwise_and),
                       inc=("qV", 1))
                    do("vector", lambda e: e.tensor_scalar(
                        out=mk[3][:, 0:fd], in0=cd[:, 0:fd], scalar1=8, scalar2=None, op0=ALU.bitwise_and),
                       inc=("qV", 1))
                elif kind == "leaf":
                    j2 = pay[1]
                    for j in (2 * j2, 2 * j2 + 1):
                        dd = float(NF4[2 * j + 1] - NF4[2 * j])
                        bb = float(NF4[2 * j])
                        do("scalar", lambda e, j=j, dd=dd, bb=bb: e.activation(
                            out=lv[:, j * lfd: j * lfd + fd], in_=mk[0][:, 0:fd],
                            func=AF.Copy, bias=bb, scale=dd),
                           inc=("qA", 1),
                           waits=[("qV", 4 * (k + 1)), ("cG", k - 1)] if (j2 == 0 and j % 2 == 0) else [])
                elif kind in ("cp12", "cp34"):
                    ii = (0, 1) if kind == "cp12" else (2, 3)
                    for i in ii:
                        do("vector", lambda e, i=i: e.copy_predicated(
                            out=lv[:, (2 * i) * lfd:(2 * i) * lfd + fd], mask=mk[1][:, 0:fd],
                            data=lv[:, (2 * i + 1) * lfd:(2 * i + 1) * lfd + fd]),
                           inc=("qP", 1), waits=[("qA", 8 * k + 8)] if i == 0 else [])
                elif kind == "cp56":
                    for i in range(2):
                        do("vector", lambda e, i=i: e.copy_predicated(
                            out=lv[:, (4 * i) * lfd:(4 * i) * lfd + fd], mask=mk[2][:, 0:fd],
                            data=lv[:, (4 * i + 2) * lfd:(4 * i + 2) * lfd + fd]),
                           inc=("qP", 1))
                elif kind == "cp7":
                    do("vector", lambda e: e.copy_predicated(
                        out=lv[:, 0:fd], mask=mk[3][:, 0:fd], data=lv[:, 4 * lfd:4 * lfd + fd]),
                       inc=("qP", 1))
                elif kind == "sc":
                    wv = [("qP", 7 * (k + 1)), ("qM", 32 * (k + 1)), ("qD", 16 * (k - 1))]
                    do("gpsimd", lambda e: e.tensor_tensor(
                        out=wd[:, 0:fd], in0=lv[:, 0:fd], in1=scb[:, 0:fd], op=ALU.mult),
                       inc=("cG", 1), waits=wv)
                elif kind == "st":
                    do("scalar", lambda e: e.dma_start(
                        out=dst[rt * 128:(rt + 1) * 128, c0:c0 + fd], in_=wd[:, 0:fd]),
                       inc=("qD", 16), waits=[("cG", k + 1)])
                    if kind == "st" and k in self.mark_after:
                        mk2, idx = self.mark_after[k]
                        if mk2 == "m12":
                            self.m12[idx] = CNT["qD"]
                        else:
                            self.m3[idx] = CNT["qD"]

            def pump(self, n):
                while n > 0 and self.i < len(self.glist):
                    self.emit_group(self.glist[self.i])
                    self.i += 1
                    n -= 1

            def ensure12(self, c):
                while c not in self.m12:
                    self.pump(1)
                return self.m12[c]

            def ensure3(self, rt):
                while rt not in self.m3:
                    self.pump(1)
                return self.m3[rt]

            def drain_all(self):
                while self.i < len(self.glist):
                    self.emit_group(self.glist[self.i])
                    self.i += 1

        DQ = Dequant()

        def run_phase(fn):
            start = dict(CNT)
            dqs = (DQ.i, dict(DQ.m12), dict(DQ.m3))
            final = None
            for role in ENGS:
                CNT.clear()
                CNT.update(start)
                DQ.i, m12s, m3s = dqs[0], dict(dqs[1]), dict(dqs[2])
                DQ.m12, DQ.m3 = m12s, m3s
                ROLE[0] = role
                sec = getattr(block, role)

                def _body(e, fn=fn):
                    fn()
                sec(_body)
                if final is None:
                    final = (dict(CNT), DQ.i, dict(DQ.m12), dict(DQ.m3))
                else:
                    assert final == (CNT, DQ.i, DQ.m12, DQ.m3), "nondeterministic phase emission"
            ROLE[0] = None

        def barrier():
            vals = dict(CNT)

            def _b():
                e = eng(ROLE[0])
                for s, v in vals.items():
                    if v > 0:
                        e.wait_ge(SEMS[s], v)
            run_phase(_b)

        s_tri = ent(nc.sbuf_tensor("s_tri", [128, 128], BF16))
        s_mB = ent(nc.sbuf_tensor("s_mB", [128, 128], BF16))
        s_iden = ent(nc.sbuf_tensor("s_iden", [128, 128], F32))
        s_ones = ent(nc.sbuf_tensor("s_ones", [128, 1], BF16))
        s_onef = ent(nc.sbuf_tensor("s_onef", [1, 128], F32))
        s_wqb = ent(nc.sbuf_tensor("s_wqb", [128, 32], F32))
        s_wob = ent(nc.sbuf_tensor("s_wob", [128, 32], F32))
        s_wkb = ent(nc.sbuf_tensor("s_wkb", [128, 1], F32))
        s_wvb = ent(nc.sbuf_tensor("s_wvb", [128, 1], F32))
        s_eps = ent(nc.sbuf_tensor("s_eps", [128, 1], F32))

        p0_stack = ExitStack()
        kv_stack = ExitStack()
        s_wkT = kv_stack.enter_context(nc.sbuf_tensor("s_wkT", [128, D], BF16))
        s_wvT = kv_stack.enter_context(nc.sbuf_tensor("s_wvT", [128, D], BF16))
        s_KF = kv_stack.enter_context(nc.sbuf_tensor("s_KF", [128, T], BF16))
        s_VT = kv_stack.enter_context(nc.sbuf_tensor("s_VT", [128, T], BF16))
        e0 = p0_stack.enter_context
        s_wfl = e0(nc.sbuf_tensor("s_wfl", [128, 2 * D], F32))
        s_wtmp = [e0(nc.sbuf_tensor(f"s_wtmp{i}", [128, D], F32)) for i in range(2)]
        s_wt4 = [e0(nc.sbuf_tensor(f"s_wt4_{i}", [128, 512], BF16)) for i in range(3)]
        s_bt = e0(nc.sbuf_tensor("s_bt", [32, 128], F32))

        def p0():
            do("sync", lambda e: e.dma_start(out=s_tri[:], in_=tri_in), inc=("dS", 16))
            do("sync", lambda e: e.dma_start(out=s_mB[:], in_=mB_in), inc=("dS", 16))
            do("sync", lambda e: e.dma_start(out=s_iden[:], in_=iden_in), inc=("dS", 16))
            do("sync", lambda e: e.dma_start(out=s_wkb[:], in_=wk_b.unsqueeze(1)), inc=("dS", 16))
            do("sync", lambda e: e.dma_start(out=s_wvb[:], in_=wv_b.unsqueeze(1)), inc=("dS", 16))
            do("sync", lambda e: e.dma_start(out=s_wfl[:, 0:D], in_=wk), inc=("dS", 16))
            do("sync", lambda e: e.dma_start(out=s_wfl[:, D:2 * D], in_=wv), inc=("dS", 16))
            do("sync", lambda e: e.dma_start(out=s_bt[:], in_=wq_b.rearrange("(t p) -> t p", p=128)), inc=("dS", 16))
            do("vector", lambda e: e.memset(s_ones[:], 1.0), inc=("cV", 1))
            do("vector", lambda e: e.memset(s_onef[:], 1.0), inc=("cV", 1))
            do("vector", lambda e: e.memset(s_eps[:], EPS), inc=("cV", 1))
            ld8 = CNT["dS"]
            do("tensor", lambda e: e.transpose(out=P[0][:, 0:32], in_=s_bt[:], identity=s_iden[0:32, 0:32]),
               inc=("cT", 1), waits=[("dS", ld8)])
            do("vector", lambda e: e.tensor_copy(s_wqb[:], P[0][:, 0:32]), inc=("cV", 1),
               waits=[("cT", CNT["cT"])])
            do("sync", lambda e: e.dma_start(out=s_bt[:], in_=wo_b.rearrange("(t p) -> t p", p=128)),
               inc=("dS", 16), waits=[("cV", CNT["cV"])])
            do("tensor", lambda e: e.transpose(out=P[1][:, 0:32], in_=s_bt[:], identity=s_iden[0:32, 0:32]),
               inc=("cT", 1), waits=[("dS", CNT["dS"])])
            do("vector", lambda e: e.tensor_copy(s_wob[:], P[1][:, 0:32]), inc=("cV", 1),
               waits=[("cT", CNT["cT"])])

            for w in range(2):
                dst = s_wkT if w == 0 else s_wvT
                for j in range(32):
                    par = (w * 32 + j) % 2
                    do("tensor", lambda e, par=par, w=w, j=j: e.transpose(
                        out=P[2 + par][:, 0:128], in_=s_wfl[:, w * D + j * 128: w * D + (j + 1) * 128],
                        identity=s_iden[:]),
                       inc=("cT", 1), waits=[("dS", ld8), ("cV", CNT["cV"] - 1)])
                    do("vector", lambda e, par=par, dst=dst, j=j: e.tensor_copy(
                        dst[:, j * 128:(j + 1) * 128], P[2 + par][:, 0:128]),
                       inc=("cV", 1), waits=[("cT", CNT["cT"])])

            for w in range(2):
                src = wq if w == 0 else wo
                dstd = wqT_d if w == 0 else woT_d
                for ot in range(32):
                    buf = s_wtmp[ot % 2]
                    do("sync", lambda e, src=src, ot=ot, buf=buf: e.dma_start(
                        out=buf[:], in_=src[ot * 128:(ot + 1) * 128, :]),
                       inc=("dS", 16), waits=[("cT", CNT["cT"])])
                    ldv = CNT["dS"]
                    for jb in range(8):
                        k = (w * 32 + ot) * 8 + jb
                        bb = s_wt4[k % 3]
                        par = 4 + (k % 2)
                        for ji in range(4):
                            j = jb * 4 + ji
                            do("tensor", lambda e, par=par, buf=buf, j=j, ji=ji: e.transpose(
                                out=P[par][:, ji * 128:(ji + 1) * 128], in_=buf[:, j * 128:(j + 1) * 128],
                                identity=s_iden[:]),
                               inc=("cT", 1),
                               waits=[("dS", ldv), ("cV", CNT["cV"])] if ji == 0 else [])
                        do("vector", lambda e, par=par, bb=bb: e.tensor_copy(
                            bb[:], P[par][:]),
                           inc=("cV", 1), waits=[("cT", CNT["cT"]), ("dA", CNT["dA"] - 2 * 16)])
                        do("scalar", lambda e, dstd=dstd, jb=jb, ot=ot, bb=bb: e.dma_start(
                            out=dstd.rearrange("(j p) t -> p j t", p=128)[:, jb * 4:(jb + 1) * 4, ot * 128:(ot + 1) * 128],
                            in_=bb[:].rearrange("p (j c) -> p j c", c=128)),
                           inc=("dA", 16), waits=[("cV", CNT["cV"])])
                    DQ.pump(24)

        run_phase(p0)
        barrier()
        p0_stack.close()

        p12_stack = ExitStack()
        e12 = p12_stack.enter_context
        s_xt = [e12(nc.sbuf_tensor(f"s_xt{i}", [128, D], F32)) for i in range(4)]
        s_xn1 = e12(nc.sbuf_tensor("s_xn1", [128, 32 * 512], BF16))
        s_bns = e12(nc.sbuf_tensor("s_bns", [128, 48], F32))
        s_mv = e12(nc.sbuf_tensor("s_mv", [128, 8], F32))
        s_cat = e12(nc.sbuf_tensor("s_cat", [128, 8], F32))
        s_g8 = e12(nc.sbuf_tensor("s_g8", [8, 128], F32))
        s_invb = e12(nc.sbuf_tensor("s_invb", [128, 512], F32))
        s_mib = e12(nc.sbuf_tensor("s_mib", [128, 512], F32))
        s_t1 = e12(nc.sbuf_tensor("s_t1", [128, 512], F32))
        s_vf = e12(nc.sbuf_tensor("s_vf", [128, 512], F32))

        def p12():
            for ch in range(8):
                for i in range(4):
                    t = ch * 4 + i
                    buf = s_xt[t % 4]
                    do("sync", lambda e, t=t, buf=buf: e.dma_start(
                        out=buf[:], in_=x_all[t * 128:(t + 1) * 128, :]),
                       inc=("dS", 16), waits=[("cT", CNT["cT"])] if t >= 4 else [])
                ld = CNT["dS"]
                for i in range(4):
                    buf = s_xt[(ch * 4 + i) % 4]
                    for sch in range(8):
                        w = [("dS", ld - (3 - i) * 16)]
                        if i == 0 and sch == 0:
                            w.append(("cA", CNT["cA"]))
                        do("vector", lambda e, buf=buf, sch=sch: e.bn_stats(
                            out=s_bns[:, sch * 6:(sch + 1) * 6], in_=buf[:, sch * 512:(sch + 1) * 512]),
                           inc=("cV", 1), waits=w)
                    do("vector", lambda e, i=i: e.bn_aggr(
                        out=s_mv[:, 2 * i:2 * i + 2], in_=s_bns[:]), inc=("cV", 1))
                do("scalar", lambda e: e.activation(
                    out=s_cat[:, 0:4], in_=s_mv[:].rearrange("p (i two) -> p two i", two=2)[:, 1, :],
                    func=AF.Sqrt, bias=s_eps[:, 0:1]), inc=("cA", 1), waits=[("cV", CNT["cV"])])
                do("vector", lambda e: e.reciprocal(s_cat[:, 0:4], s_cat[:, 0:4]),
                   inc=("cV", 1), waits=[("cA", CNT["cA"])])
                do("vector", lambda e: e.tensor_tensor(
                    out=s_cat[:, 4:8], in0=s_mv[:].rearrange("p (i two) -> p two i", two=2)[:, 0, :],
                    in1=s_cat[:, 0:4], op=ALU.mult), inc=("cV", 1))
                do("tensor", lambda e: e.transpose(out=P[6][0:8, 0:128], in_=s_cat[:], identity=s_iden[:]),
                   inc=("cT", 1), waits=[("cV", CNT["cV"])])
                do("vector", lambda e: e.tensor_copy(s_g8[:], P[6][0:8, 0:128]),
                   inc=("cV", 1), waits=[("cT", CNT["cT"]), ("dS", CNT["dS"])])
                do("sync", lambda e, ch=ch: e.dma_start(
                    out=mrow_d[0, ch * 512:(ch + 1) * 512].rearrange("(i c) -> i c", i=4), in_=s_g8[0:4, :]),
                   inc=("dS", 16), waits=[("cV", CNT["cV"])])
                do("sync", lambda e, ch=ch: e.dma_start(
                    out=mrow_d[1, ch * 512:(ch + 1) * 512].rearrange("(i c) -> i c", i=4), in_=s_g8[4:8, :]),
                   inc=("dS", 16))
                wrow = CNT["dS"]
                do("sync", lambda e, ch=ch: e.dma_start(
                    out=s_invb[:], in_=mrow_d[0:1, ch * 512:(ch + 1) * 512].broadcast_to((128, 512))),
                   inc=("dS", 16), waits=[("dS", wrow), ("cV", CNT["cV"])])
                do("sync", lambda e, ch=ch: e.dma_start(
                    out=s_mib[:], in_=mrow_d[1:2, ch * 512:(ch + 1) * 512].broadcast_to((128, 512))),
                   inc=("dS", 16))
                ldb = CNT["dS"]
                for dt in range(32):
                    par = dt % 2
                    for i in range(4):
                        buf = s_xt[(ch * 4 + i) % 4]
                        w = [("dS", ldb), ("cV", CNT["cV"] - 2)] if i == 0 else []
                        do("tensor", lambda e, par=par, buf=buf, dt=dt, i=i: e.transpose(
                            out=P[par][:, i * 128:(i + 1) * 128], in_=buf[:, dt * 128:(dt + 1) * 128],
                            identity=s_iden[:]),
                           inc=("cT", 1), waits=w)
                    do("vector", lambda e, par=par: e.tensor_tensor(
                        out=s_t1[:], in0=P[par][:], in1=s_invb[:], op=ALU.mult),
                       inc=("cV", 1),
                       waits=[("cT", CNT["cT"])] + ([("dG", CNT["dG"])] if dt == 0 else []))
                    do("vector", lambda e, dt=dt: e.tensor_tensor(
                        out=s_xn1[:, dt * 512:(dt + 1) * 512], in0=s_t1[:], in1=s_mib[:], op=ALU.subtract),
                       inc=("cV", 1))
                    DQ.pump(2)
                xnv = CNT["cV"]
                for w in range(2):
                    wT = s_wkT if w == 0 else s_wvT
                    pb = P[2 + w]
                    for dt in range(32):
                        do("tensor", lambda e, pb=pb, wT=wT, dt=dt: e.matmul(
                            pb[:], wT[:, dt * 128:(dt + 1) * 128], s_xn1[:, dt * 512:(dt + 1) * 512],
                            start=(dt == 0), stop=(dt == 31)),
                           inc=("cT", 1), waits=[("cV", xnv), ("cA", CNT["cA"])] if dt == 0 else [])
                    if w == 0:
                        do("scalar", lambda e, ch=ch: e.activation(
                            out=s_KF[:, ch * 512:(ch + 1) * 512], in_=P[2][:], func=AF.Identity,
                            bias=s_wkb[:, 0:1]), inc=("cA", 1), waits=[("cT", CNT["cT"])])
                    else:
                        do("scalar", lambda e: e.activation(
                            out=s_vf[:], in_=P[3][:], func=AF.Identity, bias=s_wvb[:, 0:1]),
                           inc=("cA", 1), waits=[("cT", CNT["cT"])])
                        for i in range(4):
                            do("tensor", lambda e, i=i: e.transpose(
                                out=P[5][:, 0:128], in_=s_vf[:, i * 128:(i + 1) * 128], identity=s_iden[:]),
                               inc=("cT", 1), waits=[("cA", CNT["cA"]), ("cV", CNT["cV"])])
                            do("vector", lambda e, ch=ch, i=i: e.tensor_copy(
                                s_VT[:, ch * 512 + i * 128: ch * 512 + (i + 1) * 128], P[5][:, 0:128]),
                               inc=("cV", 1), waits=[("cT", CNT["cT"])])
                for j, i in enumerate((0, 2)):
                    do("gpsimd", lambda e, ch=ch, j=j, i=i: e.dma_start(
                        out=xn1_d.rearrange("(dt p) t -> p dt t", p=128)[:, :, ch * 256 + j * 128: ch * 256 + (j + 1) * 128],
                        in_=s_xn1[:].rearrange("p (dt c) -> p dt c", c=512)[:, :, i * 128:(i + 1) * 128]),
                       inc=("dG", 16), waits=[("cV", xnv)])
                DQ.pump(10)

        run_phase(p12)
        barrier()
        p12_stack.close()

        p2_stack = ExitStack()
        e2 = p2_stack.enter_context
        s_x2 = [e2(nc.sbuf_tensor(f"s_x2_{i}", [128, 32 * 512], BF16)) for i in range(2)]
        s_wq2 = [e2(nc.sbuf_tensor(f"s_wq2_{i}", [128, D], BF16)) for i in range(2)]
        s_qt = [e2(nc.sbuf_tensor(f"s_qt{i}", [128, 512], BF16)) for i in range(2)]

        def p2b():
            tc_done = {}
            for tc in range(4):
                buf = s_x2[tc % 2]
                w = [("cT", tc_done[tc - 2])] if tc >= 2 else []
                for half in range(2):
                    do("sync", lambda e, tc=tc, buf=buf, half=half: e.dma_start(
                        out=buf[:, half * 16 * 512:(half + 1) * 16 * 512].rearrange("p (dt c) -> p dt c", c=512),
                        in_=xn1_d.rearrange("(dt p) t -> p dt t", p=128)[:, half * 16:(half + 1) * 16, tc * 512:(tc + 1) * 512]),
                       inc=("dS", 16), waits=w if half == 0 else [])
                ldx = CNT["dS"]
                for qt in range(32):
                    k = tc * 32 + qt
                    wbuf = s_wq2[k % 2]
                    for half in range(2):
                        do("gpsimd", lambda e, qt=qt, wbuf=wbuf, half=half: e.dma_start(
                            out=wbuf[:, half * 2048:(half + 1) * 2048].rearrange("p (dt c) -> p dt c", c=128),
                            in_=wqT_d.rearrange("(dt p) t -> p dt t", p=128)[:, half * 16:(half + 1) * 16, qt * 128:(qt + 1) * 128]),
                           inc=("dG", 16),
                           waits=[("cT", CNT["cT"] - 32)] if (k >= 2 and half == 0) else [])
                    ldw = CNT["dG"]
                    par = k % 2
                    for dt in range(32):
                        do("tensor", lambda e, par=par, wbuf=wbuf, buf=buf, dt=dt: e.matmul(
                            P[par][:], wbuf[:, dt * 128:(dt + 1) * 128], buf[:, dt * 512:(dt + 1) * 512],
                            start=(dt == 0), stop=(dt == 31)),
                           inc=("cT", 1),
                           waits=[("dS", ldx), ("dG", ldw), ("cA", CNT["cA"])] if dt == 0 else [])
                    qb = s_qt[k % 2]
                    do("scalar", lambda e, par=par, qb=qb, qt=qt: e.activation(
                        out=qb[:], in_=P[par][:], func=AF.Identity, bias=s_wqb[:, qt:qt + 1]),
                       inc=("cA", 1), waits=[("cT", CNT["cT"]), ("dS", CNT["dS"])])
                    do("sync", lambda e, qb=qb, qt=qt, tc=tc: e.dma_start(
                        out=q_d[qt * 128:(qt + 1) * 128, tc * 512:(tc + 1) * 512], in_=qb[:]),
                       inc=("dS", 16), waits=[("cA", CNT["cA"])])
                    DQ.pump(36)
                tc_done[tc] = CNT["cT"]

        run_phase(p2b)
        barrier()
        p2_stack.close()

        p3_stack = ExitStack()
        e3 = p3_stack.enter_context
        s_qh = [e3(nc.sbuf_tensor(f"s_qh{i}", [128, TO], BF16)) for i in range(2)]
        s_pt = [e3(nc.sbuf_tensor(f"s_pt{i}", [128, 512], BF16)) for i in range(3)]
        s_rs = e3(nc.sbuf_tensor("s_rs", [1, 512], F32))
        s_rb = [e3(nc.sbuf_tensor(f"s_rb{i}", [128, 512], F32)) for i in range(2)]
        s_at = [e3(nc.sbuf_tensor(f"s_at{i}", [128, 512], BF16)) for i in range(2)]

        def p3():
            h_done = {}
            for h in range(32):
                qbuf = s_qh[h % 2]
                do("sync", lambda e, h=h, qbuf=qbuf: e.dma_start(
                    out=qbuf[:], in_=q_d[h * 128:(h + 1) * 128, :]),
                   inc=("dS", 16), waits=[("cT", h_done[h - 2])] if h >= 2 else [])
                ldq = CNT["dS"]
                for qc in range(4):
                    u = h * 4 + qc
                    nkt = 8 * qc + 8
                    av = P[2 + (u % 2)]
                    sm = P[4 + (u % 2)]
                    avw = [("cV", CNT["cV"])]
                    for kt in range(nkt):
                        sl = max(0, kt // 2 - 4 * qc)
                        c0 = sl * 128
                        pb = P[kt % 2]
                        ptb = s_pt[kt % 3]
                        do("tensor", lambda e, pb=pb, kt=kt, qbuf=qbuf, qc=qc, c0=c0: e.matmul(
                            pb[:, c0:512], s_KF[:, kt * 128:(kt + 1) * 128],
                            qbuf[:, qc * 512 + c0: (qc + 1) * 512],
                            start=True, stop=True, skip_group_check=True),
                           inc=("cT", 1),
                           waits=([("dS", ldq)] if kt == 0 else []) + [("cA", CNT["cA"] - 1)])
                        do("scalar", lambda e, pb=pb, ptb=ptb, c0=c0: e.activation(
                            out=ptb[:, c0:512], in_=pb[:, c0:512], func=AF.Exp, scale=INV_SQRT_HD),
                           inc=("cA", 1), waits=[("cT", CNT["cT"])])
                        masked = False
                        i_d = kt // 2
                        if 4 * qc <= i_d < 4 * qc + 4:
                            cm = (i_d - 4 * qc) * 128
                            msk = s_tri if kt % 2 == 0 else s_mB
                            do("vector", lambda e, ptb=ptb, cm=cm, msk=msk: e.tensor_tensor(
                                out=ptb[:, cm:cm + 128], in0=ptb[:, cm:cm + 128], in1=msk[:], op=ALU.mult),
                               inc=("cV", 1), waits=[("cA", CNT["cA"])])
                            masked = True
                        pw = [("cA", CNT["cA"])] + ([("cV", CNT["cV"])] if masked else [])
                        do("tensor", lambda e, av=av, kt=kt, ptb=ptb, c0=c0, nkt=nkt: e.matmul(
                            av[:, c0:512], s_VT[:, kt * 128:(kt + 1) * 128], ptb[:, c0:512],
                            start=(kt == 0), stop=(kt == nkt - 1), skip_group_check=True),
                           inc=("cT", 1), waits=pw + (avw if kt == 0 else []))
                        do("tensor", lambda e, sm=sm, kt=kt, ptb=ptb, c0=c0, nkt=nkt: e.matmul(
                            sm[0:1, c0:512], s_ones[:], ptb[:, c0:512],
                            start=(kt == 0), stop=(kt == nkt - 1), skip_group_check=True),
                           inc=("cT", 1))
                    do("vector", lambda e, sm=sm: e.reciprocal(s_rs[:], sm[0:1, :]),
                       inc=("cV", 1), waits=[("cT", CNT["cT"]), ("cA", CNT["cA"] - 1)])
                    pr = P[6 + (u % 2)]
                    do("tensor", lambda e, pr=pr: e.matmul(
                        pr[:], s_onef[:], s_rs[:], start=True, stop=True, skip_group_check=True),
                       inc=("cT", 1), waits=[("cV", CNT["cV"]), ("cA", CNT["cA"] - 1)])
                    rb = s_rb[u % 2]
                    do("scalar", lambda e, pr=pr, rb=rb: e.activation(
                        out=rb[:], in_=pr[:], func=AF.Copy),
                       inc=("cA", 1), waits=[("cT", CNT["cT"]), ("cV", CNT["cV"] - 1)])
                    at = s_at[u % 2]
                    do("vector", lambda e, av=av, rb=rb, at=at: e.tensor_tensor(
                        out=at[:], in0=av[:], in1=rb[:], op=ALU.mult),
                       inc=("cV", 1), waits=[("cA", CNT["cA"]), ("dG", CNT["dG"] - 16)])
                    do("gpsimd", lambda e, at=at, h=h, qc=qc: e.dma_start(
                        out=attn_d[h * 128:(h + 1) * 128, qc * 512:(qc + 1) * 512], in_=at[:]),
                       inc=("dG", 16), waits=[("cV", CNT["cV"])])
                    DQ.pump(36)
                h_done[h] = CNT["cT"]

        run_phase(p3)
        barrier()
        p3_stack.close()
        kv_stack.close()

        p4_stack = ExitStack()
        e4 = p4_stack.enter_context
        s_accS = e4(nc.sbuf_tensor("s_accS", [1, TO], F32))
        s_accQ = e4(nc.sbuf_tensor("s_accQ", [1, TO], F32))
        s_ln2m = e4(nc.sbuf_tensor("s_ln2m", [1, TO], F32))
        s_ln2v = e4(nc.sbuf_tensor("s_ln2v", [1, TO], F32))
        s_ac4 = [e4(nc.sbuf_tensor(f"s_ac4_{i}", [128, 32 * 512], BF16)) for i in range(2)]
        s_wo4 = [e4(nc.sbuf_tensor(f"s_wo4_{i}", [128, D], BF16)) for i in range(2)]
        s_xr = [e4(nc.sbuf_tensor(f"s_xr{i}", [128, 512], F32)) for i in range(2)]
        s_tmp4 = e4(nc.sbuf_tensor("s_tmp4", [128, 512], F32))
        s_r2 = [e4(nc.sbuf_tensor(f"s_r2_{i}", [128, 512], F32)) for i in range(2)]
        s_r2b = [e4(nc.sbuf_tensor(f"s_r2b{i}", [128, 512], BF16)) for i in range(2)]
        s_sq4 = [e4(nc.sbuf_tensor(f"s_sq4_{i}", [128, 512], BF16)) for i in range(2)]

        def p4():
            do("vector", lambda e: e.memset(s_accS[:], 0.0), inc=("cV", 1))
            do("vector", lambda e: e.memset(s_accQ[:], 0.0), inc=("cV", 1))
            qc_done = {}
            for qc in range(4):
                abuf = s_ac4[qc % 2]
                w = [("cT", qc_done[qc - 2])] if qc >= 2 else []
                for half in range(2):
                    do("sync", lambda e, qc=qc, abuf=abuf, half=half: e.dma_start(
                        out=abuf[:, half * 16 * 512:(half + 1) * 16 * 512].rearrange("p (dt c) -> p dt c", c=512),
                        in_=attn_d.rearrange("(dt p) t -> p dt t", p=128)[:, half * 16:(half + 1) * 16, qc * 512:(qc + 1) * 512]),
                       inc=("dS", 16), waits=w if half == 0 else [])
                lda = CNT["dS"]
                for ot in range(32):
                    k = qc * 32 + ot
                    wbuf = s_wo4[k % 2]
                    for half in range(2):
                        do("gpsimd", lambda e, ot=ot, wbuf=wbuf, half=half: e.dma_start(
                            out=wbuf[:, half * 2048:(half + 1) * 2048].rearrange("p (dt c) -> p dt c", c=128),
                            in_=woT_d.rearrange("(dt p) t -> p dt t", p=128)[:, half * 16:(half + 1) * 16, ot * 128:(ot + 1) * 128]),
                           inc=("dG", 16),
                           waits=[("cT", CNT["cT"] - 44)] if (k >= 2 and half == 0) else [])
                    ldw = CNT["dG"]
                    xr = s_xr[k % 2]
                    do("sync", lambda e, qc=qc, ot=ot, xr=xr: e.dma_start(
                        out=xr[:].rearrange("p (g c) -> p g c", c=128),
                        in_=x_own.rearrange("(g p) d -> p g d", p=128)[:, qc * 4:(qc + 1) * 4, ot * 128:(ot + 1) * 128]),
                       inc=("dS", 16), waits=[("cT", CNT["cT"] - 40)] if k >= 2 else [])
                    ldx = CNT["dS"]
                    par = k % 2
                    for dt in range(32):
                        do("tensor", lambda e, par=par, wbuf=wbuf, abuf=abuf, dt=dt: e.matmul(
                            P[par][:], wbuf[:, dt * 128:(dt + 1) * 128], abuf[:, dt * 512:(dt + 1) * 512],
                            start=(dt == 0), stop=(dt == 31)),
                           inc=("cT", 1),
                           waits=[("dS", lda), ("dG", ldw), ("cV", CNT["cV"])] if dt == 0 else [])
                    mmv = CNT["cT"]
                    for i in range(4):
                        do("tensor", lambda e, par=par, xr=xr, i=i: e.transpose(
                            out=P[4 + par][:, i * 128:(i + 1) * 128], in_=xr[:, i * 128:(i + 1) * 128],
                            identity=s_iden[:]),
                           inc=("cT", 1), waits=[("dS", ldx), ("cV", CNT["cV"])] if i == 0 else [])
                    do("scalar", lambda e, par=par, ot=ot: e.activation(
                        out=s_tmp4[:], in_=P[par][:], func=AF.Identity, bias=s_wob[:, ot:ot + 1]),
                       inc=("cA", 1), waits=[("cT", mmv), ("cV", CNT["cV"])])
                    r2 = s_r2[k % 2]
                    do("vector", lambda e, par=par, r2=r2: e.tensor_tensor(
                        out=r2[:], in0=s_tmp4[:], in1=P[4 + par][:], op=ALU.add),
                       inc=("cV", 1), waits=[("cA", CNT["cA"]), ("cT", CNT["cT"]), ("dS", CNT["dS"])])
                    do("sync", lambda e, r2=r2, ot=ot, qc=qc: e.dma_start(
                        out=r2_d[ot * 128:(ot + 1) * 128, qc * 512:(qc + 1) * 512], in_=r2[:]),
                       inc=("dS", 16), waits=[("cV", CNT["cV"])])
                    r2b = s_r2b[k % 2]
                    do("vector", lambda e, r2=r2, r2b=r2b: e.tensor_copy(r2b[:], r2[:]),
                       inc=("cV", 1), waits=[("cT", CNT["cT"])])
                    sq = s_sq4[k % 2]
                    do("scalar", lambda e, r2b=r2b, sq=sq: e.activation(out=sq[:], in_=r2b[:], func=AF.Square),
                       inc=("cA", 1), waits=[("cV", CNT["cV"]), ("cT", CNT["cT"])])
                    do("tensor", lambda e, r2b=r2b: e.matmul(
                        P[6][0:1, :], s_ones[:], r2b[:], start=True, stop=True, skip_group_check=True),
                       inc=("cT", 1), waits=[("cV", CNT["cV"])])
                    do("tensor", lambda e, sq=sq: e.matmul(
                        P[7][0:1, :], s_ones[:], sq[:], start=True, stop=True, skip_group_check=True),
                       inc=("cT", 1), waits=[("cA", CNT["cA"])])
                    do("vector", lambda e, qc=qc: e.tensor_tensor(
                        out=s_accS[0:1, qc * 512:(qc + 1) * 512], in0=s_accS[0:1, qc * 512:(qc + 1) * 512],
                        in1=P[6][0:1, :], op=ALU.add),
                       inc=("cV", 1), waits=[("cT", CNT["cT"])])
                    do("vector", lambda e, qc=qc: e.tensor_tensor(
                        out=s_accQ[0:1, qc * 512:(qc + 1) * 512], in0=s_accQ[0:1, qc * 512:(qc + 1) * 512],
                        in1=P[7][0:1, :], op=ALU.add),
                       inc=("cV", 1))
                    DQ.pump(36)
                qc_done[qc] = CNT["cT"]
            do("vector", lambda e: e.tensor_scalar(
                out=s_ln2m[:], in0=s_accS[:], scalar1=1.0 / D, scalar2=None, op0=ALU.mult),
               inc=("cV", 1))
            do("vector", lambda e: e.tensor_tensor(
                out=s_ln2v[:], in0=s_ln2m[:], in1=s_ln2m[:], op=ALU.mult),
               inc=("cV", 1))
            do("vector", lambda e: e.scalar_tensor_tensor(
                out=s_ln2v[:], in0=s_accQ[:], scalar=1.0 / D, in1=s_ln2v[:],
                op0=ALU.mult, op1=ALU.subtract),
               inc=("cV", 1))
            do("scalar", lambda e: e.activation(
                out=s_ln2v[:], in_=s_ln2v[:], func=AF.Sqrt, bias=s_eps[0:1, 0:1]),
               inc=("cA", 1), waits=[("cV", CNT["cV"])])
            do("vector", lambda e: e.reciprocal(s_ln2v[:], s_ln2v[:]),
               inc=("cV", 1), waits=[("cA", CNT["cA"])])
            do("vector", lambda e: e.tensor_tensor(
                out=s_ln2m[:], in0=s_ln2m[:], in1=s_ln2v[:], op=ALU.mult),
               inc=("cV", 1))
            do("sync", lambda e: e.dma_start(out=mrow_d[3, 0:TO].unsqueeze(0), in_=s_ln2v[:]),
               inc=("dS", 16), waits=[("cV", CNT["cV"])])
            do("sync", lambda e: e.dma_start(out=mrow_d[4, 0:TO].unsqueeze(0), in_=s_ln2m[:]),
               inc=("dS", 16))

        run_phase(p4)
        barrier()
        p4_stack.close()

        p5b_stack = ExitStack()
        e5b = p5b_stack.enter_context
        s_xn2 = e5b(nc.sbuf_tensor("s_xn2", [128, 32 * 2048], BF16))
        s_r2t = [e5b(nc.sbuf_tensor("s_r2t0", [128, 512], F32))] * 2
        s_w12 = [e5b(nc.sbuf_tensor(f"s_w12_{i}", [128, D], BF16)) for i in range(3)]
        s_sg = e5b(nc.sbuf_tensor("s_sg", [128, 512], F32))
        s_ht = [e5b(nc.sbuf_tensor("s_ht0", [128, 2048], BF16))] * 2

        def p5b():
            s_i2b = s_w12[0].bitcast(F32)[:, 0:512]
            s_m2b = s_w12[0].bitcast(F32)[:, 512:1024]
            for tc in range(4):
                do("sync", lambda e, tc=tc: e.dma_start(
                    out=s_i2b, in_=mrow_d[3:4, tc * 512:(tc + 1) * 512].broadcast_to((128, 512))),
                   inc=("dS", 16), waits=[("cV", CNT["cV"])])
                do("sync", lambda e, tc=tc: e.dma_start(
                    out=s_m2b, in_=mrow_d[4:5, tc * 512:(tc + 1) * 512].broadcast_to((128, 512))),
                   inc=("dS", 16))
                ldb = CNT["dS"]
                for dt in range(32):
                    rb = s_r2t[dt % 2]
                    do("gpsimd", lambda e, rb=rb, dt=dt, tc=tc: e.dma_start(
                        out=rb[:], in_=r2_d[dt * 128:(dt + 1) * 128, tc * 512:(tc + 1) * 512]),
                       inc=("dG", 16), waits=[("cV", CNT["cV"] - 1)] if (tc * 32 + dt) >= 1 else [])
                    do("vector", lambda e, rb=rb: e.tensor_tensor(
                        out=s_sg[:], in0=rb[:], in1=s_i2b, op=ALU.mult),
                       inc=("cV", 1), waits=[("dG", CNT["dG"]), ("dS", ldb)])
                    do("vector", lambda e, dt=dt, tc=tc: e.tensor_tensor(
                        out=s_xn2[:, dt * 2048 + tc * 512: dt * 2048 + (tc + 1) * 512],
                        in0=s_sg[:], in1=s_m2b, op=ALU.subtract),
                       inc=("cV", 1))
            xnv = CNT["cV"]
            hc_done = {}
            for hc in range(NHC):
                w1b = s_w12[hc % 2]
                w2b = s_w12[2]
                wwar = ([("cT", hc_done[hc - 2])] if hc >= 2 else [("cV", xnv)]) + [("qD", DQ.ensure12(hc // 4))]
                wwar2 = ([("cT", hc_done[hc - 1])] if hc >= 1 else [("cV", xnv)]) + [("qD", DQ.ensure12(hc // 4))]
                for half in range(2):
                    do("sync", lambda e, w1b=w1b, hc=hc, half=half: e.dma_start(
                        out=w1b[:, half * 2048:(half + 1) * 2048].rearrange("p (dt c) -> p dt c", c=128),
                        in_=w1t_d.rearrange("(dt p) t -> p dt t", p=128)[:, half * 16:(half + 1) * 16, hc * 128:(hc + 1) * 128]),
                       inc=("dS", 16), waits=wwar if half == 0 else [])
                    do("gpsimd", lambda e, w2b=w2b, hc=hc, half=half: e.dma_start(
                        out=w2b[:, half * 2048:(half + 1) * 2048].rearrange("p (dt c) -> p dt c", c=128),
                        in_=w2t_d.rearrange("(dt p) t -> p dt t", p=128)[:, half * 16:(half + 1) * 16, hc * 128:(hc + 1) * 128]),
                       inc=("dG", 16), waits=wwar2 if half == 0 else [])
                ldw1 = CNT["dS"]
                ldw2 = CNT["dG"]
                ht = s_ht[hc % 2]
                for tc in range(4):
                    p1 = P[tc % 2]
                    p2 = P[2 + tc % 2]
                    for dt in range(32):
                        do("tensor", lambda e, p1=p1, w1b=w1b, dt=dt, tc=tc: e.matmul(
                            p1[:], w1b[:, dt * 128:(dt + 1) * 128],
                            s_xn2[:, dt * 2048 + tc * 512: dt * 2048 + (tc + 1) * 512],
                            start=(dt == 0), stop=(dt == 31)),
                           inc=("cT", 1),
                           waits=[("cV", xnv), ("dS", ldw1), ("cA", CNT["cA"])] if dt == 0 else [])
                        if dt % 8 == 7:
                            DQ.pump(6)
                    do("scalar", lambda e, p1=p1: e.activation(out=s_sg[:], in_=p1[:], func=AF.Silu),
                       inc=("cA", 1), waits=[("cT", CNT["cT"]), ("cV", CNT["cV"])])
                    for dt in range(32):
                        do("tensor", lambda e, p2=p2, w2b=w2b, dt=dt, tc=tc: e.matmul(
                            p2[:], w2b[:, dt * 128:(dt + 1) * 128],
                            s_xn2[:, dt * 2048 + tc * 512: dt * 2048 + (tc + 1) * 512],
                            start=(dt == 0), stop=(dt == 31)),
                           inc=("cT", 1),
                           waits=[("dG", ldw2), ("cV", CNT["cV"])] if dt == 0 else [])
                        if dt % 8 == 7:
                            DQ.pump(6)
                    do("vector", lambda e, p2=p2, ht=ht, tc=tc: e.tensor_tensor(
                        out=ht[:, tc * 512:(tc + 1) * 512], in0=p2[:], in1=s_sg[:], op=ALU.mult),
                       inc=("cV", 1), waits=[("cT", CNT["cT"]), ("cA", CNT["cA"]), ("dS", CNT["dS"])])

                do("sync", lambda e, ht=ht, hc=hc: e.dma_start(
                    out=h_d[hc * 128:(hc + 1) * 128, :], in_=ht[:]),
                   inc=("dS", 16), waits=[("cV", CNT["cV"])])
                hc_done[hc] = CNT["cT"]

        run_phase(p5b)
        barrier()
        p5b_stack.close()

        p6_stack = ExitStack()
        e6 = p6_stack.enter_context
        s_H = e6(nc.sbuf_tensor("s_H", [128, NHC * 512], BF16))
        s_w3 = [e6(nc.sbuf_tensor(f"s_w3_{i}", [128, 8 * 256], BF16)) for i in range(3)]
        s_r26 = [e6(nc.sbuf_tensor(f"s_r26_{i}", [128, 512], F32)) for i in range(2)]
        s_of = [e6(nc.sbuf_tensor(f"s_of{i}", [128, 512], F32)) for i in range(2)]
        s_ot4 = [e6(nc.sbuf_tensor(f"s_ot4_{i}", [128, 512], F32)) for i in range(2)]

        def p6():
            NG = (NHC + 7) // 8
            g_done = {}
            tq_done = {}
            for tq in range(4):
                hw = [("cT", tq_done[tq - 1])] if tq >= 1 else []
                for g in range(NG):
                    n = min(8, NHC - g * 8)
                    do("sync", lambda e, g=g, n=n, tq=tq: e.dma_start(
                        out=s_H[:, g * 8 * 512:(g * 8 + n) * 512].rearrange("p (j c) -> p j c", c=512),
                        in_=h_d.rearrange("(j p) t -> p j t", p=128)[:, g * 8:g * 8 + n, tq * 512:(tq + 1) * 512]),
                       inc=("dS", 16), waits=hw if g == 0 else [])
                ldh = CNT["dS"]
                for og in range(16):
                    u = tq * 16 + og
                    for g in range(NG):
                        n = min(8, NHC - g * 8)
                        gi = u * NG + g
                        wb = s_w3[gi % 3]
                        do("scalar", lambda e, wb=wb, g=g, n=n, og=og: e.dma_start(
                            out=wb[:, 0:n * 256].rearrange("p (j c) -> p j c", c=256),
                            in_=w3t_d.rearrange("(j p) t -> p j t", p=128)[:, g * 8:g * 8 + n, og * 256:(og + 1) * 256]),
                           inc=("dA", 16),
                           waits=([("cT", g_done[gi - 3])] if gi >= 3 else []) + [("qD", DQ.ensure3(g * 8 + n - 1))])
                        gld = CNT["dA"]
                        for j in range(n):
                            hc = g * 8 + j
                            for sub in range(2):
                                do("tensor", lambda e, wb=wb, j=j, sub=sub, hc=hc: e.matmul(
                                    P[sub][:], wb[:, j * 256 + sub * 128: j * 256 + (sub + 1) * 128],
                                    s_H[:, hc * 512:(hc + 1) * 512],
                                    start=(hc == 0), stop=(hc == NHC - 1), skip_group_check=True),
                                   inc=("cT", 1),
                                   waits=([("dS", ldh), ("dA", gld), ("cV", CNT["cV"])] if hc == 0 else
                                          ([("dA", gld)] if j == 0 else [])))
                        g_done[gi] = CNT["cT"]
                        DQ.pump(12)
                    mmv = CNT["cT"]
                    for sub in range(2):
                        ot = og * 2 + sub
                        r26 = s_r26[sub]
                        do("sync", lambda e, r26=r26, ot=ot, tq=tq: e.dma_start(
                            out=r26[:], in_=r2_d[ot * 128:(ot + 1) * 128, tq * 512:(tq + 1) * 512]),
                           inc=("dS", 16), waits=[("cV", CNT["cV"])] if u >= 1 else [])
                        of = s_of[sub]
                        do("vector", lambda e, sub=sub, r26=r26, of=of: e.tensor_tensor(
                            out=of[:], in0=P[sub][:], in1=r26[:], op=ALU.add),
                           inc=("cV", 1), waits=[("cT", mmv), ("dS", CNT["dS"])])
                        ofv = CNT["cV"]
                        for tt in range(4):
                            do("tensor", lambda e, of=of, tt=tt, sub=sub: e.transpose(
                                out=P[4 + sub][:, tt * 128:(tt + 1) * 128], in_=of[:, tt * 128:(tt + 1) * 128],
                                identity=s_iden[:]),
                               inc=("cT", 1), waits=[("cV", ofv)] if tt == 0 else [])
                        trv = CNT["cT"]
                        o4 = s_ot4[sub]
                        do("vector", lambda e, o4=o4, sub=sub: e.tensor_copy(
                            o4[:], P[4 + sub][:]),
                           inc=("cV", 1), waits=[("cT", trv), ("dS", CNT["dS"])])
                        do("sync", lambda e, o4=o4, ot=ot, tq=tq: e.dma_start(
                            out=y_own.rearrange("(g p) d -> p g d", p=128)[:, tq * 4:(tq + 1) * 4, ot * 128:(ot + 1) * 128],
                            in_=o4[:].rearrange("p (g c) -> p g c", c=128)),
                           inc=("dS", 16), waits=[("cV", CNT["cV"])])
                tq_done[tq] = CNT["cT"]
            DQ.drain_all()

        run_phase(p6)
        barrier()
        p6_stack.close()

    return nc


_NC_CACHE = None


def kernel(**inputs):
    global _NC_CACHE
    B = 4
    x = np.asarray(inputs["L0_attn_x"], dtype=np.float32)
    wq_w = np.asarray(inputs["L0_attn_wq_w"], dtype=np.float32)
    wq_b = np.asarray(inputs["L0_attn_wq_b"], dtype=np.float32)
    wk_w = np.asarray(inputs["L0_attn_wk_w"], dtype=np.float32)
    wk_b = np.asarray(inputs["L0_attn_wk_b"], dtype=np.float32)
    wv_w = np.asarray(inputs["L0_attn_wv_w"], dtype=np.float32)
    wv_b = np.asarray(inputs["L0_attn_wv_b"], dtype=np.float32)
    wo_w = np.asarray(inputs["L0_attn_wo_w"], dtype=np.float32)
    wo_b = np.asarray(inputs["L0_attn_wo_b"], dtype=np.float32)

    def codes_u8_T(a):
        t = np.ascontiguousarray(np.asarray(a).T)
        return np.ascontiguousarray(t.view(np.uint8)[:, 0::4])

    w1qT = codes_u8_T(inputs["L0_ffn_w1_q"])
    w2qT = codes_u8_T(inputs["L0_ffn_w2_q"])
    w3qT = codes_u8_T(inputs["L0_ffn_w3_q"])
    sc1T = np.ascontiguousarray(np.asarray(inputs["L0_ffn_w1_scale"], dtype=np.float32).T)
    sc2T = np.ascontiguousarray(np.asarray(inputs["L0_ffn_w2_scale"], dtype=np.float32).T)
    sc3T = np.ascontiguousarray(np.asarray(inputs["L0_ffn_w3_scale"], dtype=np.float32).T)

    tri = np.tril(np.ones((128, 128), dtype=np.float32)).T.astype(ml_dtypes.bfloat16)
    iden = np.eye(128, dtype=np.float32)
    ones128 = np.ones((128, 128), dtype=ml_dtypes.bfloat16)
    zeros128 = np.zeros((128, 128), dtype=ml_dtypes.bfloat16)

    if _NC_CACHE is None:
        _NC_CACHE = build_nc()
    nc = _NC_CACHE

    in_maps = []
    for c in range(8):
        b, h = c // 2, c % 2
        r4, r2 = c % 4, c % 2
        xt = x[b].reshape(32, 128, D)
        if h == 0:
            perm = np.arange(32)
        else:
            perm = np.arange(32).reshape(16, 2)[:, ::-1].reshape(32)
        x_perm = np.ascontiguousarray(xt[perm].reshape(T, D))
        x_own_a = np.ascontiguousarray(xt[perm[0::2]].reshape(TO, D))
        in_maps.append({
            "x_all": x_perm, "x_own": x_own_a,
            "wq": wq_w, "wo": wo_w, "wk": wk_w, "wv": wv_w,
            "wq_b": wq_b, "wo_b": wo_b, "wk_b": wk_b, "wv_b": wv_b,
            "w1qT": w1qT, "w2qT": w2qT, "w3qT": w3qT,
            "sc1T": sc1T, "sc2T": sc2T, "sc3T": sc3T,
            "tri_in": tri, "mB_in": zeros128 if h == 0 else ones128,
            "iden_in": iden,
        })

    res = run_bass_kernel_spmd(nc, in_maps, core_ids=list(range(8)))

    out = np.empty((B, T, D), dtype=np.float32)
    for c in range(8):
        b, h = c // 2, c % 2
        yo = np.asarray(res.results[c]["y_own"], dtype=np.float32).reshape(16, 128, D)
        ob = out[b].reshape(32, 128, D)
        for k in range(16):
            ob[2 * k + h] = yo[k]
    return out
